# version 02
# baseline (speedup 1.0000x reference)
import math
import numpy as np
import ml_dtypes
import concourse.bass as bass
import concourse.mybir as mybir
from concourse.bass import IndirectOffsetOnAxis
from concourse.bass_utils import run_bass_kernel_spmd
from contextlib import ExitStack

F32 = mybir.dt.float32
BF16 = mybir.dt.bfloat16
U32 = mybir.dt.uint32
AF = mybir.ActivationFunctionType
ALU = mybir.AluOpType
AX = mybir.AxisListType
bfdt = ml_dtypes.bfloat16

B, L, D = 8, 4096, 1024
DCKV, DCQ = 256, 512
NH, DH, DR, HALF = 8, 64, 32, 16
NIH = 4
NE, FF = 64, 512
SCALE = (DH + DR) ** -0.5
NT = L // 128
EPS_TIE = 1e-10
NEG = -1.0e30
NSL = 16
TSL = 512

TAPS = []

ENGINES = ("sync", "gpsimd", "scalar", "vector", "tensor")

_HOST_CACHE = {}


def host_tables():
    if _HOST_CACHE:
        return _HOST_CACHE
    t = {}
    f32 = np.float32
    theta = (1.0 / 10000.0 ** (2.0 * np.arange(HALF) / DR)).astype(np.float64)
    k = np.arange(L, dtype=np.float64)[:, None]

    def tile_view(a):
        return np.ascontiguousarray(
            a.reshape(NT, 128, HALF).transpose(1, 0, 2)).astype(f32)

    t["cos_q"] = tile_view(np.cos(k * theta))
    t["sin_q"] = tile_view(np.sin(k * theta))
    t["cos_2q"] = tile_view(np.cos(2.0 * k * theta))
    t["sin_2q"] = tile_view(np.sin(2.0 * k * theta))
    t["cos_k63"] = tile_view(np.cos((k + 63.0) * theta))
    t["sin_k63"] = tile_view(np.sin((k + 63.0) * theta))
    j = np.arange(64, 96, dtype=np.float64)[:, None]
    t["cos_snl"] = np.broadcast_to(np.cos(j * theta).reshape(1, -1),
                                   (128, 32 * HALF)).astype(f32).copy()
    t["sin_snl"] = np.broadcast_to(np.sin(j * theta).reshape(1, -1),
                                   (128, 32 * HALF)).astype(f32).copy()
    j0 = np.arange(0, 96, dtype=np.float64)[:, None]
    t["cos_s0"] = np.broadcast_to(np.cos(j0 * theta).reshape(1, -1),
                                  (128, 96 * HALF)).astype(f32).copy()
    t["sin_s0"] = np.broadcast_to(np.sin(j0 * theta).reshape(1, -1),
                                  (128, 96 * HALF)).astype(f32).copy()
    t["ident_bf"] = np.eye(128).astype(bfdt)
    t["ident_f32"] = np.eye(128).astype(f32)
    t["colv"] = np.broadcast_to(np.arange(L, dtype=f32).reshape(1, L),
                                (128, L)).copy()
    p = np.arange(128)[:, None]
    s = np.arange(192)[None, :]
    t["band"] = np.where((s >= p + 1) & (s <= p + 64), 0.0, NEG).astype(f32)
    c0 = np.arange(128)[None, :].astype(np.float64)
    q0 = np.arange(128)[:, None].astype(np.float64)
    loc = (c0 >= q0 - 63) & (c0 <= q0)
    fut = c0 > q0
    t["m0"] = np.where(loc, 2e9 - 1e3 * c0,
                       np.where(fut, -2e9 - 1e3 * c0,
                                -EPS_TIE * c0)).astype(f32)
    t["iota64"] = np.broadcast_to(np.arange(NE, dtype=f32).reshape(1, NE),
                                  (128, NE)).copy()
    t["pvec"] = np.arange(128, dtype=f32).reshape(128, 1).copy()
    t["qm63"] = (np.arange(NT, dtype=f32)[None, :] * 128.0 - 63.0
                 + np.arange(128, dtype=f32)[:, None]).copy()
    t["ones_row_f"] = np.ones((1, 128), f32)
    _HOST_CACHE.update(t)
    return t


class Sem:
    def __init__(self, handle, name):
        self.h = handle
        self.c = 0
        self.name = name


class Prog:

    def __init__(self):
        self.ops = {e: [] for e in ENGINES}
        self.n = 0
        self.marks = {}

    def mark(self, name):
        self.marks[name] = {e: len(self.ops[e]) for e in ENGINES}

    def truncate(self, name):
        if name is None or name not in self.marks:
            return
        for e in ENGINES:
            del self.ops[e][self.marks[name][e]:]

    def emit(self, eng, fn):
        self.ops[eng].append(fn)
        self.n += 1

    def wait(self, eng, sem, val=None):
        v = sem.c if val is None else val
        if v > 0:
            self.emit(eng, lambda q, s=sem.h, v=v: q.wait_ge(s, v))

    def dma(self, eng, out, in_, sem, **kw):
        sem.c += 16
        self.emit(eng, lambda q, o=out, i=in_, s=sem.h, kw=kw:
                  q.dma_start(out=o, in_=i, **kw).then_inc(s, 16))
        return sem.c

    DRAIN_AFTER = ("tensor_reduce", "reciprocal")
    NO_RT = ("max", "max_index", "match_replace", "drain", "engine_nop")
    rt_sem = None
    paranoid = True

    def op(self, eng, meth, *args, sem=None, **kw):
        need_drain = (eng == "vector" and meth in self.DRAIN_AFTER)
        need_rt = False
        if (eng in ("vector", "scalar") and meth not in self.NO_RT
                and not need_drain and args):
            if self.paranoid:
                need_rt = True
            else:
                try:
                    out0 = args[0] if not ("out" in kw) else kw["out"]
                    if hasattr(out0, "free_size") and out0.free_size <= 64:
                        need_rt = True
                except Exception:
                    pass
        if sem is not None:
            sem.c += 1
        use = sem if sem is not None else (self.rt_sem if need_rt else None)
        if sem is None and need_rt:
            self.rt_sem.c += 1

        def fn(q, m=meth, a=args, kw=kw, s=(None if need_drain else use)):
            r = getattr(q, m)(*a, **kw)
            if s is not None:
                r.then_inc(s.h, 1)
            return r
        self.emit(eng, fn)
        if need_drain:
            def fd(q, s=sem):
                r = q.drain()
                if s is not None:
                    r.then_inc(s.h, 1)
                return r
            self.emit(eng, fd)
        if need_rt and not need_drain:
            self.wait(eng, use)
        return None if sem is None else sem.c

    def gather(self, out_ap, table_ap, idx_col_ap, sem):
        sem.c += 16
        self.emit("gpsimd", lambda q, o=out_ap, t=table_ap, i=idx_col_ap,
                  s=sem.h: q.indirect_dma_start(
                      out=o, out_offset=None, in_=t,
                      in_offset=IndirectOffsetOnAxis(ap=i, axis=0)
                  ).then_inc(s, 16))
        return sem.c


def build_program(taps=(), limit=None):
    nc = bass.Bass("TRN2", target_bir_lowering=False)
    HT = host_tables()
    pr = Prog()

    d_in = {}

    def din(name, shape, dt=F32):
        d_in[name] = nc.dram_tensor(name, list(shape), dt,
                                    kind="ExternalInput").ap()
        return d_in[name]

    x_tok = din("x_tok", (L, D))
    x_fm = din("x_fm", (D, L))
    w_dkv = din("W_dkv", (D, DCKV))
    w_uk = din("W_uk", (DCKV, NH * DH))
    w_uv = din("W_uv", (DCKV, NH * DH))
    w_dq = din("W_dq", (D, DCQ))
    w_uq = din("W_uq", (DCQ, NH * DH))
    w_qr = din("W_qr", (DCQ, NH * DR))
    w_kr = din("W_kr", (D, DR))
    w_out = din("W_out", (NH * DH, D))
    idx_wq = din("idx_wq", (D, NIH * DH))
    idx_wk = din("idx_wk", (D, DH))
    idx_w = din("idx_w", (1, NIH))
    raw_delta = din("raw_delta", (1, HALF))
    cent_T = din("cent_T", (D, NE))
    exp_bias = din("exp_bias", (1, NE))
    ws_up = din("Ws_up", (D, FF))
    ws_down = din("Ws_down", (FF, D))
    w_up = din("W_up", (NE, D, FF))
    w_down = din("W_down", (NE, FF, D))
    for nm, arr in HT.items():
        din("t_" + nm, arr.shape, BF16 if arr.dtype == bfdt else F32)

    y_fm = nc.dram_tensor("y_fm", [D, L], F32, kind="ExternalOutput").ap()
    tap_out = {}

    def tap(name, shape, dt=F32):
        if name in taps:
            tap_out[name] = nc.dram_tensor("tap_" + name, list(shape), dt,
                                           kind="ExternalOutput").ap()
            return tap_out[name]
        return None

    def dscratch(name, shape, dt):
        return nc.dram_tensor(name, list(shape), dt, kind="Internal").ap()

    xn_fm_hbm = dscratch("xn_fm_hbm", (D, L), BF16)
    ktab = dscratch("ktab", (L, NH * DH + DR), BF16)
    vtab = dscratch("vtab", (L, NH * DH), BF16)
    shared_hbm = dscratch("shared_hbm", (D, L), F32)
    partial_hbm = dscratch("partial_hbm", (D, L), F32)
    cw_hbm = dscratch("cw_hbm", (NE, L), F32)
    h_hbm = dscratch("h_hbm", (NE, FF, L), BF16)
    kcT_hbm = dscratch("kcT_hbm", (NH * DH, L), BF16)
    krl_hbm = dscratch("krl_hbm", (DR, L), BF16)
    qrl_hbm = dscratch("qrl_hbm", (NH * DR, L), BF16)
    qc_hbm = dscratch("qc_hbm", (128, NT * NH * DH), BF16)
    qrot_hbm = dscratch("qrot_hbm", (128, NT * NH * DR), BF16)

    with ExitStack() as ctx:
        stacks = {}

        def sbuf(name, shape, dt=BF16, st="persist", side="left"):
            if st not in stacks:
                stacks[st] = ctx.enter_context(ExitStack())
            return stacks[st].enter_context(
                nc.sbuf_tensor(name, list(shape), dt, side=side))

        def rbuf(name, shape, dt=BF16, st=None):
            return sbuf(name, shape, dt, st=st, side="right")

        def sem(name):
            return Sem(ctx.enter_context(nc.semaphore(name)), name)

        sl = sem("sl")
        sg = sem("sg")
        sst = sem("sst")
        spe = sem("spe")
        sdv = sem("sdv")
        sac = sem("sac")
        srt = sem("srt")
        pr.rt_sem = srt
        ALLSEMS = (sl, sg, sst, spe, sdv, sac, srt)

        def barrier():
            for e in ENGINES:
                for s in ALLSEMS:
                    pr.wait(e, s)

        def free(*names):
            barrier()
            for nm in names:
                if nm in stacks:
                    stacks[nm].pop_all().close()
                    del stacks[nm]

        C = {}
        for nm in ("cos_q", "sin_q", "cos_2q", "sin_2q", "cos_k63", "sin_k63",
                   "cos_snl", "sin_snl", "ident_bf", "ident_f32", "band",
                   "m0", "iota64", "pvec", "qm63", "ones_row_f"):
            arr = HT[nm]
            dt = BF16 if arr.dtype == bfdt else F32
            C[nm] = sbuf("c_" + nm, arr.shape, dt)
            src_ap = d_in["t_" + nm]
            dst_ap = C[nm][:, :] if arr.ndim == 2 else C[nm][:, :, :]
            src_ap = src_ap[:, :] if arr.ndim == 2 else src_ap[:, :, :]
            pr.dma("sync", dst_ap, src_ap, sl)

        pr.wait("vector", sl)
        one_col = sbuf("one_col", (128, 1), F32)
        pr.op("vector", "memset", one_col[:, :], 1.0)
        epsln = sbuf("epsln", (128, 1), F32)
        pr.op("vector", "memset", epsln[:, :], 1e-5)
        ones1 = sbuf("ones1", (128, 1), F32)
        pr.op("vector", "memset", ones1[:, :], 1.0)
        halfpi = sbuf("halfpi", (1, 1), F32)
        pr.op("vector", "memset", halfpi[:, :], math.pi / 2.0, sem=sdv)

        idx32f = sbuf("idx32f", (128, NT, 32), F32)
        idx0f = sbuf("idx0f", (128, 96), F32)

        PS_A = ctx.enter_context(nc.psum_tensor("PS_A", [128, 2048], F32))
        PS_B = ctx.enter_context(nc.psum_tensor("PS_B", [128, 512], F32))
        PS_C = ctx.enter_context(nc.psum_tensor("PS_C", [128, 512], F32))
        PS_D = ctx.enter_context(nc.psum_tensor("PS_D", [128, 512], F32))
        PS_T = ctx.enter_context(nc.psum_tensor("PS_T", [128, 512], BF16))

        def wload(dst, src, kchunks):
            for kk in range(kchunks):
                pr.dma("gpsimd", dst[:, kk, :],
                       src[kk * 128:(kk + 1) * 128, :], sg)

        idxw_sb = sbuf("idxw_sb", (1, NIH), F32)
        pr.dma("sync", idxw_sb[:, :], idx_w[:, :], sl)
        rdelta_sb = sbuf("rdelta_sb", (1, HALF), F32)
        pr.dma("sync", rdelta_sb[:, :], raw_delta[:, :], sl)
        ebias_sb = sbuf("ebias_sb", (1, NE), F32)
        pr.dma("sync", ebias_sb[:, :], exp_bias[:, :], sl)

        dl = sbuf("dl", (1, 8 * HALF), F32)
        pr.wait("scalar", sl)
        pr.wait("scalar", sdv)
        sigv = dl[:, 0:HALF]
        hv = dl[:, HALF:2 * HALF]
        sinh = dl[:, 2 * HALF:3 * HALF]
        cosh = dl[:, 3 * HALF:4 * HALF]
        cosd1 = dl[:, 4 * HALF:5 * HALF]
        sind1 = dl[:, 5 * HALF:6 * HALF]
        pr.op("scalar", "activation", sigv, rdelta_sb[:, :], AF.Sigmoid)
        pr.op("scalar", "mul", hv, sigv, -math.pi)
        pr.op("scalar", "activation", sinh, hv, AF.Sin)
        pr.op("scalar", "activation", cosh, hv, AF.Sin, bias=halfpi[0:1, :],
              sem=sac)
        pr.wait("vector", sac)
        pr.op("vector", "tensor_tensor", cosd1, sinh, sinh, op=ALU.mult)
        pr.op("vector", "tensor_scalar", cosd1, cosd1, -2.0, 1.0,
              op0=ALU.mult, op1=ALU.add)
        pr.op("vector", "tensor_tensor", sind1, sinh, cosh, op=ALU.mult)
        pr.op("vector", "tensor_scalar", sind1, sind1, 2.0, None,
              op0=ALU.mult, sem=sdv)
        cosd = sbuf("cosd", (128, HALF), F32)
        sind = sbuf("sind", (128, HALF), F32)
        ebias_rep = sbuf("ebias_rep", (128, NE), F32)
        idxw_rep = sbuf("idxw_rep", (128, NIH), F32)
        pr.wait("tensor", sdv)
        pr.wait("tensor", sl)
        pr.op("tensor", "matmul", PS_D[:, 0:HALF], C["ones_row_f"][:, :],
              cosd1, start=True, stop=True)
        pr.op("tensor", "matmul", PS_D[:, HALF:2 * HALF],
              C["ones_row_f"][:, :], sind1, start=True, stop=True)
        pr.op("tensor", "matmul", PS_D[:, 64:64 + NE], C["ones_row_f"][:, :],
              ebias_sb[:, :], start=True, stop=True)
        pr.op("tensor", "matmul", PS_D[:, 128:128 + NIH],
              C["ones_row_f"][:, :], idxw_sb[:, :], start=True, stop=True,
              sem=spe)
        pr.wait("vector", spe)
        pr.op("vector", "tensor_copy", cosd[:, :], PS_D[:, 0:HALF])
        pr.op("vector", "tensor_copy", sind[:, :], PS_D[:, HALF:2 * HALF])
        pr.op("vector", "tensor_copy", ebias_rep[:, :], PS_D[:, 64:64 + NE])
        pr.op("vector", "tensor_copy", idxw_rep[:, :],
              PS_D[:, 128:128 + NIH], sem=sdv)

        def make_slot_tables(cosrep_ap, sinrep_ap, nslots, nm, st="persist",
                             side="left"):
            ca = sbuf("ca_" + nm, (128, nslots * HALF), F32, st=st, side=side)
            sa = sbuf("sa_" + nm, (128, nslots * HALF), F32, st=st, side=side)
            t1 = sbuf("tmp_" + nm, (128, nslots * HALF), F32, st=st,
                      side=side)
            cv = cosrep_ap.rearrange("p (j f) -> p j f", f=HALF)
            sv = sinrep_ap.rearrange("p (j f) -> p j f", f=HALF)
            cosd3 = cosd[:, :].unsqueeze(1).to_broadcast([128, nslots, HALF])
            sind3 = sind[:, :].unsqueeze(1).to_broadcast([128, nslots, HALF])
            ca3 = ca[:, :].rearrange("p (j f) -> p j f", f=HALF)
            sa3 = sa[:, :].rearrange("p (j f) -> p j f", f=HALF)
            t13 = t1[:, :].rearrange("p (j f) -> p j f", f=HALF)
            pr.op("vector", "tensor_tensor", ca3, cv, cosd3, op=ALU.mult)
            pr.op("vector", "tensor_tensor", t13, sv, sind3, op=ALU.mult)
            pr.op("vector", "tensor_tensor", ca3, ca3, t13, op=ALU.subtract)
            pr.op("vector", "tensor_tensor", sa3, sv, cosd3, op=ALU.mult)
            pr.op("vector", "tensor_tensor", t13, cv, sind3, op=ALU.mult)
            pr.op("vector", "tensor_tensor", sa3, sa3, t13, op=ALU.add,
                  sem=sdv)
            return ca, sa

        ca_nl, sa_nl = make_slot_tables(C["cos_snl"][:, :],
                                        C["sin_snl"][:, :], 32, "nl")

        def angle_combine(cv, sv, cc, ss, tmp):
            pr.op("vector", "tensor_tensor", cc, cv, cosd[:, :], op=ALU.mult)
            pr.op("vector", "tensor_tensor", tmp, sv, sind[:, :], op=ALU.mult)
            pr.op("vector", "tensor_tensor", cc, cc, tmp, op=ALU.subtract)
            pr.op("vector", "tensor_tensor", ss, sv, cosd[:, :], op=ALU.mult)
            pr.op("vector", "tensor_tensor", tmp, cv, sind[:, :], op=ALU.mult)
            pr.op("vector", "tensor_tensor", ss, ss, tmp, op=ALU.add)

        def softplus(src, dst, scr):
            pr.wait("scalar", sdv)
            pr.op("scalar", "activation", scr, src, AF.Abs)
            pr.op("scalar", "activation", scr, scr, AF.Exp, scale=-1.0)
            pr.op("scalar", "activation", scr, scr, AF.Ln, bias=ones1[:, :])
            pr.op("scalar", "activation", dst, src, AF.Relu, sem=sac)
            pr.wait("vector", sac)
            pr.op("vector", "tensor_tensor", dst, dst, scr, op=ALU.add)

        xn_fm = [sbuf(f"xn_fm{c}", (128, L), st="xn") for c in range(8)]
        ln_x = rbuf("ln_x", (128, D), F32, st="ln")
        ln_sq = rbuf("ln_sq", (128, D), F32, st="ln")
        ln_st = rbuf("ln_st", (128, 8), F32, st="ln")
        ln_bf = rbuf("ln_bf", (128, D), st="ln")
        for t in range(NT):
            pr.wait("sync", sdv)
            pr.dma("sync", ln_x[:, :], x_tok[t * 128:(t + 1) * 128, :], sl)
            pr.wait("vector", sl)
            pr.wait("vector", sac)
            mean = ln_st[:, 0:1]
            m2 = ln_st[:, 1:2]
            var = ln_st[:, 2:3]
            lnv = ln_st[:, 3:4]
            rstd = ln_st[:, 4:5]
            pr.op("vector", "tensor_reduce", mean, ln_x[:, :], axis=AX.X,
                  op=ALU.add)
            pr.op("vector", "tensor_scalar", mean, mean, 1.0 / D, None,
                  op0=ALU.mult)
            pr.op("vector", "tensor_copy", ln_st[:, 5:6], ln_st[:, 0:1],
                  sem=sdv)
            pr.wait("scalar", sdv)
            pr.op("scalar", "activation", ln_sq[:, :], ln_x[:, :], AF.Square,
                  sem=sac)
            pr.wait("vector", sac)
            pr.op("vector", "tensor_reduce", m2, ln_sq[:, :], axis=AX.X,
                  op=ALU.add)
            pr.op("vector", "tensor_scalar", m2, m2, 1.0 / D, None,
                  op0=ALU.mult)
            pr.op("vector", "tensor_copy", ln_st[:, 6:7], ln_st[:, 1:2],
                  sem=sdv)
            pr.wait("scalar", sdv)
            pr.op("scalar", "activation", var, mean, AF.Square, sem=sac)
            pr.wait("vector", sac)
            pr.op("vector", "tensor_tensor", var, m2, var, op=ALU.subtract,
                  sem=sdv)
            pr.wait("scalar", sdv)
            pr.op("scalar", "activation", lnv, var, AF.Ln, bias=epsln[:, :])
            pr.op("scalar", "activation", rstd, lnv, AF.Exp, scale=-0.5,
                  sem=sac)
            pr.wait("vector", sac)
            pr.op("vector", "tensor_scalar", ln_bf[:, :], ln_x[:, :], mean,
                  rstd, op0=ALU.subtract, op1=ALU.mult, sem=sdv)
            pr.wait("tensor", sdv)
            for c in range(8):
                slot = (c % 4) * 128
                pr.wait("tensor", sac)
                pr.op("tensor", "transpose", PS_T[:, slot:slot + 128],
                      ln_bf[:, c * 128:(c + 1) * 128], C["ident_bf"][:, :],
                      sem=spe)
                pr.wait("scalar", spe)
                pr.op("scalar", "copy", xn_fm[c][:, t * 128:(t + 1) * 128],
                      PS_T[:, slot:slot + 128], sem=sac)
        barrier()
        for c in range(8):
            pr.dma("sync", xn_fm_hbm[c * 128:(c + 1) * 128, :],
                   xn_fm[c][:, :], sst)
        if tap("xn_fm0", (128, L), BF16) is not None:
            pr.dma("sync", tap_out["xn_fm0"][:, :], xn_fm[0][:, :], sst)
        free("ln")
        pr.mark("aa")

        def fm_chain(dst_chunks, wsb, kchunks, mtot, src_chunks=None,
                     mchunk=128):
            src = src_chunks or xn_fm
            nm = (mtot + mchunk - 1) // mchunk
            for ts in range(L // TSL):
                for mc in range(nm):
                    mw = min(mchunk, mtot - mc * mchunk)
                    ps = PS_B if (mc % 2 == 0) else PS_C
                    pr.wait("tensor", sdv)
                    for kk in range(kchunks):
                        pr.op("tensor", "matmul", ps[0:mw, 0:TSL],
                              wsb[:, kk, mc * mchunk:mc * mchunk + mw],
                              src[kk][:, ts * TSL:(ts + 1) * TSL],
                              start=(kk == 0), stop=(kk == kchunks - 1),
                              sem=(spe if kk == kchunks - 1 else None))
                    pr.wait("vector", spe)
                    pr.op("vector", "tensor_copy",
                          dst_chunks[mc][0:mw, ts * TSL:(ts + 1) * TSL],
                          ps[0:mw, 0:TSL], sem=sdv)

        def fm_chain_hbm(dst_hbm, stage, wsb, kchunks, mtot,
                         src_chunks=None):
            src = src_chunks or xn_fm
            nm = (mtot + 127) // 128
            for ts in range(L // TSL):
                for mc in range(nm):
                    mw = min(128, mtot - mc * 128)
                    pr.wait("tensor", sdv)
                    pr.wait("tensor", sst)
                    for kk in range(kchunks):
                        pr.op("tensor", "matmul", PS_B[0:mw, 0:TSL],
                              wsb[:, kk, mc * 128:mc * 128 + mw],
                              src[kk][:, ts * TSL:(ts + 1) * TSL],
                              start=(kk == 0), stop=(kk == kchunks - 1),
                              sem=(spe if kk == kchunks - 1 else None))
                    pr.wait("vector", spe)
                    pr.wait("vector", sst)
                    pr.op("vector", "tensor_copy", stage[0:mw, :],
                          PS_B[0:mw, 0:TSL], sem=sdv)
                    pr.wait("sync", sdv)
                    pr.dma("sync",
                           dst_hbm[mc * 128:mc * 128 + mw,
                                   ts * TSL:(ts + 1) * TSL],
                           stage[0:mw, :], sst)

        k_iT = rbuf("k_iT", (64, L), st="qk")
        q_iT = [rbuf(f"q_iT{c}", (64, L), st="qk") for c in range(4)]

        W1 = {}
        W1["idx_wq"] = rbuf("s_idx_wq", (128, 8, NIH * DH), st="w1")
        wload(W1["idx_wq"], idx_wq, 8)
        W1["idx_wk"] = rbuf("s_idx_wk", (128, 8, DH), st="w1")
        wload(W1["idx_wk"], idx_wk, 8)
        W1["w_kr"] = rbuf("s_w_kr", (128, 8, DR), st="w1")
        wload(W1["w_kr"], w_kr, 8)
        W1["cent"] = rbuf("s_cent", (128, 8, NE), st="w1")
        wload(W1["cent"], cent_T, 8)
        W1["ws_up"] = rbuf("s_ws_up", (128, 8, FF), st="w1")
        wload(W1["ws_up"], ws_up, 8)
        W1["ws_down"] = rbuf("s_ws_down", (128, 4, D), st="w1")
        wload(W1["ws_down"], ws_down, 4)
        pr.wait("tensor", sg)

        fm_chain([k_iT], W1["idx_wk"], 8, DH)
        fm_chain(q_iT, W1["idx_wq"], 8, NIH * DH, mchunk=64)
        if tap("k_iT", (64, L), BF16) is not None:
            pr.wait("sync", sdv)
            pr.dma("sync", tap_out["k_iT"][:, :], k_iT[:, :], sst)

        kr_f = rbuf("kr_f", (128, DR), F32, st="w1")
        kr_scr = rbuf("kr_scr", (128, 4 * DR), F32, st="w1")
        kr_bf = rbuf("kr_bf", (128, DR), st="w1")
        krl_st = rbuf("krl_st", (32, 128), st="w1")
        ang_t = rbuf("ang_t", (128, 3 * HALF), F32, st="w1")
        for t in range(NT):
            pr.wait("tensor", sdv)
            pr.wait("tensor", sst)
            for kk in range(8):
                pr.op("tensor", "matmul", PS_D[:, 0:DR],
                      xn_fm[kk][:, t * 128:(t + 1) * 128],
                      W1["w_kr"][:, kk, :], start=(kk == 0), stop=(kk == 7),
                      sem=(spe if kk == 7 else None))
            pr.wait("vector", spe)
            pr.op("vector", "tensor_copy", kr_f[:, :], PS_D[:, 0:DR])
            pr.op("vector", "tensor_copy", kr_bf[:, :], PS_D[:, 0:DR],
                  sem=sdv)
            pr.wait("sync", sdv)
            pr.dma("sync", ktab[t * 128:(t + 1) * 128, NH * DH:NH * DH + DR],
                   kr_bf[:, :], sst)
            sp_ = kr_scr[:, 0:DR]
            scr = kr_scr[:, DR:2 * DR]
            softplus(kr_f[:, :], sp_, scr)
            cc = ang_t[:, 0:HALF]
            ss = ang_t[:, HALF:2 * HALF]
            tmp = ang_t[:, 2 * HALF:3 * HALF]
            angle_combine(C["cos_k63"][:, t, :], C["sin_k63"][:, t, :],
                          cc, ss, tmp)
            mu1 = sp_[:, 0:HALF]
            mu2 = sp_[:, HALF:DR]
            t2 = kr_scr[:, 2 * DR:2 * DR + HALF]
            t3 = kr_scr[:, 2 * DR + HALF:3 * DR]
            pr.wait("vector", sst)
            pr.op("vector", "tensor_tensor", t2, mu1, cc, op=ALU.mult)
            pr.op("vector", "tensor_tensor", t3, mu2, ss, op=ALU.mult)
            pr.op("vector", "tensor_tensor", kr_bf[:, 0:HALF], t2, t3,
                  op=ALU.subtract)
            pr.op("vector", "tensor_tensor", t2, mu1, ss, op=ALU.mult)
            pr.op("vector", "tensor_tensor", t3, mu2, cc, op=ALU.mult)
            pr.op("vector", "tensor_tensor", kr_bf[:, HALF:DR], t2, t3,
                  op=ALU.add, sem=sdv)
            pr.wait("tensor", sdv)
            pr.wait("tensor", sac)
            pr.op("tensor", "transpose", PS_T[0:32, 0:128], kr_bf[:, :],
                  C["ident_bf"][:, :], sem=spe)
            pr.wait("scalar", spe)
            pr.wait("scalar", sst)
            pr.op("scalar", "copy", krl_st[:, :], PS_T[0:32, 0:128], sem=sac)
            pr.wait("sync", sac)
            pr.dma("sync", krl_hbm[:, t * 128:(t + 1) * 128], krl_st[:, :],
                   sst)

        sh_h = rbuf("sh_h", (128, 4, TSL), st="w1")
        sh_o = rbuf("sh_o", (128, TSL), F32, st="w1")
        for ts in range(L // TSL):
            for fc in range(4):
                pr.wait("tensor", sac)
                pr.wait("tensor", sst)
                for kk in range(8):
                    pr.op("tensor", "matmul", PS_B[:, 0:TSL],
                          W1["ws_up"][:, kk, fc * 128:(fc + 1) * 128],
                          xn_fm[kk][:, ts * TSL:(ts + 1) * TSL],
                          start=(kk == 0), stop=(kk == 7),
                          sem=(spe if kk == 7 else None))
                pr.wait("scalar", spe)
                pr.op("scalar", "activation", sh_h[:, fc, :], PS_B[:, 0:TSL],
                      AF.Silu, sem=sac)
            for dd in range(8):
                pr.wait("tensor", sac)
                pr.wait("tensor", sdv)
                for fc in range(4):
                    pr.op("tensor", "matmul", PS_C[:, 0:TSL],
                          W1["ws_down"][:, fc, dd * 128:(dd + 1) * 128],
                          sh_h[:, fc, :], start=(fc == 0), stop=(fc == 3),
                          sem=(spe if fc == 3 else None))
                pr.wait("vector", spe)
                pr.wait("vector", sst)
                pr.op("vector", "tensor_copy", sh_o[:, :], PS_C[:, 0:TSL],
                      sem=sdv)
                pr.wait("sync", sdv)
                pr.dma("sync", shared_hbm[dd * 128:(dd + 1) * 128,
                                          ts * TSL:(ts + 1) * TSL],
                       sh_o[:, :], sst)

        aff = rbuf("aff", (128, NE), F32, st="w1")
        rt_sc = rbuf("rt_sc", (128, NE + 8), F32, st="w1")
        top8v = rbuf("top8v", (128, 8), F32, st="w1")
        idx8u = rbuf("idx8u", (128, 8), U32, st="w1")
        idx8f = rbuf("idx8f", (128, 8), F32, st="w1")
        wsel = rbuf("wsel", (128, 8), F32, st="w1")
        eq3 = rbuf("eq3", (128, 8 * NE), F32, st="w1")
        cwt = rbuf("cwt", (128, NE), F32, st="w1")
        cw_fm = rbuf("cw_fm", (64, L), F32, st="w1")
        for t in range(NT):
            pr.wait("tensor", sdv)
            for kk in range(8):
                pr.op("tensor", "matmul", PS_D[:, 0:NE],
                      xn_fm[kk][:, t * 128:(t + 1) * 128],
                      W1["cent"][:, kk, :], start=(kk == 0), stop=(kk == 7),
                      sem=(spe if kk == 7 else None))
            pr.wait("vector", spe)
            pr.op("vector", "tensor_copy", aff[:, :], PS_D[:, 0:NE], sem=sdv)
            biased = rt_sc[:, 0:NE]
            pr.op("vector", "tensor_tensor", biased, aff[:, :],
                  ebias_rep[:, :], op=ALU.add)
            pr.op("vector", "max", out=top8v[:, :], in_=biased, sem=sdv)
            pr.wait("vector", sdv)
            pr.op("vector", "max_index", idx8u[:, :], top8v[:, :], biased,
                  sem=sdv)
            pr.wait("vector", sdv)
            pr.op("vector", "tensor_copy", idx8f[:, :], idx8u[:, :])
            eqv = eq3[:, :].rearrange("p (s e) -> p s e", e=NE)
            io3 = C["iota64"][:, :].unsqueeze(1).to_broadcast([128, 8, NE])
            id3 = idx8f[:, :].unsqueeze(2).to_broadcast([128, 8, NE])
            pr.op("vector", "tensor_tensor", eqv, io3, id3, op=ALU.is_equal)
            af3 = aff[:, :].unsqueeze(1).to_broadcast([128, 8, NE])
            pr.op("vector", "tensor_tensor", eqv, eqv, af3, op=ALU.mult)
            pr.op("vector", "tensor_reduce", wsel[:, :], eqv, axis=AX.X,
                  op=ALU.add, sem=sdv)
            pr.wait("scalar", sdv)
            pr.op("scalar", "activation", wsel[:, :], wsel[:, :], AF.Sigmoid,
                  sem=sac)
            pr.wait("vector", sac)
            wsum = rt_sc[:, NE:NE + 1]
            pr.op("vector", "tensor_reduce", wsum, wsel[:, :], axis=AX.X,
                  op=ALU.add)
            pr.op("vector", "tensor_scalar", wsum, wsum, 1e-8, None,
                  op0=ALU.add)
            pr.op("vector", "reciprocal", wsum, wsum)
            pr.op("vector", "tensor_scalar", wsel[:, :], wsel[:, :], wsum,
                  None, op0=ALU.mult)
            eq2 = eq3[:, :].rearrange("p (e s) -> p e s", s=8)
            io2 = C["iota64"][:, :].unsqueeze(2).to_broadcast([128, NE, 8])
            id2 = idx8f[:, :].unsqueeze(1).to_broadcast([128, NE, 8])
            pr.op("vector", "tensor_tensor", eq2, io2, id2, op=ALU.is_equal)
            w2 = wsel[:, :].unsqueeze(1).to_broadcast([128, NE, 8])
            pr.op("vector", "tensor_tensor", eq2, eq2, w2, op=ALU.mult)
            pr.op("vector", "tensor_reduce", cwt[:, :], eq2, axis=AX.X,
                  op=ALU.add, sem=sdv)
            if t == 2:
                for tn, tb_ in (("aff2", aff), ("idx8f2", idx8f),
                                ("wsel2", wsel), ("cwt2", cwt),
                                ("eq32", eq3)):
                    if tap(tn, (128, tb_.shape[-1])) is not None:
                        pr.wait("sync", sdv)
                        pr.dma("sync", tap_out[tn][:, :], tb_[:, :], sst)
            pr.wait("tensor", sdv)
            pr.op("tensor", "transpose", PS_D[0:64, 128:256], cwt[:, :],
                  C["ident_f32"][:, :], sem=spe)
            pr.wait("vector", spe)
            pr.op("vector", "tensor_copy", cw_fm[:, t * 128:(t + 1) * 128],
                  PS_D[0:64, 128:256], sem=sdv)
            pr.wait("tensor", sdv)
        pr.wait("sync", sdv)
        pr.dma("sync", cw_hbm[:, :], cw_fm[:, :], sst)
        if tap("cw_fm", (64, L)) is not None:
            pr.dma("sync", tap_out["cw_fm"][:, :], cw_fm[:, :], sst)
        free("w1")
        pr.mark("ab1")

        c_qT = [rbuf(f"c_qT{c}", (128, L), st="cq") for c in range(4)]
        c_kvT = [rbuf(f"c_kvT{c}", (128, L), st="ckv") for c in range(2)]
        W2 = {}
        W2["w_dkv"] = rbuf("s_w_dkv", (128, 8, DCKV), st="w2")
        wload(W2["w_dkv"], w_dkv, 8)
        W2["w_dq"] = rbuf("s_w_dq", (128, 8, DCQ), st="w2")
        wload(W2["w_dq"], w_dq, 8)
        pr.wait("tensor", sg)
        fm_chain(c_kvT, W2["w_dkv"], 8, DCKV)
        fm_chain(c_qT, W2["w_dq"], 8, DCQ)
        if tap("c_kvT0", (128, L), BF16) is not None:
            pr.wait("sync", sdv)
            pr.dma("sync", tap_out["c_kvT0"][:, :], c_kvT[0][:, :], sst)
        free("w2")
        pr.mark("ab2")

        W3 = {}
        W3["w_uk"] = rbuf("s_w_uk", (128, 2, NH * DH), st="w3")
        wload(W3["w_uk"], w_uk, 2)
        W3["w_uv"] = rbuf("s_w_uv", (128, 2, NH * DH), st="w3")
        wload(W3["w_uv"], w_uv, 2)
        kc_stage = rbuf("kc_stage", (128, TSL), st="w3")
        kv_st = rbuf("kv_st", (128, NH * DH), st="w3")
        pr.wait("tensor", sg)
        fm_chain_hbm(kcT_hbm, kc_stage, W3["w_uk"], 2, NH * DH,
                     src_chunks=c_kvT)
        for t in range(NT):
            for which in ("kc", "v"):
                wsb = W3["w_uk"] if which == "kc" else W3["w_uv"]
                pr.wait("tensor", sac)
                pr.wait("tensor", sst)
                for kk in range(2):
                    pr.op("tensor", "matmul", PS_B[:, 0:NH * DH],
                          c_kvT[kk][:, t * 128:(t + 1) * 128], wsb[:, kk, :],
                          start=(kk == 0), stop=(kk == 1),
                          sem=(spe if kk == 1 else None))
                pr.wait("scalar", spe)
                pr.wait("scalar", sst)
                pr.op("scalar", "copy", kv_st[:, :], PS_B[:, 0:NH * DH],
                      sem=sac)
                pr.wait("sync", sac)
                if which == "kc":
                    pr.dma("sync", ktab[t * 128:(t + 1) * 128, 0:NH * DH],
                           kv_st[:, :], sst)
                else:
                    pr.dma("sync", vtab[t * 128:(t + 1) * 128, :],
                           kv_st[:, :], sst)
        free("w3", "ckv")
        pr.mark("ac")

        W4 = {}
        W4["w_uq"] = rbuf("s_w_uq", (128, 4, NH * DH), st="w4")
        wload(W4["w_uq"], w_uq, 4)
        W4["w_qr"] = rbuf("s_w_qr", (128, 4, NH * DR), st="w4")
        wload(W4["w_qr"], w_qr, 4)
        qc_st = rbuf("qc_st", (128, NH * DH), st="w4")
        qrot_st = rbuf("qrot_st", (128, NH * DR), st="w4")
        qrl_st = rbuf("qrl_st", (128, NH * DR), st="w4")
        qp_f = rbuf("qp_f", (128, NH * DR), F32, st="w4")
        qp_sp = rbuf("qp_sp", (128, 2 * NH * DR), F32, st="w4")
        qang = rbuf("qang", (128, 3 * HALF), F32, st="w4")
        qrl_bf = rbuf("qrl_bf", (128, NH * DR), st="w4")
        pr.wait("tensor", sg)

        def pope_heads(src_f32, cc, ss, dst_bf, scr2):
            n32 = NH * DR
            sp_ = scr2[:, 0:n32]
            scr = scr2[:, n32:2 * n32]
            softplus(src_f32, sp_, scr)
            sp3 = sp_.rearrange("p (h two f) -> p h two f", two=2, f=HALF)
            mu1 = sp3[:, :, 0, :]
            mu2 = sp3[:, :, 1, :]
            d3 = dst_bf.rearrange("p (h two f) -> p h two f", two=2, f=HALF)
            cc3 = cc.unsqueeze(1).to_broadcast([128, NH, HALF])
            ss3 = ss.unsqueeze(1).to_broadcast([128, NH, HALF])
            t3 = scr.rearrange("p (h two f) -> p h two f", two=2, f=HALF)
            pr.op("vector", "tensor_tensor", t3[:, :, 0, :], mu1, cc3,
                  op=ALU.mult)
            pr.op("vector", "tensor_tensor", t3[:, :, 1, :], mu2, ss3,
                  op=ALU.mult)
            pr.op("vector", "tensor_tensor", d3[:, :, 0, :], t3[:, :, 0, :],
                  t3[:, :, 1, :], op=ALU.subtract)
            pr.op("vector", "tensor_tensor", t3[:, :, 0, :], mu1, ss3,
                  op=ALU.mult)
            pr.op("vector", "tensor_tensor", t3[:, :, 1, :], mu2, cc3,
                  op=ALU.mult)
            pr.op("vector", "tensor_tensor", d3[:, :, 1, :], t3[:, :, 0, :],
                  t3[:, :, 1, :], op=ALU.add, sem=sdv)

        for t in range(NT):
            pr.wait("tensor", sdv)
            pr.wait("tensor", sac)
            pr.wait("tensor", sst)
            for kk in range(4):
                pr.op("tensor", "matmul", PS_B[:, 0:NH * DH],
                      c_qT[kk][:, t * 128:(t + 1) * 128],
                      W4["w_uq"][:, kk, :], start=(kk == 0), stop=(kk == 3),
                      sem=(spe if kk == 3 else None))
            pr.wait("scalar", spe)
            pr.wait("scalar", sst)
            pr.op("scalar", "copy", qc_st[:, :], PS_B[:, 0:NH * DH], sem=sac)
            pr.wait("sync", sac)
            pr.dma("sync", qc_hbm[:, t * NH * DH:(t + 1) * NH * DH],
                   qc_st[:, :], sst)
            pr.wait("tensor", sac)
            for kk in range(4):
                pr.op("tensor", "matmul", PS_C[:, 0:NH * DR],
                      c_qT[kk][:, t * 128:(t + 1) * 128],
                      W4["w_qr"][:, kk, :], start=(kk == 0), stop=(kk == 3),
                      sem=(spe if kk == 3 else None))
            pr.wait("vector", spe)
            pr.op("vector", "tensor_copy", qp_f[:, :], PS_C[:, 0:NH * DR],
                  sem=sdv)
            cc = qang[:, 0:HALF]
            ss = qang[:, HALF:2 * HALF]
            tmp = qang[:, 2 * HALF:3 * HALF]
            pr.wait("vector", sst)
            angle_combine(C["cos_q"][:, t, :], C["sin_q"][:, t, :], cc, ss,
                          tmp)
            pope_heads(qp_f[:, :], cc, ss, qrot_st[:, :], qp_sp)
            pr.wait("sync", sdv)
            pr.dma("sync", qrot_hbm[:, t * NH * DR:(t + 1) * NH * DR],
                   qrot_st[:, :], sst)
            angle_combine(C["cos_2q"][:, t, :], C["sin_2q"][:, t, :], cc, ss,
                          tmp)
            pope_heads(qp_f[:, :], cc, ss, qrl_bf[:, :], qp_sp)
            pr.wait("tensor", sdv)
            pr.wait("tensor", sac)
            for c in range(2):
                pr.op("tensor", "transpose", PS_T[:, c * 128:(c + 1) * 128],
                      qrl_bf[:, c * 128:(c + 1) * 128], C["ident_bf"][:, :],
                      sem=(spe if c == 1 else None))
            pr.wait("scalar", spe)
            pr.wait("scalar", sst)
            for c in range(2):
                pr.op("scalar", "copy", qrl_st[:, c * 128:(c + 1) * 128],
                      PS_T[:, c * 128:(c + 1) * 128],
                      sem=(sac if c == 1 else None))
            pr.wait("sync", sac)
            for c in range(2):
                pr.dma("sync",
                       qrl_hbm[c * 128:(c + 1) * 128, t * 128:(t + 1) * 128],
                       qrl_st[:, c * 128:(c + 1) * 128], sst)
        free("w4", "cq", "xn")
        pr.mark("ad")

        colv = rbuf("colv_b1", (128, L), F32, st="b1x")
        pr.dma("sync", colv[:, :], d_in["t_colv"][:, :], sl)
        Ibuf = rbuf("Ibuf", (128, L), F32, st="b1x")
        relu_b = rbuf("relu_b", (128, 4, 512), F32, st="b1x")
        qm63 = rbuf("qm63", (128, 1), F32, st="b1x")
        i8v = rbuf("i8v", (128, 8), F32, st="b1x")
        i8u = rbuf("i8u", (128, 8), U32, st="b1x")
        pr.wait("vector", sl)

        for t in range(NT):
            q0 = t * 128
            Lt = 128 if t == 0 else q0 + 64
            if t == 0:
                pr.op("vector", "tensor_copy", Ibuf[:, 0:128], C["m0"][:, :],
                      sem=sdv)
            else:
                for cc_ in range(0, Lt, 512):
                    cw_ = min(512, Lt - cc_)
                    pen = Ibuf[:, cc_:cc_ + cw_]
                    pr.op("vector", "tensor_scalar", pen,
                          colv[:, cc_:cc_ + cw_], C["qm63"][:, t:t + 1], NEG,
                          op0=ALU.is_ge, op1=ALU.mult)
                    pr.op("vector", "scalar_tensor_tensor", pen,
                          colv[:, cc_:cc_ + cw_], -EPS_TIE, pen,
                          op0=ALU.mult, op1=ALU.add, sem=sdv)
            for cc_ in range(0, Lt, 512):
                cw_ = min(512, Lt - cc_)
                pr.wait("tensor", sac)
                pr.wait("tensor", sdv)
                for h in range(NIH):
                    pr.op("tensor", "matmul", PS_A[:, h * 512:h * 512 + cw_],
                          q_iT[h][:, q0:q0 + 128],
                          k_iT[0:64, cc_:cc_ + cw_], start=True, stop=True,
                          sem=(spe if h == NIH - 1 else None))
                pr.wait("scalar", spe)
                for h in range(NIH):
                    pr.op("scalar", "activation", relu_b[:, h, 0:cw_],
                          PS_A[:, h * 512:h * 512 + cw_], AF.Relu,
                          sem=(sac if h == NIH - 1 else None))
                pr.wait("vector", sac)
                for h in range(NIH):
                    pr.op("vector", "scalar_tensor_tensor",
                          Ibuf[:, cc_:cc_ + cw_], relu_b[:, h, 0:cw_],
                          idxw_rep[:, h:h + 1], Ibuf[:, cc_:cc_ + cw_],
                          op0=ALU.mult, op1=ALU.add,
                          sem=(sdv if h == NIH - 1 else None))
            pr.wait("vector", sdv)
            nrounds = 12 if t == 0 else 4
            for r in range(nrounds):
                pr.op("vector", "max", out=i8v[:, :], in_=Ibuf[:, 0:Lt],
                      sem=sdv)
                pr.wait("vector", sdv)
                pr.op("vector", "max_index", i8u[:, :], i8v[:, :],
                      Ibuf[:, 0:Lt], sem=sdv)
                pr.wait("vector", sdv)
                if t == 0:
                    pr.op("vector", "tensor_copy", idx0f[:, r * 8:r * 8 + 8],
                          i8u[:, :])
                else:
                    pr.op("vector", "tensor_copy",
                          idx32f[:, t, r * 8:r * 8 + 8], i8u[:, :])
                if r < nrounds - 1:
                    pr.op("vector", "match_replace", Ibuf[:, 0:Lt],
                          i8v[:, :], Ibuf[:, 0:Lt], imm_value=NEG, sem=sdv)
                    pr.wait("vector", sdv)
            pr.op("vector", "tensor_copy", i8v[:, :], i8v[:, :], sem=sdv)
        if tap("idx32f", (128, NT * 32)) is not None:
            pr.wait("sync", sdv)
            pr.dma("sync", tap_out["idx32f"][:, :],
                   idx32f[:, :, :].rearrange("p t s -> p (t s)"), sst)
        if tap("idx0f", (128, 96)) is not None:
            pr.wait("sync", sdv)
            pr.dma("sync", tap_out["idx0f"][:, :], idx0f[:, :], sst)
        free("b1x", "qk")
        pr.mark("b1")

        gbuf = rbuf("gbuf", (128, NSL * (NH * DH + DR)), st="b2")
        kg = gbuf[:, :].rearrange("p (j m) -> p j m", m=NH * DH + DR)
        vg = gbuf[:, 0:NSL * NH * DH].rearrange("p (j m) -> p j m",
                                                m=NH * DH)
        idxu = rbuf("idxu", (128, 96), U32, st="b2")
        vsA = rbuf("vsA", (128, NH * DH), st="b2")
        vsB = rbuf("vsB", (64, NH * DH), st="b2")
        kcs = rbuf("kcs", (64, 8, 192), st="b2")
        krls = rbuf("krls", (32, 192), st="b2")
        qrls = rbuf("qrls", (32, 8, 128), st="b2")
        qct = rbuf("qct", (128, NH * DH), st="b2")
        qrott = rbuf("qrott", (128, NH * DR), st="b2")
        qcfm = rbuf("qcfm", (64, 8, 128), st="b2")
        scores = rbuf("scores", (128, NH, 224), F32, st="b2")
        prod = rbuf("prod", (128, 8192), st="b2")
        tree = rbuf("tree", (128, 4096), st="b2")
        krot = rbuf("krot", (128, NSL, DR), st="b2")
        ksp = rbuf("ksp", (128, 2 * NSL * DR), F32, st="b2")
        scsave = rbuf("scsave", (128, NSL * NH), F32, st="b2")
        smx = rbuf("smx", (128, 2 * NH), F32, st="b2")
        attn_bf = rbuf("attn_bf", (128, NH, 224), st="b2")
        attnT_A = rbuf("attnT_A", (128, NH, 128), st="b2")
        attnT_B = rbuf("attnT_B", (64, NH, 128), st="b2")
        nl_acc = rbuf("nl_acc", (128, NH * DH), F32, st="b2")
        mix_bf = rbuf("mix_bf", (128, NH * DH), st="b2")
        mixT = rbuf("mixT", (128, 4, 128), st="b2")
        po_x = rbuf("po_x", (128, 128), F32, st="b2")
        po_sh = rbuf("po_sh", (128, 128), F32, st="b2")
        po_o = rbuf("po_o", (128, 128), F32, st="b2")
        w_out_sb = rbuf("w_out_sb", (128, 4, D), st="b2")
        wload(w_out_sb, w_out, 4)

        c_s0 = rbuf("c_s0", (128, 96 * HALF), F32, st="s0")
        s_s0 = rbuf("s_s0", (128, 96 * HALF), F32, st="s0")
        pr.dma("sync", c_s0[:, :], d_in["t_cos_s0"][:, :], sl)
        pr.dma("sync", s_s0[:, :], d_in["t_sin_s0"][:, :], sl)
        pr.wait("vector", sl)
        ca_s0, sa_s0 = make_slot_tables(c_s0[:, :], s_s0[:, :], 96, "s0",
                                        st="s0", side="right")

        def nl_scores_half(t, hf, ca_ap, sa_ap, sc_dst):
            kr_v = kg[:, :, NH * DH:NH * DH + DR]
            n32 = NSL * DR
            sp_ = ksp[:, 0:n32]
            scr = ksp[:, n32:2 * n32]
            softplus(kr_v, sp_, scr)
            sp3 = sp_.rearrange("p (j two f) -> p j two f", two=2, f=HALF)
            ca3 = ca_ap[:, :].rearrange("p (j f) -> p j f", f=HALF)[
                :, hf * NSL:(hf + 1) * NSL, :]
            sa3 = sa_ap[:, :].rearrange("p (j f) -> p j f", f=HALF)[
                :, hf * NSL:(hf + 1) * NSL, :]
            k3 = krot[:, :, :].rearrange("p j (two f) -> p j two f",
                                         two=2, f=HALF)
            t3 = scr.rearrange("p (j two f) -> p j two f", two=2, f=HALF)
            pr.op("vector", "tensor_tensor", t3[:, :, 0, :], sp3[:, :, 0, :],
                  ca3, op=ALU.mult)
            pr.op("vector", "tensor_tensor", t3[:, :, 1, :], sp3[:, :, 1, :],
                  sa3, op=ALU.mult)
            pr.op("vector", "tensor_tensor", k3[:, :, 0, :], t3[:, :, 0, :],
                  t3[:, :, 1, :], op=ALU.subtract)
            pr.op("vector", "tensor_tensor", t3[:, :, 0, :], sp3[:, :, 0, :],
                  sa3, op=ALU.mult)
            pr.op("vector", "tensor_tensor", t3[:, :, 1, :], sp3[:, :, 1, :],
                  ca3, op=ALU.mult)
            pr.op("vector", "tensor_tensor", k3[:, :, 1, :], t3[:, :, 0, :],
                  t3[:, :, 1, :], op=ALU.add)
            pv = prod[:, 0:NSL * 512].rearrange("p (j m) -> p j m", m=512)
            qcb = qct[:, :].unsqueeze(1).to_broadcast([128, NSL, 512])
            pr.op("vector", "tensor_tensor", pv, kg[:, :, 0:NH * DH], qcb,
                  op=ALU.mult)
            src = prod[:, 0:NSL * 512]
            width = DH
            tv = tree
            while width > 1:
                half_ = width // 2
                s4 = src.rearrange("p (j h d) -> p j h d", h=NH, d=width)
                d4 = tv[:, 0:NSL * NH * half_].rearrange(
                    "p (j h d) -> p j h d", h=NH, d=half_)
                pr.op("vector", "tensor_tensor", d4, s4[:, :, :, 0:half_],
                      s4[:, :, :, half_:width], op=ALU.add)
                src = tv[:, 0:NSL * NH * half_]
                tv = prod if tv is tree else tree
                width = half_
            pr.op("vector", "tensor_copy", scsave[:, :], src)
            pr_r = tree[:, 0:NSL * NH * DR].rearrange(
                "p (j h f) -> p j h f", h=NH, f=DR)
            kro4 = krot[:, :, :].unsqueeze(2).to_broadcast(
                [128, NSL, NH, DR])
            qro4 = qrott[:, :].rearrange("p (h f) -> p h f",
                                         f=DR).unsqueeze(1).to_broadcast(
                [128, NSL, NH, DR])
            pr.op("vector", "tensor_tensor", pr_r, kro4, qro4, op=ALU.mult)
            src = tree[:, 0:NSL * NH * DR]
            width = DR
            tv = prod
            while width > 1:
                half_ = width // 2
                s4 = src.rearrange("p (j h f) -> p j h f", h=NH, f=width)
                d4 = tv[:, 0:NSL * NH * half_].rearrange(
                    "p (j h f) -> p j h f", h=NH, f=half_)
                pr.op("vector", "tensor_tensor", d4, s4[:, :, :, 0:half_],
                      s4[:, :, :, half_:width], op=ALU.add)
                src = tv[:, 0:NSL * NH * half_]
                tv = tree if tv is prod else prod
                width = half_
            scr_jh = src.rearrange("p (j h) -> p j h", h=NH)
            scv_jh = scsave[:, :].rearrange("p (j h) -> p j h", h=NH)
            base = 0 if t == 0 else 192
            dsl = sc_dst[:, :, base + hf * NSL:base + (hf + 1) * NSL]
            dsl = dsl.rearrange("p h j -> p j h")
            pr.op("vector", "tensor_tensor", scv_jh, scv_jh, scr_jh,
                  op=ALU.add)
            pr.op("vector", "tensor_scalar", dsl, scv_jh, SCALE, None,
                  op0=ALU.mult, sem=sdv)

        def nl_values_half(t, hf, first, kbase):
            a4 = attn_bf[:, :, kbase + hf * NSL:kbase + (hf + 1) * NSL]
            a4 = a4.rearrange("p h j -> p j h").unsqueeze(3).to_broadcast(
                [128, NSL, NH, DH])
            v4 = vg[:, :, :].rearrange("p j (h d) -> p j h d", d=DH)
            pv = prod[:, 0:NSL * 512].rearrange("p (j h d) -> p j h d",
                                                h=NH, d=DH)
            pr.op("vector", "tensor_tensor", pv, v4, a4, op=ALU.mult)
            src = prod[:, 0:NSL * 512]
            width = NSL
            tv = tree
            while width > 1:
                half_ = width // 2
                s3 = src.rearrange("p (j m) -> p j m", m=512)
                d3 = tv[:, 0:half_ * 512].rearrange("p (j m) -> p j m",
                                                    m=512)
                pr.op("vector", "tensor_tensor", d3, s3[:, 0:half_, :],
                      s3[:, half_:width, :], op=ALU.add)
                src = tv[:, 0:half_ * 512]
                tv = prod if tv is tree else tree
                width = half_
            if first:
                pr.op("vector", "tensor_copy", nl_acc[:, :], src, sem=sdv)
            else:
                pr.op("vector", "tensor_tensor", nl_acc[:, :], nl_acc[:, :],
                      src, op=ALU.add, sem=sdv)

        def b2_tile(t):
            q0 = t * 128
            nslots = 96 if t == 0 else 32
            kbase = 0 if t == 0 else 192
            nhalf = nslots // NSL
            ncols = 96 if t == 0 else 224
            ca_ap = ca_s0 if t == 0 else ca_nl
            sa_ap = sa_s0 if t == 0 else sa_nl
            pr.wait("vector", sdv)
            if t == 0:
                pr.op("vector", "tensor_copy", idxu[:, 0:96], idx0f[:, :],
                      sem=sdv)
            else:
                pr.op("vector", "tensor_copy", idxu[:, 0:32],
                      idx32f[:, t, :], sem=sdv)
            pr.wait("sync", spe)
            pr.wait("sync", sdv)
            if t > 0:
                pr.dma("sync", vsA[:, :], vtab[q0 - 64:q0 + 64, :], sl)
                pr.dma("sync", vsB[:, :], vtab[q0 + 64:q0 + 128, :], sl)
                pr.dma("sync", kcs[:, :, :],
                       kcT_hbm[:, :].rearrange("(h p) l -> p h l", p=64)[
                           :, :, q0 - 64:q0 + 128], sl)
                pr.dma("sync", krls[:, :], krl_hbm[:, q0 - 64:q0 + 128], sl)
                pr.dma("sync", qrls[:, :, :],
                       qrl_hbm[:, :].rearrange("(h p) l -> p h l", p=32)[
                           :, :, q0:q0 + 128], sl)
            pr.dma("sync", qct[:, :],
                   qc_hbm[:, t * NH * DH:(t + 1) * NH * DH], sl)
            pr.dma("sync", qrott[:, :],
                   qrot_hbm[:, t * NH * DR:(t + 1) * NH * DR], sl)
            if t > 0:
                pr.wait("tensor", sac)
                pr.wait("tensor", sl)
                for c in range(4):
                    pr.op("tensor", "transpose",
                          PS_T[:, c * 128:(c + 1) * 128],
                          qct[:, c * 128:(c + 1) * 128], C["ident_bf"][:, :],
                          sem=(spe if c == 3 else None))
                pr.wait("scalar", spe)
                for c in range(4):
                    pr.op("scalar", "copy", qcfm[:, 2 * c, :],
                          PS_T[0:64, c * 128:(c + 1) * 128])
                    pr.op("scalar", "copy", qcfm[:, 2 * c + 1, :],
                          PS_T[64:128, c * 128:(c + 1) * 128],
                          sem=(sac if c == 3 else None))
                if t == 2:
                    for tn, tb_, sh_ in (
                            ("kcs2", kcs, (64, 8 * 192)),
                            ("krls2", krls, (32, 192)),
                            ("qrls2", qrls, (32, 8 * 128)),
                            ("qcfm2", qcfm, (64, 8 * 128)),
                            ("qct2", qct, (128, 512))):
                        if tap(tn, sh_, BF16) is not None:
                            pr.wait("sync", sac)
                            pr.wait("sync", sl)
                            ap_ = tb_[:, :] if len(tb_.shape) == 2 else                                 tb_[:, :, :].rearrange("p a b -> p (a b)")
                            pr.dma("sync", tap_out[tn][:, :], ap_, sst)
                    pr.wait("tensor", sst)
                pr.wait("tensor", sac)
                for h in range(NH):
                    pcol = (h % 2) * 224
                    pr.wait("tensor", sdv)
                    pr.op("tensor", "matmul", PS_D[:, pcol:pcol + 192],
                          qcfm[:, h, :], kcs[:, h, :],
                          start=True, stop=False)
                    pr.op("tensor", "matmul", PS_D[:, pcol:pcol + 192],
                          qrls[:, h, :],
                          krls[:, :], start=False, stop=True, sem=spe)
                    pr.wait("vector", spe)
                    pr.op("vector", "scalar_tensor_tensor",
                          scores[:, h, 0:192], PS_D[:, pcol:pcol + 192],
                          SCALE, C["band"][:, :], op0=ALU.mult, op1=ALU.add,
                          sem=sdv)
            else:
                pr.wait("vector", sl)
            for hf in range(nhalf):
                pr.wait("gpsimd", sdv)
                for j in range(NSL):
                    pr.gather(kg[:, j, :], ktab[:, :],
                              idxu[:, hf * NSL + j:hf * NSL + j + 1], sg)
                pr.wait("scalar", sg)
                pr.wait("vector", sg)
                nl_scores_half(t, hf, ca_ap, sa_ap, scores)
            if t == 2 and tap("sc2", (128, NH * 224)) is not None:
                pr.wait("sync", sdv)
                pr.dma("sync", tap_out["sc2"][:, :],
                       scores[:, :, :].rearrange("p h k -> p (h k)"), sst)
                pr.wait("vector", sst)
            mx = smx[:, 0:NH]
            sm = smx[:, NH:2 * NH]
            scv = scores[:, :, 0:ncols]
            pr.op("vector", "tensor_reduce", mx, scv, axis=AX.X, op=ALU.max)
            mxb = mx.unsqueeze(2).to_broadcast([128, NH, ncols])
            pr.op("vector", "tensor_tensor", scv, scv, mxb, op=ALU.subtract,
                  sem=sdv)
            pr.wait("scalar", sdv)
            pr.op("scalar", "activation", scv, scv, AF.Exp, sem=sac)
            pr.wait("vector", sac)
            pr.op("vector", "tensor_reduce", sm, scv, axis=AX.X, op=ALU.add)
            pr.op("vector", "reciprocal", sm, sm)
            smb = sm.unsqueeze(2).to_broadcast([128, NH, ncols])
            pr.op("vector", "tensor_tensor", attn_bf[:, :, 0:ncols], scv,
                  smb, op=ALU.mult, sem=sdv)
            if t > 0:
                pr.wait("tensor", sdv)
                for h in range(NH):
                    slot = (h % 4) * 128
                    pr.wait("tensor", sac)
                    pr.op("tensor", "transpose", PS_T[:, slot:slot + 128],
                          attn_bf[:, h, 0:128], C["ident_bf"][:, :], sem=spe)
                    pr.wait("scalar", spe)
                    pr.op("scalar", "copy", attnT_A[:, h, :],
                          PS_T[:, slot:slot + 128], sem=sac)
                for h in range(NH):
                    slot = (h % 4) * 128
                    pr.wait("tensor", sac)
                    pr.op("tensor", "transpose", PS_T[0:64, slot:slot + 128],
                          attn_bf[:, h, 128:192], C["ident_bf"][:, :],
                          sem=spe)
                    pr.wait("scalar", spe)
                    pr.op("scalar", "copy", attnT_B[:, h, :],
                          PS_T[0:64, slot:slot + 128], sem=sac)
                pr.wait("tensor", sac)
                pr.wait("tensor", sdv)
                for h in range(NH):
                    pr.op("tensor", "matmul", PS_B[:, h * DH:(h + 1) * DH],
                          attnT_A[:, h, :], vsA[:, h * DH:(h + 1) * DH],
                          start=True, stop=False)
                    pr.op("tensor", "matmul", PS_B[:, h * DH:(h + 1) * DH],
                          attnT_B[:, h, :], vsB[:, h * DH:(h + 1) * DH],
                          start=False, stop=True,
                          sem=(spe if h == NH - 1 else None))
            for hf in range(nhalf):
                pr.wait("gpsimd", sdv)
                for j in range(NSL):
                    pr.gather(vg[:, j, :], vtab[:, :],
                              idxu[:, hf * NSL + j:hf * NSL + j + 1], sg)
                pr.wait("vector", sg)
                nl_values_half(t, hf, first=(hf == 0), kbase=kbase)
            if t > 0:
                pr.wait("vector", spe)
                pr.op("vector", "tensor_tensor", mix_bf[:, :],
                      PS_B[:, 0:512], nl_acc[:, :], op=ALU.add, sem=sdv)
            else:
                pr.op("vector", "tensor_copy", mix_bf[:, :], nl_acc[:, :],
                      sem=sdv)
            if t == 2 and tap("mix2", (128, NH * DH), BF16) is not None:
                pr.wait("sync", sdv)
                pr.dma("sync", tap_out["mix2"][:, :], mix_bf[:, :], sst)
                pr.wait("tensor", sst)
            pr.wait("tensor", sdv)
            pr.wait("tensor", sac)
            for c in range(4):
                pr.op("tensor", "transpose", PS_T[:, c * 128:(c + 1) * 128],
                      mix_bf[:, c * 128:(c + 1) * 128], C["ident_bf"][:, :],
                      sem=(spe if c == 3 else None))
            pr.wait("scalar", spe)
            for c in range(4):
                pr.op("scalar", "copy", mixT[:, c, :],
                      PS_T[:, c * 128:(c + 1) * 128],
                      sem=(sac if c == 3 else None))
            pr.wait("tensor", sac)
            for dd in range(8):
                pr.wait("sync", sdv)
                pr.dma("sync", po_x[:, :],
                       x_fm[dd * 128:(dd + 1) * 128, q0:q0 + 128], sl)
                pr.dma("sync", po_sh[:, :],
                       shared_hbm[dd * 128:(dd + 1) * 128, q0:q0 + 128], sl)
                pr.wait("tensor", sdv)
                for c in range(4):
                    pr.op("tensor", "matmul", PS_C[:, 0:128],
                          w_out_sb[:, c, dd * 128:(dd + 1) * 128],
                          mixT[:, c, :], start=(c == 0), stop=(c == 3),
                          sem=(spe if c == 3 else None))
                pr.wait("vector", spe)
                pr.wait("vector", sl)
                pr.wait("vector", sst)
                pr.op("vector", "tensor_tensor", po_o[:, :], PS_C[:, 0:128],
                      po_x[:, :], op=ALU.add)
                pr.op("vector", "tensor_tensor", po_o[:, :], po_o[:, :],
                      po_sh[:, :], op=ALU.add, sem=sdv)
                pr.wait("sync", sdv)
                pr.dma("sync", partial_hbm[dd * 128:(dd + 1) * 128,
                                           q0:q0 + 128], po_o[:, :], sst)

        b2_tile(0)
        free("s0")
        for t in range(1, NT):
            b2_tile(t)
        if tap("partial", (D, L)) is not None:
            pr.wait("sync", sst)
            pr.dma("sync", tap_out["partial"][:, :], partial_hbm[:, :], sst)
        free("b2")
        pr.mark("b2")

        xnh = [rbuf(f"xnh{c}", (128, 2048), st="f") for c in range(8)]
        wup_b = [rbuf(f"wup{b_}", (128, 8, FF), st="f") for b_ in range(2)]
        cwrep = rbuf("cwrep", (128, 2048), F32, st="f")
        cwrow = rbuf("cwrow", (1, 2048), F32, st="f")
        hsil = rbuf("hsil", (128, TSL), st="f")
        hscl = rbuf("hscl", (128, TSL), st="f")
        wup_done = [0, 0]
        for th in range(2):
            tb = th * 2048
            pr.wait("sync", spe)
            for c in range(8):
                pr.dma("sync", xnh[c][:, :],
                       xn_fm_hbm[c * 128:(c + 1) * 128, tb:tb + 2048], sl)
            for e in range(NE):
                par = e % 2
                bb = wup_b[par]
                pr.wait("gpsimd", spe, wup_done[par])
                for kk in range(8):
                    pr.dma("gpsimd", bb[:, kk, :],
                           w_up[e, kk * 128:(kk + 1) * 128, :], sg)
                pr.wait("sync", sdv)
                pr.dma("sync", cwrow[:, :], cw_hbm[e:e + 1, tb:tb + 2048],
                       sl)
                pr.wait("tensor", sl)
                pr.wait("tensor", sdv)
                for nn in range(4):
                    pr.op("tensor", "matmul", PS_A[:, 1536:1536 + 512],
                          C["ones_row_f"][:, :],
                          cwrow[:, nn * 512:(nn + 1) * 512],
                          start=True, stop=True, sem=spe)
                    pr.wait("vector", spe)
                    pr.op("vector", "tensor_copy",
                          cwrep[:, nn * 512:(nn + 1) * 512],
                          PS_A[:, 1536:1536 + 512], sem=sdv)
                    pr.wait("tensor", sdv)
                pr.wait("tensor", sg)
                for ts in range(4):
                    for fc in range(4):
                        pr.wait("tensor", sac)
                        for kk in range(8):
                            pr.op("tensor", "matmul", PS_B[:, 0:TSL],
                                  bb[:, kk, fc * 128:(fc + 1) * 128],
                                  xnh[kk][:, ts * TSL:(ts + 1) * TSL],
                                  start=(kk == 0), stop=(kk == 7),
                                  sem=(spe if kk == 7 else None))
                        pr.wait("scalar", spe)
                        pr.wait("scalar", sst)
                        pr.op("scalar", "activation", hsil[:, :],
                              PS_B[:, 0:TSL], AF.Silu, sem=sac)
                        pr.wait("vector", sac)
                        pr.wait("vector", sst)
                        pr.op("vector", "tensor_tensor", hscl[:, :],
                              hsil[:, :], cwrep[:, ts * TSL:(ts + 1) * TSL],
                              op=ALU.mult, sem=sdv)
                        pr.wait("sync", sdv)
                        pr.dma("sync",
                               h_hbm[e, fc * 128:(fc + 1) * 128,
                                     tb + ts * TSL:tb + (ts + 1) * TSL],
                               hscl[:, :], sst)
                wup_done[par] = spe.c

        hrd = [rbuf(f"hrd{b_}", (128, 4, TSL), st="f") for b_ in range(2)]
        wdn_b = [rbuf(f"wdn{b_}", (128, 4, 512), st="f") for b_ in range(2)]
        po_p = rbuf("po_p", (128, TSL), F32, st="f")
        y_sb = rbuf("y_sb", (128, TSL), F32, st="f")
        buf_done = [0, 0]
        for dh in range(2):
            dbase = dh * 512
            for T in range(L // TSL):
                tb = T * TSL
                pr.wait("tensor", sdv)
                for e in range(NE):
                    par = e % 2
                    hb = hrd[par]
                    wb = wdn_b[par]
                    pr.wait("sync", spe, buf_done[par])
                    pr.dma("sync", hb[:, :, :],
                           h_hbm[e, :, tb:tb + TSL].rearrange(
                               "(c p) t -> p c t", p=128), sl)
                    pr.wait("gpsimd", spe, buf_done[par])
                    for fc in range(4):
                        pr.dma("gpsimd", wb[:, fc, :],
                               w_down[e, fc * 128:(fc + 1) * 128,
                                      dbase:dbase + 512], sg)
                    pr.wait("tensor", sl)
                    pr.wait("tensor", sg)
                    for dd in range(4):
                        for fc in range(4):
                            pr.op("tensor", "matmul",
                                  PS_A[:, dd * 512:dd * 512 + TSL],
                                  wb[:, fc, dd * 128:(dd + 1) * 128],
                                  hb[:, fc, :],
                                  start=(e == 0 and fc == 0),
                                  stop=(e == NE - 1 and fc == 3),
                                  sem=(spe if fc == 3 else None))
                    buf_done[par] = spe.c
                pr.wait("vector", spe)
                for dd in range(4):
                    ddg = dbase + dd * 128
                    pr.wait("sync", sdv)
                    pr.dma("sync", po_p[:, :],
                           partial_hbm[ddg:ddg + 128, tb:tb + TSL], sl)
                    pr.wait("vector", sl)
                    pr.wait("vector", sst)
                    pr.op("vector", "tensor_tensor", y_sb[:, :],
                          PS_A[:, dd * 512:dd * 512 + TSL], po_p[:, :],
                          op=ALU.add, sem=sdv)
                    pr.wait("sync", sdv)
                    pr.dma("sync", y_fm[ddg:ddg + 128, tb:tb + TSL],
                           y_sb[:, :], sst)

        pr.wait("sync", sst)
        pr.mark("f")
        pr.truncate(limit)

        block = ctx.enter_context(nc.Block())

        @block.sync
        def _(q):
            for fn in pr.ops["sync"]:
                fn(q)

        @block.gpsimd
        def _(q):
            for fn in pr.ops["gpsimd"]:
                fn(q)

        @block.scalar
        def _(q):
            for fn in pr.ops["scalar"]:
                fn(q)

        @block.vector
        def _(q):
            for fn in pr.ops["vector"]:
                fn(q)

        @block.tensor
        def _(q):
            for fn in pr.ops["tensor"]:
                fn(q)

    return nc, list(tap_out.keys())


def kernel(**inputs):
    HT = host_tables()
    nc, tap_names = build_program(taps=TAPS)
    A = {k: np.asarray(v) for k, v in inputs.items()}
    base = {
        "W_dkv": A["L0_attn_W_dkv"], "W_uk": A["L0_attn_W_uk"],
        "W_uv": A["L0_attn_W_uv"], "W_dq": A["L0_attn_W_dq"],
        "W_uq": A["L0_attn_W_uq"], "W_qr": A["L0_attn_W_qr"],
        "W_kr": A["L0_attn_W_kr"], "W_out": A["L0_attn_W_out"],
        "idx_wq": A["L0_attn_idx_wq"], "idx_wk": A["L0_attn_idx_wk"],
        "idx_w": np.ascontiguousarray(A["L0_attn_idx_w"].reshape(1, NIH)),
        "raw_delta": np.ascontiguousarray(
            A["L0_attn_raw_delta"].reshape(1, HALF)),
        "cent_T": np.ascontiguousarray(A["L0_ffn_centroids"].T),
        "exp_bias": np.ascontiguousarray(
            A["L0_ffn_expert_biases"].reshape(1, NE)),
        "Ws_up": A["L0_ffn_Ws_up"], "Ws_down": A["L0_ffn_Ws_down"],
        "W_up": A["L0_ffn_W_up"], "W_down": A["L0_ffn_W_down"],
    }
    for nm, arr in HT.items():
        base["t_" + nm] = arr
    in_maps = []
    for c in range(B):
        m = dict(base)
        xc = np.ascontiguousarray(A["L0_attn_x"][c])
        m["x_tok"] = xc
        m["x_fm"] = np.ascontiguousarray(xc.T)
        in_maps.append(m)
    res = run_bass_kernel_spmd(nc, in_maps, core_ids=list(range(B)))
    outs = []
    taps = []
    for c in range(B):
        r = res.results[c]
        outs.append(np.ascontiguousarray(np.asarray(r["y_fm"]).T))
        taps.append({n: np.asarray(r["tap_" + n]) for n in tap_names})
    y = np.stack(outs, axis=0).astype(np.float32)
    if tap_names:
        kernel.last_taps = taps
    return y


# version 11
# speedup vs baseline: 1.4894x; 1.4894x over previous
import math
import numpy as np
import ml_dtypes
import concourse.bass as bass
import concourse.mybir as mybir
from concourse.bass import IndirectOffsetOnAxis
from concourse.bass_utils import run_bass_kernel_spmd
from contextlib import ExitStack

F32 = mybir.dt.float32
BF16 = mybir.dt.bfloat16
U32 = mybir.dt.uint32
AF = mybir.ActivationFunctionType
ALU = mybir.AluOpType
AX = mybir.AxisListType
bfdt = ml_dtypes.bfloat16

B, L, D = 8, 4096, 1024
DCKV, DCQ = 256, 512
NH, DH, DR, HALF = 8, 64, 32, 16
NIH = 4
NE, FF = 64, 512
SCALE = (DH + DR) ** -0.5
NT = L // 128
EPS_TIE = 1e-10
NEG = -1.0e30
NSL = 16
TSL = 512

TAPS = []

ENGINES = ("sync", "gpsimd", "scalar", "vector", "tensor")

_HOST_CACHE = {}


def host_tables():
    if _HOST_CACHE:
        return _HOST_CACHE
    t = {}
    f32 = np.float32
    theta = (1.0 / 10000.0 ** (2.0 * np.arange(HALF) / DR)).astype(np.float64)
    k = np.arange(L, dtype=np.float64)[:, None]

    def tile_view(a):
        return np.ascontiguousarray(
            a.reshape(NT, 128, HALF).transpose(1, 0, 2)).astype(f32)

    t["cos_q"] = tile_view(np.cos(k * theta))
    t["sin_q"] = tile_view(np.sin(k * theta))
    t["cos_2q"] = tile_view(np.cos(2.0 * k * theta))
    t["sin_2q"] = tile_view(np.sin(2.0 * k * theta))
    t["cos_k63"] = tile_view(np.cos((k + 63.0) * theta))
    t["sin_k63"] = tile_view(np.sin((k + 63.0) * theta))
    j = np.arange(64, 96, dtype=np.float64)[:, None]
    t["cos_snl"] = np.broadcast_to(np.cos(j * theta).reshape(1, -1),
                                   (128, 32 * HALF)).astype(f32).copy()
    t["sin_snl"] = np.broadcast_to(np.sin(j * theta).reshape(1, -1),
                                   (128, 32 * HALF)).astype(f32).copy()
    j0 = np.arange(0, 96, dtype=np.float64)[:, None]
    t["cos_s0"] = np.broadcast_to(np.cos(j0 * theta).reshape(1, -1),
                                  (128, 96 * HALF)).astype(f32).copy()
    t["sin_s0"] = np.broadcast_to(np.sin(j0 * theta).reshape(1, -1),
                                  (128, 96 * HALF)).astype(f32).copy()
    t["ident_bf"] = np.eye(128).astype(bfdt)
    t["ident_f32"] = np.eye(128).astype(f32)
    t["colv"] = np.broadcast_to(np.arange(L, dtype=f32).reshape(1, L),
                                (128, L)).copy()
    p = np.arange(128)[:, None]
    s = np.arange(192)[None, :]
    t["band"] = np.where((s >= p + 1) & (s <= p + 64), 0.0, NEG).astype(f32)
    c0 = np.arange(128)[None, :].astype(np.float64)
    q0 = np.arange(128)[:, None].astype(np.float64)
    loc = (c0 >= q0 - 63) & (c0 <= q0)
    fut = c0 > q0
    t["m0"] = np.where(loc, 2e9 - 1e3 * c0,
                       np.where(fut, -2e9 - 1e3 * c0,
                                -EPS_TIE * c0)).astype(f32)
    t["iota64"] = np.broadcast_to(np.arange(NE, dtype=f32).reshape(1, NE),
                                  (128, NE)).copy()
    t["pvec"] = np.arange(128, dtype=f32).reshape(128, 1).copy()
    t["qm63"] = (np.arange(NT, dtype=f32)[None, :] * 128.0 - 63.0
                 + np.arange(128, dtype=f32)[:, None]).copy()
    t["ones_row_f"] = np.ones((1, 128), f32)
    _HOST_CACHE.update(t)
    return t


class Sem:
    def __init__(self, handle, name):
        self.h = handle
        self.c = 0
        self.name = name


class Prog:

    def __init__(self):
        self.ops = {e: [] for e in ENGINES}
        self.n = 0
        self.marks = {}

    def mark(self, name):
        self.marks[name] = {e: len(self.ops[e]) for e in ENGINES}

    def truncate(self, name):
        if name is None or name not in self.marks:
            return
        for e in ENGINES:
            del self.ops[e][self.marks[name][e]:]

    def emit(self, eng, fn):
        self.ops[eng].append(fn)
        self.n += 1

    def wait(self, eng, sem, val=None):
        v = sem.c if val is None else val
        if v > 0:
            self.emit(eng, lambda q, s=sem.h, v=v: q.wait_ge(s, v))

    def dma(self, eng, out, in_, sem, **kw):
        sem.c += 16
        self.emit(eng, lambda q, o=out, i=in_, s=sem.h, kw=kw:
                  q.dma_start(out=o, in_=i, **kw).then_inc(s, 16))
        return sem.c

    DRAIN_AFTER = ("tensor_reduce", "reciprocal")
    NO_RT = ("max", "max_index", "match_replace", "drain", "engine_nop")
    rt_sem = None
    paranoid = True

    def op(self, eng, meth, *args, sem=None, **kw):
        need_drain = (eng == "vector" and meth in self.DRAIN_AFTER)
        need_rt = False
        if (eng in ("vector", "scalar") and meth not in self.NO_RT
                and not need_drain and args):
            if self.paranoid:
                need_rt = True
            else:
                try:
                    out0 = args[0] if not ("out" in kw) else kw["out"]
                    if hasattr(out0, "free_size") and out0.free_size <= 64:
                        need_rt = True
                except Exception:
                    pass
        if sem is not None:
            sem.c += 1
        use = sem if sem is not None else (self.rt_sem if need_rt else None)
        if sem is None and need_rt:
            self.rt_sem.c += 1

        def fn(q, m=meth, a=args, kw=kw, s=(None if need_drain else use)):
            r = getattr(q, m)(*a, **kw)
            if s is not None:
                r.then_inc(s.h, 1)
            return r
        self.emit(eng, fn)
        if need_drain:
            def fd(q, s=sem):
                r = q.drain()
                if s is not None:
                    r.then_inc(s.h, 1)
                return r
            self.emit(eng, fd)
        if need_rt and not need_drain:
            self.wait(eng, use)
        return None if sem is None else sem.c

    def gather(self, out_ap, table_ap, idx_col_ap, sem):
        sem.c += 16
        self.emit("gpsimd", lambda q, o=out_ap, t=table_ap, i=idx_col_ap,
                  s=sem.h: q.indirect_dma_start(
                      out=o, out_offset=None, in_=t,
                      in_offset=IndirectOffsetOnAxis(ap=i, axis=0)
                  ).then_inc(s, 16))
        return sem.c


def build_program(taps=(), limit=None):
    nc = bass.Bass("TRN2", target_bir_lowering=False)
    HT = host_tables()
    pr = Prog()

    d_in = {}

    def din(name, shape, dt=F32):
        d_in[name] = nc.dram_tensor(name, list(shape), dt,
                                    kind="ExternalInput").ap()
        return d_in[name]

    x_tok = din("x_tok", (L, D))
    x_fm = din("x_fm", (D, L))
    w_dkv = din("W_dkv", (D, DCKV))
    w_uk = din("W_uk", (DCKV, NH * DH))
    w_uv = din("W_uv", (DCKV, NH * DH))
    w_dq = din("W_dq", (D, DCQ))
    w_uq = din("W_uq", (DCQ, NH * DH))
    w_qr = din("W_qr", (DCQ, NH * DR))
    w_kr = din("W_kr", (D, DR))
    w_out = din("W_out", (NH * DH, D))
    idx_wq = din("idx_wq", (D, NIH * DH))
    idx_wk = din("idx_wk", (D, DH))
    idx_w = din("idx_w", (1, NIH))
    raw_delta = din("raw_delta", (1, HALF))
    cent_T = din("cent_T", (D, NE))
    exp_bias = din("exp_bias", (1, NE))
    ws_up = din("Ws_up", (D, FF))
    ws_down = din("Ws_down", (FF, D))
    w_up = din("W_up", (NE, D, FF))
    w_down = din("W_down", (NE, FF, D))
    for nm, arr in HT.items():
        din("t_" + nm, arr.shape, BF16 if arr.dtype == bfdt else F32)

    y_fm = nc.dram_tensor("y_fm", [D, L], F32, kind="ExternalOutput").ap()
    tap_out = {}

    def tap(name, shape, dt=F32):
        if name in taps:
            tap_out[name] = nc.dram_tensor("tap_" + name, list(shape), dt,
                                           kind="ExternalOutput").ap()
            return tap_out[name]
        return None

    def dscratch(name, shape, dt):
        return nc.dram_tensor(name, list(shape), dt, kind="Internal").ap()

    xn_fm_hbm = dscratch("xn_fm_hbm", (D, L), BF16)
    ktab = dscratch("ktab", (L, NH * DH + DR), BF16)
    vtab = dscratch("vtab", (L, NH * DH), BF16)
    shared_hbm = dscratch("shared_hbm", (D, L), F32)
    partial_hbm = dscratch("partial_hbm", (D, L), F32)
    cw_hbm = dscratch("cw_hbm", (NE, L), F32)
    h_hbm = dscratch("h_hbm", (NE, 8, 128, 2048), BF16)
    w_down_bf = dscratch("w_down_bf", (NE, FF, D), BF16)
    cw_bf_hbm = dscratch("cw_bf_hbm", (NE, L), BF16)
    kcT_hbm = dscratch("kcT_hbm", (NH * DH, L), BF16)
    krl_hbm = dscratch("krl_hbm", (DR, L), BF16)
    qrl_hbm = dscratch("qrl_hbm", (NH * DR, L), BF16)
    qc_hbm = dscratch("qc_hbm", (128, NT * NH * DH), BF16)
    qrot_hbm = dscratch("qrot_hbm", (128, NT * NH * DR), BF16)

    with ExitStack() as ctx:
        stacks = {}

        def sbuf(name, shape, dt=BF16, st="persist", side="left"):
            if st not in stacks:
                stacks[st] = ctx.enter_context(ExitStack())
            return stacks[st].enter_context(
                nc.sbuf_tensor(name, list(shape), dt, side=side))

        def rbuf(name, shape, dt=BF16, st=None):
            return sbuf(name, shape, dt, st=st, side="right")

        def sem(name):
            return Sem(ctx.enter_context(nc.semaphore(name)), name)

        sl = sem("sl")
        sg = sem("sg")
        sst = sem("sst")
        spe = sem("spe")
        sdv = sem("sdv")
        sac = sem("sac")
        srt = sem("srt")
        sep = sem("sep")
        sarr = sem("sarr")
        pr.rt_sem = srt
        ALLSEMS = (sl, sg, sst, spe, sdv, sac, srt)
        arrs = sbuf("arrs", (128, 16), F32, st="persist")

        def barrier(reset=True):
            for e in ENGINES:
                for s in ALLSEMS:
                    pr.wait(e, s)

        def free(*names):
            barrier()
            for nm in names:
                if nm in stacks:
                    stacks[nm].pop_all().close()
                    del stacks[nm]

        C = {}
        for nm in ("cos_q", "sin_q", "cos_2q", "sin_2q", "cos_k63", "sin_k63",
                   "cos_snl", "sin_snl", "ident_bf", "ident_f32", "band",
                   "m0", "iota64", "pvec", "qm63", "ones_row_f"):
            arr = HT[nm]
            dt = BF16 if arr.dtype == bfdt else F32
            C[nm] = sbuf("c_" + nm, arr.shape, dt)
            src_ap = d_in["t_" + nm]
            dst_ap = C[nm][:, :] if arr.ndim == 2 else C[nm][:, :, :]
            src_ap = src_ap[:, :] if arr.ndim == 2 else src_ap[:, :, :]
            pr.dma("sync", dst_ap, src_ap, sl)

        pr.wait("vector", sl)
        one_col = sbuf("one_col", (128, 1), F32)
        pr.op("vector", "memset", one_col[:, :], 1.0)
        epsln = sbuf("epsln", (128, 1), F32)
        pr.op("vector", "memset", epsln[:, :], 1e-5)
        ones1 = sbuf("ones1", (128, 1), F32)
        pr.op("vector", "memset", ones1[:, :], 1.0)
        halfpi = sbuf("halfpi", (1, 1), F32)
        pr.op("vector", "memset", halfpi[:, :], math.pi / 2.0, sem=sdv)

        idx32f = sbuf("idx32f", (128, NT, 32), F32)
        idx0f = sbuf("idx0f", (128, 96), F32)

        stacks["ps_main"] = ctx.enter_context(ExitStack())
        _psm = stacks["ps_main"]
        PS_A = _psm.enter_context(nc.psum_tensor("PS_A", [128, 2048], F32))
        PS_B = _psm.enter_context(nc.psum_tensor("PS_B", [128, 512], F32))
        PS_C = _psm.enter_context(nc.psum_tensor("PS_C", [128, 512], F32))
        PS_D = _psm.enter_context(nc.psum_tensor("PS_D", [128, 512], F32))
        PS_T = _psm.enter_context(nc.psum_tensor("PS_T", [128, 512], BF16))

        def wload(dst, src, kchunks):
            for kk in range(kchunks):
                pr.dma("gpsimd", dst[:, kk, :],
                       src[kk * 128:(kk + 1) * 128, :], sg)

        idxw_sb = sbuf("idxw_sb", (1, NIH), F32)
        pr.dma("sync", idxw_sb[:, :], idx_w[:, :], sl)
        rdelta_sb = sbuf("rdelta_sb", (1, HALF), F32)
        pr.dma("sync", rdelta_sb[:, :], raw_delta[:, :], sl)
        ebias_sb = sbuf("ebias_sb", (1, NE), F32)
        pr.dma("sync", ebias_sb[:, :], exp_bias[:, :], sl)

        dl = sbuf("dl", (1, 8 * HALF), F32)
        pr.wait("scalar", sl)
        pr.wait("scalar", sdv)
        sigv = dl[:, 0:HALF]
        hv = dl[:, HALF:2 * HALF]
        sinh = dl[:, 2 * HALF:3 * HALF]
        cosh = dl[:, 3 * HALF:4 * HALF]
        cosd1 = dl[:, 4 * HALF:5 * HALF]
        sind1 = dl[:, 5 * HALF:6 * HALF]
        pr.op("scalar", "activation", sigv, rdelta_sb[:, :], AF.Sigmoid)
        pr.op("scalar", "mul", hv, sigv, -math.pi)
        pr.op("scalar", "activation", sinh, hv, AF.Sin)
        pr.op("scalar", "activation", cosh, hv, AF.Sin, bias=halfpi[0:1, :],
              sem=sac)
        pr.wait("vector", sac)
        pr.op("vector", "tensor_tensor", cosd1, sinh, sinh, op=ALU.mult)
        pr.op("vector", "tensor_scalar", cosd1, cosd1, -2.0, 1.0,
              op0=ALU.mult, op1=ALU.add)
        pr.op("vector", "tensor_tensor", sind1, sinh, cosh, op=ALU.mult)
        pr.op("vector", "tensor_scalar", sind1, sind1, 2.0, None,
              op0=ALU.mult, sem=sdv)
        cosd = sbuf("cosd", (128, HALF), F32)
        sind = sbuf("sind", (128, HALF), F32)
        ebias_rep = sbuf("ebias_rep", (128, NE), F32)
        idxw_rep = sbuf("idxw_rep", (128, NIH), F32)
        pr.wait("tensor", sdv)
        pr.wait("tensor", sl)
        pr.op("tensor", "matmul", PS_D[:, 0:HALF], C["ones_row_f"][:, :],
              cosd1, start=True, stop=True)
        pr.op("tensor", "matmul", PS_D[:, HALF:2 * HALF],
              C["ones_row_f"][:, :], sind1, start=True, stop=True)
        pr.op("tensor", "matmul", PS_D[:, 64:64 + NE], C["ones_row_f"][:, :],
              ebias_sb[:, :], start=True, stop=True)
        pr.op("tensor", "matmul", PS_D[:, 128:128 + NIH],
              C["ones_row_f"][:, :], idxw_sb[:, :], start=True, stop=True,
              sem=spe)
        pr.wait("vector", spe)
        pr.op("vector", "tensor_copy", cosd[:, :], PS_D[:, 0:HALF])
        pr.op("vector", "tensor_copy", sind[:, :], PS_D[:, HALF:2 * HALF])
        pr.op("vector", "tensor_copy", ebias_rep[:, :], PS_D[:, 64:64 + NE])
        pr.op("vector", "tensor_copy", idxw_rep[:, :],
              PS_D[:, 128:128 + NIH], sem=sdv)

        def make_slot_tables(cosrep_ap, sinrep_ap, nslots, nm, st="persist",
                             side="left"):
            ca = sbuf("ca_" + nm, (128, nslots * HALF), F32, st=st, side=side)
            sa = sbuf("sa_" + nm, (128, nslots * HALF), F32, st=st, side=side)
            t1 = sbuf("tmp_" + nm, (128, nslots * HALF), F32, st=st,
                      side=side)
            cv = cosrep_ap.rearrange("p (j f) -> p j f", f=HALF)
            sv = sinrep_ap.rearrange("p (j f) -> p j f", f=HALF)
            cosd3 = cosd[:, :].unsqueeze(1).to_broadcast([128, nslots, HALF])
            sind3 = sind[:, :].unsqueeze(1).to_broadcast([128, nslots, HALF])
            ca3 = ca[:, :].rearrange("p (j f) -> p j f", f=HALF)
            sa3 = sa[:, :].rearrange("p (j f) -> p j f", f=HALF)
            t13 = t1[:, :].rearrange("p (j f) -> p j f", f=HALF)
            pr.op("vector", "tensor_tensor", ca3, cv, cosd3, op=ALU.mult)
            pr.op("vector", "tensor_tensor", t13, sv, sind3, op=ALU.mult)
            pr.op("vector", "tensor_tensor", ca3, ca3, t13, op=ALU.subtract)
            pr.op("vector", "tensor_tensor", sa3, sv, cosd3, op=ALU.mult)
            pr.op("vector", "tensor_tensor", t13, cv, sind3, op=ALU.mult)
            pr.op("vector", "tensor_tensor", sa3, sa3, t13, op=ALU.add,
                  sem=sdv)
            return ca, sa

        ca_nl, sa_nl = make_slot_tables(C["cos_snl"][:, :],
                                        C["sin_snl"][:, :], 32, "nl")

        def angle_combine(cv, sv, cc, ss, tmp):
            pr.op("vector", "tensor_tensor", cc, cv, cosd[:, :], op=ALU.mult)
            pr.op("vector", "tensor_tensor", tmp, sv, sind[:, :], op=ALU.mult)
            pr.op("vector", "tensor_tensor", cc, cc, tmp, op=ALU.subtract)
            pr.op("vector", "tensor_tensor", ss, sv, cosd[:, :], op=ALU.mult)
            pr.op("vector", "tensor_tensor", tmp, cv, sind[:, :], op=ALU.mult)
            pr.op("vector", "tensor_tensor", ss, ss, tmp, op=ALU.add)

        def softplus(src, dst, scr):
            pr.wait("scalar", sdv)
            pr.op("scalar", "activation", scr, src, AF.Abs)
            pr.op("scalar", "activation", scr, scr, AF.Exp, scale=-1.0)
            pr.op("scalar", "activation", scr, scr, AF.Ln, bias=ones1[:, :])
            pr.op("scalar", "activation", dst, src, AF.Relu, sem=sac)
            pr.wait("vector", sac)
            pr.op("vector", "tensor_tensor", dst, dst, scr, op=ALU.add)

        xn_fm = [sbuf(f"xn_fm{c}", (128, L), st="xn") for c in range(8)]
        ln_x = rbuf("ln_x", (128, D), F32, st="ln")
        ln_sq = rbuf("ln_sq", (128, D), F32, st="ln")
        ln_st = rbuf("ln_st", (128, 8), F32, st="ln")
        ln_bf = rbuf("ln_bf", (128, D), st="ln")
        for t in range(NT):
            pr.wait("sync", sdv)
            pr.dma("sync", ln_x[:, :], x_tok[t * 128:(t + 1) * 128, :], sl)
            pr.wait("vector", sl)
            pr.wait("vector", sac)
            mean = ln_st[:, 0:1]
            m2 = ln_st[:, 1:2]
            var = ln_st[:, 2:3]
            lnv = ln_st[:, 3:4]
            rstd = ln_st[:, 4:5]
            pr.op("vector", "tensor_reduce", mean, ln_x[:, :], axis=AX.X,
                  op=ALU.add)
            pr.op("vector", "tensor_scalar", mean, mean, 1.0 / D, None,
                  op0=ALU.mult)
            pr.op("vector", "tensor_copy", ln_st[:, 5:6], ln_st[:, 0:1],
                  sem=sdv)
            pr.wait("scalar", sdv)
            pr.op("scalar", "activation", ln_sq[:, :], ln_x[:, :], AF.Square,
                  sem=sac)
            pr.wait("vector", sac)
            pr.op("vector", "tensor_reduce", m2, ln_sq[:, :], axis=AX.X,
                  op=ALU.add)
            pr.op("vector", "tensor_scalar", m2, m2, 1.0 / D, None,
                  op0=ALU.mult)
            pr.op("vector", "tensor_copy", ln_st[:, 6:7], ln_st[:, 1:2],
                  sem=sdv)
            pr.wait("scalar", sdv)
            pr.op("scalar", "activation", var, mean, AF.Square, sem=sac)
            pr.wait("vector", sac)
            pr.op("vector", "tensor_tensor", var, m2, var, op=ALU.subtract,
                  sem=sdv)
            pr.wait("scalar", sdv)
            pr.op("scalar", "activation", lnv, var, AF.Ln, bias=epsln[:, :])
            pr.op("scalar", "activation", rstd, lnv, AF.Exp, scale=-0.5,
                  sem=sac)
            pr.wait("vector", sac)
            pr.op("vector", "tensor_scalar", ln_bf[:, :], ln_x[:, :], mean,
                  rstd, op0=ALU.subtract, op1=ALU.mult, sem=sdv)
            pr.wait("tensor", sdv)
            for c in range(8):
                slot = (c % 4) * 128
                pr.wait("tensor", sac)
                pr.op("tensor", "transpose", PS_T[:, slot:slot + 128],
                      ln_bf[:, c * 128:(c + 1) * 128], C["ident_bf"][:, :],
                      sem=spe)
                pr.wait("scalar", spe)
                pr.op("scalar", "copy", xn_fm[c][:, t * 128:(t + 1) * 128],
                      PS_T[:, slot:slot + 128], sem=sac)
        barrier()
        for c in range(8):
            pr.dma("sync", xn_fm_hbm[c * 128:(c + 1) * 128, :],
                   xn_fm[c][:, :], sst)
        if tap("xn_fm0", (128, L), BF16) is not None:
            pr.dma("sync", tap_out["xn_fm0"][:, :], xn_fm[0][:, :], sst)
        free("ln")
        pr.mark("aa")

        def fm_chain(dst_chunks, wsb, kchunks, mtot, src_chunks=None,
                     mchunk=128):
            src = src_chunks or xn_fm
            nm = (mtot + mchunk - 1) // mchunk
            for ts in range(L // TSL):
                for mc in range(nm):
                    mw = min(mchunk, mtot - mc * mchunk)
                    ps = PS_B if (mc % 2 == 0) else PS_C
                    pr.wait("tensor", sdv)
                    for kk in range(kchunks):
                        pr.op("tensor", "matmul", ps[0:mw, 0:TSL],
                              wsb[:, kk, mc * mchunk:mc * mchunk + mw],
                              src[kk][:, ts * TSL:(ts + 1) * TSL],
                              start=(kk == 0), stop=(kk == kchunks - 1),
                              sem=(spe if kk == kchunks - 1 else None))
                    pr.wait("vector", spe)
                    pr.op("vector", "tensor_copy",
                          dst_chunks[mc][0:mw, ts * TSL:(ts + 1) * TSL],
                          ps[0:mw, 0:TSL], sem=sdv)

        def fm_chain_hbm(dst_hbm, stage, wsb, kchunks, mtot,
                         src_chunks=None):
            src = src_chunks or xn_fm
            nm = (mtot + 127) // 128
            for ts in range(L // TSL):
                for mc in range(nm):
                    mw = min(128, mtot - mc * 128)
                    pr.wait("tensor", sdv)
                    pr.wait("tensor", sst)
                    for kk in range(kchunks):
                        pr.op("tensor", "matmul", PS_B[0:mw, 0:TSL],
                              wsb[:, kk, mc * 128:mc * 128 + mw],
                              src[kk][:, ts * TSL:(ts + 1) * TSL],
                              start=(kk == 0), stop=(kk == kchunks - 1),
                              sem=(spe if kk == kchunks - 1 else None))
                    pr.wait("vector", spe)
                    pr.wait("vector", sst)
                    pr.op("vector", "tensor_copy", stage[0:mw, :],
                          PS_B[0:mw, 0:TSL], sem=sdv)
                    pr.wait("sync", sdv)
                    pr.dma("sync",
                           dst_hbm[mc * 128:mc * 128 + mw,
                                   ts * TSL:(ts + 1) * TSL],
                           stage[0:mw, :], sst)

        k_iT = rbuf("k_iT", (64, L), st="qk")
        q_iT = [rbuf(f"q_iT{c}", (64, L), st="qk") for c in range(4)]

        W1 = {}
        W1["idx_wq"] = rbuf("s_idx_wq", (128, 8, NIH * DH), st="w1")
        wload(W1["idx_wq"], idx_wq, 8)
        W1["idx_wk"] = rbuf("s_idx_wk", (128, 8, DH), st="w1")
        wload(W1["idx_wk"], idx_wk, 8)
        W1["w_kr"] = rbuf("s_w_kr", (128, 8, DR), st="w1")
        wload(W1["w_kr"], w_kr, 8)
        W1["cent"] = rbuf("s_cent", (128, 8, NE), st="w1")
        wload(W1["cent"], cent_T, 8)
        W1["ws_up"] = rbuf("s_ws_up", (128, 8, FF), st="w1")
        wload(W1["ws_up"], ws_up, 8)
        W1["ws_down"] = rbuf("s_ws_down", (128, 4, D), st="w1")
        wload(W1["ws_down"], ws_down, 4)
        pr.wait("tensor", sg)

        fm_chain([k_iT], W1["idx_wk"], 8, DH)
        fm_chain(q_iT, W1["idx_wq"], 8, NIH * DH, mchunk=64)
        if tap("k_iT", (64, L), BF16) is not None:
            pr.wait("sync", sdv)
            pr.dma("sync", tap_out["k_iT"][:, :], k_iT[:, :], sst)

        kr_f = rbuf("kr_f", (128, DR), F32, st="w1")
        kr_scr = rbuf("kr_scr", (128, 4 * DR), F32, st="w1")
        kr_bf = rbuf("kr_bf", (128, DR), st="w1")
        krl_st = rbuf("krl_st", (32, 128), st="w1")
        ang_t = rbuf("ang_t", (128, 3 * HALF), F32, st="w1")
        for t in range(NT):
            pr.wait("tensor", sdv)
            pr.wait("tensor", sst)
            for kk in range(8):
                pr.op("tensor", "matmul", PS_D[:, 0:DR],
                      xn_fm[kk][:, t * 128:(t + 1) * 128],
                      W1["w_kr"][:, kk, :], start=(kk == 0), stop=(kk == 7),
                      sem=(spe if kk == 7 else None))
            pr.wait("vector", spe)
            pr.op("vector", "tensor_copy", kr_f[:, :], PS_D[:, 0:DR])
            pr.op("vector", "tensor_copy", kr_bf[:, :], PS_D[:, 0:DR],
                  sem=sdv)
            pr.wait("sync", sdv)
            pr.dma("sync", ktab[t * 128:(t + 1) * 128, NH * DH:NH * DH + DR],
                   kr_bf[:, :], sst)
            sp_ = kr_scr[:, 0:DR]
            scr = kr_scr[:, DR:2 * DR]
            softplus(kr_f[:, :], sp_, scr)
            cc = ang_t[:, 0:HALF]
            ss = ang_t[:, HALF:2 * HALF]
            tmp = ang_t[:, 2 * HALF:3 * HALF]
            angle_combine(C["cos_k63"][:, t, :], C["sin_k63"][:, t, :],
                          cc, ss, tmp)
            mu1 = sp_[:, 0:HALF]
            mu2 = sp_[:, HALF:DR]
            t2 = kr_scr[:, 2 * DR:2 * DR + HALF]
            t3 = kr_scr[:, 2 * DR + HALF:3 * DR]
            pr.wait("vector", sst)
            pr.op("vector", "tensor_tensor", t2, mu1, cc, op=ALU.mult)
            pr.op("vector", "tensor_tensor", t3, mu2, ss, op=ALU.mult)
            pr.op("vector", "tensor_tensor", kr_bf[:, 0:HALF], t2, t3,
                  op=ALU.subtract)
            pr.op("vector", "tensor_tensor", t2, mu1, ss, op=ALU.mult)
            pr.op("vector", "tensor_tensor", t3, mu2, cc, op=ALU.mult)
            pr.op("vector", "tensor_tensor", kr_bf[:, HALF:DR], t2, t3,
                  op=ALU.add, sem=sdv)
            pr.wait("tensor", sdv)
            pr.wait("tensor", sac)
            pr.op("tensor", "transpose", PS_T[0:32, 0:128], kr_bf[:, :],
                  C["ident_bf"][:, :], sem=spe)
            pr.wait("scalar", spe)
            pr.wait("scalar", sst)
            pr.op("scalar", "copy", krl_st[:, :], PS_T[0:32, 0:128], sem=sac)
            pr.wait("sync", sac)
            pr.dma("sync", krl_hbm[:, t * 128:(t + 1) * 128], krl_st[:, :],
                   sst)

        sh_h = rbuf("sh_h", (128, 4, TSL), st="w1")
        sh_o = rbuf("sh_o", (128, TSL), F32, st="w1")
        for ts in range(L // TSL):
            for fc in range(4):
                pr.wait("tensor", sac)
                pr.wait("tensor", sst)
                for kk in range(8):
                    pr.op("tensor", "matmul", PS_B[:, 0:TSL],
                          W1["ws_up"][:, kk, fc * 128:(fc + 1) * 128],
                          xn_fm[kk][:, ts * TSL:(ts + 1) * TSL],
                          start=(kk == 0), stop=(kk == 7),
                          sem=(spe if kk == 7 else None))
                pr.wait("scalar", spe)
                pr.op("scalar", "activation", sh_h[:, fc, :], PS_B[:, 0:TSL],
                      AF.Silu, sem=sac)
            for dd in range(8):
                pr.wait("tensor", sac)
                pr.wait("tensor", sdv)
                for fc in range(4):
                    pr.op("tensor", "matmul", PS_C[:, 0:TSL],
                          W1["ws_down"][:, fc, dd * 128:(dd + 1) * 128],
                          sh_h[:, fc, :], start=(fc == 0), stop=(fc == 3),
                          sem=(spe if fc == 3 else None))
                pr.wait("vector", spe)
                pr.wait("vector", sst)
                pr.op("vector", "tensor_copy", sh_o[:, :], PS_C[:, 0:TSL],
                      sem=sdv)
                pr.wait("sync", sdv)
                pr.dma("sync", shared_hbm[dd * 128:(dd + 1) * 128,
                                          ts * TSL:(ts + 1) * TSL],
                       sh_o[:, :], sst)

        aff = rbuf("aff", (128, NE), F32, st="w1")
        rt_sc = rbuf("rt_sc", (128, NE + 8), F32, st="w1")
        top8v = rbuf("top8v", (128, 8), F32, st="w1")
        idx8u = rbuf("idx8u", (128, 8), U32, st="w1")
        idx8f = rbuf("idx8f", (128, 8), F32, st="w1")
        wsel = rbuf("wsel", (128, 8), F32, st="w1")
        eq3 = rbuf("eq3", (128, 8 * NE), F32, st="w1")
        cwt = rbuf("cwt", (128, NE), F32, st="w1")
        cw_fm = rbuf("cw_fm", (64, L), F32, st="w1")
        for t in range(NT):
            pr.wait("tensor", sdv)
            for kk in range(8):
                pr.op("tensor", "matmul", PS_D[:, 0:NE],
                      xn_fm[kk][:, t * 128:(t + 1) * 128],
                      W1["cent"][:, kk, :], start=(kk == 0), stop=(kk == 7),
                      sem=(spe if kk == 7 else None))
            pr.wait("vector", spe)
            pr.op("vector", "tensor_copy", aff[:, :], PS_D[:, 0:NE], sem=sdv)
            biased = rt_sc[:, 0:NE]
            pr.op("vector", "tensor_tensor", biased, aff[:, :],
                  ebias_rep[:, :], op=ALU.add)
            pr.op("vector", "max", out=top8v[:, :], in_=biased, sem=sdv)
            pr.wait("vector", sdv)
            pr.op("vector", "max_index", idx8u[:, :], top8v[:, :], biased,
                  sem=sdv)
            pr.wait("vector", sdv)
            pr.op("vector", "tensor_copy", idx8f[:, :], idx8u[:, :])
            eqv = eq3[:, :].rearrange("p (s e) -> p s e", e=NE)
            io3 = C["iota64"][:, :].unsqueeze(1).to_broadcast([128, 8, NE])
            id3 = idx8f[:, :].unsqueeze(2).to_broadcast([128, 8, NE])
            pr.op("vector", "tensor_tensor", eqv, io3, id3, op=ALU.is_equal)
            af3 = aff[:, :].unsqueeze(1).to_broadcast([128, 8, NE])
            pr.op("vector", "tensor_tensor", eqv, eqv, af3, op=ALU.mult)
            pr.op("vector", "tensor_reduce", wsel[:, :], eqv, axis=AX.X,
                  op=ALU.add, sem=sdv)
            pr.wait("scalar", sdv)
            pr.op("scalar", "activation", wsel[:, :], wsel[:, :], AF.Sigmoid,
                  sem=sac)
            pr.wait("vector", sac)
            wsum = rt_sc[:, NE:NE + 1]
            pr.op("vector", "tensor_reduce", wsum, wsel[:, :], axis=AX.X,
                  op=ALU.add)
            pr.op("vector", "tensor_scalar", wsum, wsum, 1e-8, None,
                  op0=ALU.add)
            pr.op("vector", "reciprocal", wsum, wsum)
            pr.op("vector", "tensor_scalar", wsel[:, :], wsel[:, :], wsum,
                  None, op0=ALU.mult)
            eq2 = eq3[:, :].rearrange("p (e s) -> p e s", s=8)
            io2 = C["iota64"][:, :].unsqueeze(2).to_broadcast([128, NE, 8])
            id2 = idx8f[:, :].unsqueeze(1).to_broadcast([128, NE, 8])
            pr.op("vector", "tensor_tensor", eq2, io2, id2, op=ALU.is_equal)
            w2 = wsel[:, :].unsqueeze(1).to_broadcast([128, NE, 8])
            pr.op("vector", "tensor_tensor", eq2, eq2, w2, op=ALU.mult)
            pr.op("vector", "tensor_reduce", cwt[:, :], eq2, axis=AX.X,
                  op=ALU.add, sem=sdv)
            if t == 2:
                for tn, tb_ in (("aff2", aff), ("idx8f2", idx8f),
                                ("wsel2", wsel), ("cwt2", cwt),
                                ("eq32", eq3)):
                    if tap(tn, (128, tb_.shape[-1])) is not None:
                        pr.wait("sync", sdv)
                        pr.dma("sync", tap_out[tn][:, :], tb_[:, :], sst)
            pr.wait("tensor", sdv)
            pr.op("tensor", "transpose", PS_D[0:64, 128:256], cwt[:, :],
                  C["ident_f32"][:, :], sem=spe)
            pr.wait("vector", spe)
            pr.op("vector", "tensor_copy", cw_fm[:, t * 128:(t + 1) * 128],
                  PS_D[0:64, 128:256], sem=sdv)
            pr.wait("tensor", sdv)
        cw_bfs = rbuf("cw_bfs", (64, L), BF16, st="w1")
        pr.op("vector", "tensor_copy", cw_bfs[:, :], cw_fm[:, :], sem=sdv)
        pr.wait("sync", sdv)
        pr.dma("sync", cw_hbm[:, :], cw_fm[:, :], sst)
        pr.dma("sync", cw_bf_hbm[:, :], cw_bfs[:, :], sst)
        if tap("cw_fm", (64, L)) is not None:
            pr.dma("sync", tap_out["cw_fm"][:, :], cw_fm[:, :], sst)
        free("w1")
        pr.mark("ab1")

        c_qT = [rbuf(f"c_qT{c}", (128, L), st="cq") for c in range(4)]
        c_kvT = [rbuf(f"c_kvT{c}", (128, L), st="ckv") for c in range(2)]
        W2 = {}
        W2["w_dkv"] = rbuf("s_w_dkv", (128, 8, DCKV), st="w2")
        wload(W2["w_dkv"], w_dkv, 8)
        W2["w_dq"] = rbuf("s_w_dq", (128, 8, DCQ), st="w2")
        wload(W2["w_dq"], w_dq, 8)
        pr.wait("tensor", sg)
        fm_chain(c_kvT, W2["w_dkv"], 8, DCKV)
        fm_chain(c_qT, W2["w_dq"], 8, DCQ)
        if tap("c_kvT0", (128, L), BF16) is not None:
            pr.wait("sync", sdv)
            pr.dma("sync", tap_out["c_kvT0"][:, :], c_kvT[0][:, :], sst)
        free("w2")
        pr.mark("ab2")

        W3 = {}
        W3["w_uk"] = rbuf("s_w_uk", (128, 2, NH * DH), st="w3")
        wload(W3["w_uk"], w_uk, 2)
        W3["w_uv"] = rbuf("s_w_uv", (128, 2, NH * DH), st="w3")
        wload(W3["w_uv"], w_uv, 2)
        kc_stage = rbuf("kc_stage", (128, TSL), st="w3")
        kv_st = rbuf("kv_st", (128, NH * DH), st="w3")
        pr.wait("tensor", sg)
        fm_chain_hbm(kcT_hbm, kc_stage, W3["w_uk"], 2, NH * DH,
                     src_chunks=c_kvT)
        for t in range(NT):
            for which in ("kc", "v"):
                wsb = W3["w_uk"] if which == "kc" else W3["w_uv"]
                pr.wait("tensor", sac)
                pr.wait("tensor", sst)
                for kk in range(2):
                    pr.op("tensor", "matmul", PS_B[:, 0:NH * DH],
                          c_kvT[kk][:, t * 128:(t + 1) * 128], wsb[:, kk, :],
                          start=(kk == 0), stop=(kk == 1),
                          sem=(spe if kk == 1 else None))
                pr.wait("scalar", spe)
                pr.wait("scalar", sst)
                pr.op("scalar", "copy", kv_st[:, :], PS_B[:, 0:NH * DH],
                      sem=sac)
                pr.wait("sync", sac)
                if which == "kc":
                    pr.dma("sync", ktab[t * 128:(t + 1) * 128, 0:NH * DH],
                           kv_st[:, :], sst)
                else:
                    pr.dma("sync", vtab[t * 128:(t + 1) * 128, :],
                           kv_st[:, :], sst)
        free("w3", "ckv")
        pr.mark("ac")

        W4 = {}
        W4["w_uq"] = rbuf("s_w_uq", (128, 4, NH * DH), st="w4")
        wload(W4["w_uq"], w_uq, 4)
        W4["w_qr"] = rbuf("s_w_qr", (128, 4, NH * DR), st="w4")
        wload(W4["w_qr"], w_qr, 4)
        qc_st = rbuf("qc_st", (128, NH * DH), st="w4")
        qrot_st = rbuf("qrot_st", (128, NH * DR), st="w4")
        qrl_st = rbuf("qrl_st", (128, NH * DR), st="w4")
        qp_f = rbuf("qp_f", (128, NH * DR), F32, st="w4")
        qp_sp = rbuf("qp_sp", (128, 2 * NH * DR), F32, st="w4")
        qang = rbuf("qang", (128, 3 * HALF), F32, st="w4")
        qrl_bf = rbuf("qrl_bf", (128, NH * DR), st="w4")
        pr.wait("tensor", sg)

        def pope_heads(src_f32, cc, ss, dst_bf, scr2):
            n32 = NH * DR
            sp_ = scr2[:, 0:n32]
            scr = scr2[:, n32:2 * n32]
            softplus(src_f32, sp_, scr)
            sp3 = sp_.rearrange("p (h two f) -> p h two f", two=2, f=HALF)
            mu1 = sp3[:, :, 0, :]
            mu2 = sp3[:, :, 1, :]
            d3 = dst_bf.rearrange("p (h two f) -> p h two f", two=2, f=HALF)
            cc3 = cc.unsqueeze(1).to_broadcast([128, NH, HALF])
            ss3 = ss.unsqueeze(1).to_broadcast([128, NH, HALF])
            t3 = scr.rearrange("p (h two f) -> p h two f", two=2, f=HALF)
            pr.op("vector", "tensor_tensor", t3[:, :, 0, :], mu1, cc3,
                  op=ALU.mult)
            pr.op("vector", "tensor_tensor", t3[:, :, 1, :], mu2, ss3,
                  op=ALU.mult)
            pr.op("vector", "tensor_tensor", d3[:, :, 0, :], t3[:, :, 0, :],
                  t3[:, :, 1, :], op=ALU.subtract)
            pr.op("vector", "tensor_tensor", t3[:, :, 0, :], mu1, ss3,
                  op=ALU.mult)
            pr.op("vector", "tensor_tensor", t3[:, :, 1, :], mu2, cc3,
                  op=ALU.mult)
            pr.op("vector", "tensor_tensor", d3[:, :, 1, :], t3[:, :, 0, :],
                  t3[:, :, 1, :], op=ALU.add, sem=sdv)

        for t in range(NT):
            pr.wait("tensor", sdv)
            pr.wait("tensor", sac)
            pr.wait("tensor", sst)
            for kk in range(4):
                pr.op("tensor", "matmul", PS_B[:, 0:NH * DH],
                      c_qT[kk][:, t * 128:(t + 1) * 128],
                      W4["w_uq"][:, kk, :], start=(kk == 0), stop=(kk == 3),
                      sem=(spe if kk == 3 else None))
            pr.wait("scalar", spe)
            pr.wait("scalar", sst)
            pr.op("scalar", "copy", qc_st[:, :], PS_B[:, 0:NH * DH], sem=sac)
            pr.wait("sync", sac)
            pr.dma("sync", qc_hbm[:, t * NH * DH:(t + 1) * NH * DH],
                   qc_st[:, :], sst)
            pr.wait("tensor", sac)
            for kk in range(4):
                pr.op("tensor", "matmul", PS_C[:, 0:NH * DR],
                      c_qT[kk][:, t * 128:(t + 1) * 128],
                      W4["w_qr"][:, kk, :], start=(kk == 0), stop=(kk == 3),
                      sem=(spe if kk == 3 else None))
            pr.wait("vector", spe)
            pr.op("vector", "tensor_copy", qp_f[:, :], PS_C[:, 0:NH * DR],
                  sem=sdv)
            cc = qang[:, 0:HALF]
            ss = qang[:, HALF:2 * HALF]
            tmp = qang[:, 2 * HALF:3 * HALF]
            pr.wait("vector", sst)
            angle_combine(C["cos_q"][:, t, :], C["sin_q"][:, t, :], cc, ss,
                          tmp)
            pope_heads(qp_f[:, :], cc, ss, qrot_st[:, :], qp_sp)
            pr.wait("sync", sdv)
            pr.dma("sync", qrot_hbm[:, t * NH * DR:(t + 1) * NH * DR],
                   qrot_st[:, :], sst)
            angle_combine(C["cos_2q"][:, t, :], C["sin_2q"][:, t, :], cc, ss,
                          tmp)
            pope_heads(qp_f[:, :], cc, ss, qrl_bf[:, :], qp_sp)
            pr.wait("tensor", sdv)
            pr.wait("tensor", sac)
            for c in range(2):
                pr.op("tensor", "transpose", PS_T[:, c * 128:(c + 1) * 128],
                      qrl_bf[:, c * 128:(c + 1) * 128], C["ident_bf"][:, :],
                      sem=(spe if c == 1 else None))
            pr.wait("scalar", spe)
            pr.wait("scalar", sst)
            for c in range(2):
                pr.op("scalar", "copy", qrl_st[:, c * 128:(c + 1) * 128],
                      PS_T[:, c * 128:(c + 1) * 128],
                      sem=(sac if c == 1 else None))
            pr.wait("sync", sac)
            for c in range(2):
                pr.dma("sync",
                       qrl_hbm[c * 128:(c + 1) * 128, t * 128:(t + 1) * 128],
                       qrl_st[:, c * 128:(c + 1) * 128], sst)
        free("w4", "cq", "xn")
        pr.mark("ad")

        colv = rbuf("colv_b1", (128, L), F32, st="b1x")
        pr.dma("sync", colv[:, :], d_in["t_colv"][:, :], sl)
        Ibuf = rbuf("Ibuf", (128, L), F32, st="b1x")
        relu_b = rbuf("relu_b", (128, 4, 512), F32, st="b1x")
        qm63 = rbuf("qm63", (128, 1), F32, st="b1x")
        i8v = rbuf("i8v", (128, 8), F32, st="b1x")
        i8u = rbuf("i8u", (128, 8), U32, st="b1x")
        pr.wait("vector", sl)

        for t in range(NT):
            q0 = t * 128
            Lt = 128 if t == 0 else q0 + 64
            if t == 0:
                pr.op("vector", "tensor_copy", Ibuf[:, 0:128], C["m0"][:, :],
                      sem=sdv)
            else:
                for cc_ in range(0, Lt, 512):
                    cw_ = min(512, Lt - cc_)
                    pen = Ibuf[:, cc_:cc_ + cw_]
                    pr.op("vector", "tensor_scalar", pen,
                          colv[:, cc_:cc_ + cw_], C["qm63"][:, t:t + 1], NEG,
                          op0=ALU.is_ge, op1=ALU.mult)
                    pr.op("vector", "scalar_tensor_tensor", pen,
                          colv[:, cc_:cc_ + cw_], -EPS_TIE, pen,
                          op0=ALU.mult, op1=ALU.add, sem=sdv)
            for cc_ in range(0, Lt, 512):
                cw_ = min(512, Lt - cc_)
                pr.wait("tensor", sac)
                pr.wait("tensor", sdv)
                for h in range(NIH):
                    pr.op("tensor", "matmul", PS_A[:, h * 512:h * 512 + cw_],
                          q_iT[h][:, q0:q0 + 128],
                          k_iT[0:64, cc_:cc_ + cw_], start=True, stop=True,
                          sem=(spe if h == NIH - 1 else None))
                pr.wait("scalar", spe)
                for h in range(NIH):
                    pr.op("scalar", "activation", relu_b[:, h, 0:cw_],
                          PS_A[:, h * 512:h * 512 + cw_], AF.Relu,
                          sem=(sac if h == NIH - 1 else None))
                pr.wait("vector", sac)
                for h in range(NIH):
                    pr.op("vector", "scalar_tensor_tensor",
                          Ibuf[:, cc_:cc_ + cw_], relu_b[:, h, 0:cw_],
                          idxw_rep[:, h:h + 1], Ibuf[:, cc_:cc_ + cw_],
                          op0=ALU.mult, op1=ALU.add,
                          sem=(sdv if h == NIH - 1 else None))
            pr.wait("vector", sdv)
            nrounds = 12 if t == 0 else 4
            for r in range(nrounds):
                pr.op("vector", "max", out=i8v[:, :], in_=Ibuf[:, 0:Lt],
                      sem=sdv)
                pr.wait("vector", sdv)
                pr.op("vector", "max_index", i8u[:, :], i8v[:, :],
                      Ibuf[:, 0:Lt], sem=sdv)
                pr.wait("vector", sdv)
                if t == 0:
                    pr.op("vector", "tensor_copy", idx0f[:, r * 8:r * 8 + 8],
                          i8u[:, :])
                else:
                    pr.op("vector", "tensor_copy",
                          idx32f[:, t, r * 8:r * 8 + 8], i8u[:, :])
                if r < nrounds - 1:
                    pr.op("vector", "match_replace", Ibuf[:, 0:Lt],
                          i8v[:, :], Ibuf[:, 0:Lt], imm_value=NEG, sem=sdv)
                    pr.wait("vector", sdv)
            pr.op("vector", "tensor_copy", i8v[:, :], i8v[:, :], sem=sdv)
        if tap("idx32f", (128, NT * 32)) is not None:
            pr.wait("sync", sdv)
            pr.dma("sync", tap_out["idx32f"][:, :],
                   idx32f[:, :, :].rearrange("p t s -> p (t s)"), sst)
        if tap("idx0f", (128, 96)) is not None:
            pr.wait("sync", sdv)
            pr.dma("sync", tap_out["idx0f"][:, :], idx0f[:, :], sst)
        wcast = [rbuf(f"wcast{i}", (128, 4, D), st="b1x") for i in range(2)]
        wc_done = [0, 0]
        for e in range(NE):
            par = e % 2
            pr.wait("gpsimd", sst, wc_done[par])
            for fc in range(4):
                pr.dma("gpsimd", wcast[par][:, fc, :],
                       w_down[e, fc * 128:(fc + 1) * 128, :], sg)
            pr.wait("sync", sg)
            for fc in range(4):
                pr.dma("sync", w_down_bf[e, fc * 128:(fc + 1) * 128, :],
                       wcast[par][:, fc, :], sst)
            wc_done[par] = sst.c
        free("b1x", "qk")
        pr.mark("b1")

        kbuf = rbuf("kbuf", (128, 32, NH * DH + DR), st="b2")
        vbuf = rbuf("vbuf", (128, 32, NH * DH), st="b2")
        idxu = [rbuf(f"idxu{i}", (128, 96), U32, st="b2") for i in range(2)]
        vsA = rbuf("vsA", (128, NH * DH), st="b2")
        vsB = rbuf("vsB", (64, NH * DH), st="b2")
        kcs = rbuf("kcs", (64, 8, 192), st="b2")
        krls = rbuf("krls", (32, 192), st="b2")
        qrls = rbuf("qrls", (32, 8, 128), st="b2")
        qct = rbuf("qct", (128, NH * DH), st="b2")
        qrott = rbuf("qrott", (128, NH * DR), st="b2")
        qcfm = rbuf("qcfm", (64, 8, 128), st="b2")
        scores = rbuf("scores", (128, NH, 224), F32, st="b2")
        prod = rbuf("prod", (128, 8192), st="b2")
        tree = rbuf("tree", (128, 4096), st="b2")
        krot = rbuf("krot", (128, 32, DR), st="b2")
        scsave = rbuf("scsave", (128, 32 * NH), F32, st="b2")
        smx = rbuf("smx", (128, 2 * NH), F32, st="b2")
        attn_bf = rbuf("attn_bf", (128, NH, 224), st="b2")
        attnT_A = rbuf("attnT_A", (128, NH, 128), st="b2")
        attnT_B = rbuf("attnT_B", (64, NH, 128), st="b2")
        nl_acc = rbuf("nl_acc", (128, NH * DH), F32, st="b2")
        mix_bf = rbuf("mix_bf", (128, NH * DH), st="b2")
        mixT = rbuf("mixT", (128, 4, 128), st="b2")
        po_x = rbuf("po_x", (128, 128), F32, st="b2")
        po_sh = rbuf("po_sh", (128, 128), F32, st="b2")
        po_o = rbuf("po_o", (128, 128), F32, st="b2")
        w_out_sb = rbuf("w_out_sb", (128, 4, D), st="b2")
        wload(w_out_sb, w_out, 4)

        c_s0 = rbuf("c_s0", (128, 96 * HALF), F32, st="s0")
        s_s0 = rbuf("s_s0", (128, 96 * HALF), F32, st="s0")
        pr.dma("sync", c_s0[:, :], d_in["t_cos_s0"][:, :], sl)
        pr.dma("sync", s_s0[:, :], d_in["t_sin_s0"][:, :], sl)
        pr.wait("vector", sl)
        ca_s0, sa_s0 = make_slot_tables(c_s0[:, :], s_s0[:, :], 96, "s0",
                                        st="s0", side="right")

        kg = kbuf[:, :, :]
        vg = vbuf[:, :, :]
        NS2 = 32

        def issue_k(idx_ap, base):
            for j in range(NS2):
                pr.gather(kg[:, j, :], ktab[:, :],
                          idx_ap[:, base + j:base + j + 1], sg)
            return sg.c

        def issue_v(idx_ap, base):
            for j in range(NS2):
                pr.gather(vg[:, j, :], vtab[:, :],
                          idx_ap[:, base + j:base + j + 1], sg)
            return sg.c

        def nl_scores32(t, ca_ap, sa_ap, dst_base, slot_base):
            ksp = tree[:, :].bitcast(F32)
            kr_v = kg[:, :, NH * DH:NH * DH + DR]
            n32 = NS2 * DR
            sp_ = ksp[:, 0:n32]
            scr = ksp[:, n32:2 * n32]
            softplus(kr_v, sp_, scr)
            sp3 = sp_.rearrange("p (j two f) -> p j two f", two=2, f=HALF)
            ca3 = ca_ap[:, :].rearrange("p (j f) -> p j f", f=HALF)[
                :, slot_base:slot_base + NS2, :]
            sa3 = sa_ap[:, :].rearrange("p (j f) -> p j f", f=HALF)[
                :, slot_base:slot_base + NS2, :]
            k3 = krot[:, :, :].rearrange("p j (two f) -> p j two f",
                                         two=2, f=HALF)
            t3 = scr.rearrange("p (j two f) -> p j two f", two=2, f=HALF)
            pr.op("vector", "tensor_tensor", t3[:, :, 0, :], sp3[:, :, 0, :],
                  ca3, op=ALU.mult)
            pr.op("vector", "tensor_tensor", t3[:, :, 1, :], sp3[:, :, 1, :],
                  sa3, op=ALU.mult)
            pr.op("vector", "tensor_tensor", k3[:, :, 0, :], t3[:, :, 0, :],
                  t3[:, :, 1, :], op=ALU.subtract)
            pr.op("vector", "tensor_tensor", t3[:, :, 0, :], sp3[:, :, 0, :],
                  sa3, op=ALU.mult)
            pr.op("vector", "tensor_tensor", t3[:, :, 1, :], sp3[:, :, 1, :],
                  ca3, op=ALU.mult)
            pr.op("vector", "tensor_tensor", k3[:, :, 1, :], t3[:, :, 0, :],
                  t3[:, :, 1, :], op=ALU.add)
            for ch in range(2):
                j0 = ch * 16
                pv = prod[:, 0:16 * 512].rearrange("p (j m) -> p j m", m=512)
                qcb = qct[:, :].unsqueeze(1).to_broadcast([128, 16, 512])
                pr.op("vector", "tensor_tensor", pv,
                      kg[:, j0:j0 + 16, 0:NH * DH], qcb, op=ALU.mult)
                src_ = prod[:, 0:16 * 512]
                width = DH
                tv = tree
                while width > 1:
                    half_ = width // 2
                    s4 = src_.rearrange("p (j h d) -> p j h d", h=NH,
                                        d=width)
                    d4 = tv[:, 0:16 * NH * half_].rearrange(
                        "p (j h d) -> p j h d", h=NH, d=half_)
                    pr.op("vector", "tensor_tensor", d4,
                          s4[:, :, :, 0:half_], s4[:, :, :, half_:width],
                          op=ALU.add)
                    src_ = tv[:, 0:16 * NH * half_]
                    tv = prod if tv is tree else tree
                    width = half_
                pr.op("vector", "tensor_copy",
                      scsave[:, ch * 128:(ch + 1) * 128], src_)
            pr_r = prod[:, 0:NS2 * NH * DR].rearrange(
                "p (j h f) -> p j h f", h=NH, f=DR)
            kro4 = krot[:, :, :].unsqueeze(2).to_broadcast(
                [128, NS2, NH, DR])
            qro4 = qrott[:, :].rearrange("p (h f) -> p h f",
                                         f=DR).unsqueeze(1).to_broadcast(
                [128, NS2, NH, DR])
            pr.op("vector", "tensor_tensor", pr_r, kro4, qro4, op=ALU.mult)
            src_ = prod[:, 0:NS2 * NH * DR]
            width = DR
            tv = tree
            while width > 1:
                half_ = width // 2
                s4 = src_.rearrange("p (j h f) -> p j h f", h=NH, f=width)
                d4 = tv[:, 0:NS2 * NH * half_].rearrange(
                    "p (j h f) -> p j h f", h=NH, f=half_)
                pr.op("vector", "tensor_tensor", d4, s4[:, :, :, 0:half_],
                      s4[:, :, :, half_:width], op=ALU.add)
                src_ = tv[:, 0:NS2 * NH * half_]
                tv = prod if tv is tree else tree
                width = half_
            scr_jh = src_.rearrange("p (j h) -> p j h", h=NH)
            scv_jh = scsave[:, :].rearrange("p (j h) -> p j h", h=NH)
            dsl = scores[:, :, dst_base:dst_base + NS2]
            dsl = dsl.rearrange("p h j -> p j h")
            pr.op("vector", "tensor_tensor", scv_jh, scv_jh, scr_jh,
                  op=ALU.add)
            pr.op("vector", "tensor_scalar", dsl, scv_jh, SCALE, None,
                  op0=ALU.mult, sem=sdv)

        def nl_values32(kbase, first):
            for ch in range(2):
                j0 = ch * 16
                a4 = attn_bf[:, :, kbase + j0:kbase + j0 + 16]
                a4 = a4.rearrange("p h j -> p j h").unsqueeze(3).to_broadcast(
                    [128, 16, NH, DH])
                v4 = vg[:, j0:j0 + 16, :].rearrange("p j (h d) -> p j h d",
                                                    d=DH)
                pv = prod[:, 0:16 * 512].rearrange("p (j h d) -> p j h d",
                                                   h=NH, d=DH)
                pr.op("vector", "tensor_tensor", pv, v4, a4, op=ALU.mult)
                src_ = prod[:, 0:16 * 512]
                width = 16
                tv = tree
                while width > 1:
                    half_ = width // 2
                    s3 = src_.rearrange("p (j m) -> p j m", m=512)
                    d3 = tv[:, 0:half_ * 512].rearrange(
                        "p (j m) -> p j m", m=512)
                    pr.op("vector", "tensor_tensor", d3, s3[:, 0:half_, :],
                          s3[:, half_:width, :], op=ALU.add)
                    src_ = tv[:, 0:half_ * 512]
                    tv = prod if tv is tree else tree
                    width = half_
                if first and ch == 0:
                    pr.op("vector", "tensor_copy", nl_acc[:, :], src_,
                          sem=sdv)
                else:
                    pr.op("vector", "tensor_tensor", nl_acc[:, :],
                          nl_acc[:, :], src_, op=ALU.add, sem=sdv)

        kv_done = {}

        def cast_idx(t):
            par = t % 2
            pr.wait("vector", sg, kv_done.get(("v_issued", t - 2), 0))
            if t == 0:
                pr.op("vector", "tensor_copy", idxu[par][:, 0:96],
                      idx0f[:, :], sem=sdv)
            else:
                pr.op("vector", "tensor_copy", idxu[par][:, 0:32],
                      idx32f[:, t, :], sem=sdv)
            kv_done[("idx", t)] = sdv.c

        def softmax_and_local(t, ncols):
            q0 = t * 128
            mx = smx[:, 0:NH]
            sm = smx[:, NH:2 * NH]
            scv = scores[:, :, 0:ncols]
            pr.op("vector", "tensor_reduce", mx, scv, axis=AX.X, op=ALU.max)
            mxb = mx.unsqueeze(2).to_broadcast([128, NH, ncols])
            pr.op("vector", "tensor_tensor", scv, scv, mxb, op=ALU.subtract,
                  sem=sdv)
            pr.wait("scalar", sdv)
            pr.op("scalar", "activation", scv, scv, AF.Exp, sem=sac)
            pr.wait("vector", sac)
            pr.op("vector", "tensor_reduce", sm, scv, axis=AX.X, op=ALU.add)
            pr.op("vector", "reciprocal", sm, sm)
            smb = sm.unsqueeze(2).to_broadcast([128, NH, ncols])
            pr.op("vector", "tensor_tensor", attn_bf[:, :, 0:ncols], scv,
                  smb, op=ALU.mult, sem=sdv)
            if t > 0:
                pr.wait("tensor", sdv)
                for h in range(NH):
                    slot = (h % 4) * 128
                    pr.wait("tensor", sac)
                    pr.op("tensor", "transpose", PS_T[:, slot:slot + 128],
                          attn_bf[:, h, 0:128], C["ident_bf"][:, :], sem=spe)
                    pr.wait("scalar", spe)
                    pr.op("scalar", "copy", attnT_A[:, h, :],
                          PS_T[:, slot:slot + 128], sem=sac)
                for h in range(NH):
                    slot = (h % 4) * 128
                    pr.wait("tensor", sac)
                    pr.op("tensor", "transpose", PS_T[0:64, slot:slot + 128],
                          attn_bf[:, h, 128:192], C["ident_bf"][:, :],
                          sem=spe)
                    pr.wait("scalar", spe)
                    pr.op("scalar", "copy", attnT_B[:, h, :],
                          PS_T[0:64, slot:slot + 128], sem=sac)
                pr.wait("tensor", sac)
                pr.wait("tensor", sdv)
                for h in range(NH):
                    pr.op("tensor", "matmul", PS_B[:, h * DH:(h + 1) * DH],
                          attnT_A[:, h, :], vsA[:, h * DH:(h + 1) * DH],
                          start=True, stop=False)
                    pr.op("tensor", "matmul", PS_B[:, h * DH:(h + 1) * DH],
                          attnT_B[:, h, :], vsB[:, h * DH:(h + 1) * DH],
                          start=False, stop=True,
                          sem=(spe if h == NH - 1 else None))

        def finish_tile(t):
            q0 = t * 128
            if t > 0:
                pr.wait("vector", spe)
                pr.op("vector", "tensor_tensor", mix_bf[:, :],
                      PS_B[:, 0:512], nl_acc[:, :], op=ALU.add, sem=sdv)
            else:
                pr.op("vector", "tensor_copy", mix_bf[:, :], nl_acc[:, :],
                      sem=sdv)
            pr.wait("tensor", sdv)
            pr.wait("tensor", sac)
            for c in range(4):
                pr.op("tensor", "transpose", PS_T[:, c * 128:(c + 1) * 128],
                      mix_bf[:, c * 128:(c + 1) * 128], C["ident_bf"][:, :],
                      sem=(spe if c == 3 else None))
            pr.wait("scalar", spe)
            for c in range(4):
                pr.op("scalar", "copy", mixT[:, c, :],
                      PS_T[:, c * 128:(c + 1) * 128],
                      sem=(sac if c == 3 else None))
            pr.wait("tensor", sac)
            for dd in range(8):
                pr.wait("sync", sdv)
                pr.dma("sync", po_x[:, :],
                       x_fm[dd * 128:(dd + 1) * 128, q0:q0 + 128], sl)
                pr.dma("sync", po_sh[:, :],
                       shared_hbm[dd * 128:(dd + 1) * 128, q0:q0 + 128], sl)
                pr.wait("tensor", sdv)
                for c in range(4):
                    pr.op("tensor", "matmul", PS_C[:, 0:128],
                          w_out_sb[:, c, dd * 128:(dd + 1) * 128],
                          mixT[:, c, :], start=(c == 0), stop=(c == 3),
                          sem=(spe if c == 3 else None))
                pr.wait("vector", spe)
                pr.wait("vector", sl)
                pr.wait("vector", sst)
                pr.op("vector", "tensor_tensor", po_o[:, :], PS_C[:, 0:128],
                      po_x[:, :], op=ALU.add)
                pr.op("vector", "tensor_tensor", po_o[:, :], po_o[:, :],
                      po_sh[:, :], op=ALU.add, sem=sdv)
                pr.wait("sync", sdv)
                pr.dma("sync", partial_hbm[dd * 128:(dd + 1) * 128,
                                           q0:q0 + 128], po_o[:, :], sst)

        def load_strips(t):
            q0 = t * 128
            pr.wait("sync", spe)
            pr.wait("sync", sdv)
            pr.dma("sync", vsA[:, :], vtab[q0 - 64:q0 + 64, :], sl)
            pr.dma("sync", vsB[:, :], vtab[q0 + 64:q0 + 128, :], sl)
            pr.dma("sync", kcs[:, :, :],
                   kcT_hbm[:, :].rearrange("(h p) l -> p h l", p=64)[
                       :, :, q0 - 64:q0 + 128], sl)
            pr.dma("sync", krls[:, :], krl_hbm[:, q0 - 64:q0 + 128], sl)
            pr.dma("sync", qrls[:, :, :],
                   qrl_hbm[:, :].rearrange("(h p) l -> p h l", p=32)[
                       :, :, q0:q0 + 128], sl)
            pr.dma("sync", qct[:, :],
                   qc_hbm[:, t * NH * DH:(t + 1) * NH * DH], sl)
            pr.dma("sync", qrott[:, :],
                   qrot_hbm[:, t * NH * DR:(t + 1) * NH * DR], sl)

        def local_scores(t):
            pr.wait("tensor", sac)
            pr.wait("tensor", sl)
            for c in range(4):
                pr.op("tensor", "transpose", PS_T[:, c * 128:(c + 1) * 128],
                      qct[:, c * 128:(c + 1) * 128], C["ident_bf"][:, :],
                      sem=(spe if c == 3 else None))
            pr.wait("scalar", spe)
            for c in range(4):
                pr.op("scalar", "copy", qcfm[:, 2 * c, :],
                      PS_T[0:64, c * 128:(c + 1) * 128])
                pr.op("scalar", "copy", qcfm[:, 2 * c + 1, :],
                      PS_T[64:128, c * 128:(c + 1) * 128],
                      sem=(sac if c == 3 else None))
            pr.wait("tensor", sac)
            for h in range(NH):
                pcol = (h % 2) * 224
                pr.wait("tensor", sdv)
                pr.op("tensor", "matmul", PS_D[:, pcol:pcol + 192],
                      qcfm[:, h, :], kcs[:, h, :], start=True, stop=False)
                pr.op("tensor", "matmul", PS_D[:, pcol:pcol + 192],
                      qrls[:, h, :], krls[:, :], start=False, stop=True,
                      sem=spe)
                pr.wait("vector", spe)
                pr.op("vector", "scalar_tensor_tensor",
                      scores[:, h, 0:192], PS_D[:, pcol:pcol + 192],
                      SCALE, C["band"][:, :], op0=ALU.mult, op1=ALU.add,
                      sem=sdv)

        cast_idx(0)
        pr.wait("gpsimd", sdv)
        pr.dma("sync", qct[:, :], qc_hbm[:, 0:NH * DH], sl)
        pr.dma("sync", qrott[:, :], qrot_hbm[:, 0:NH * DR], sl)
        pr.wait("vector", sl)
        pr.wait("scalar", sl)
        for sw in range(3):
            pr.wait("gpsimd", sdv)
            issue_k(idxu[0], sw * 32)
            pr.wait("scalar", sg)
            pr.wait("vector", sg)
            nl_scores32(0, ca_s0, sa_s0, sw * 32, sw * 32)
        softmax_and_local(0, 96)
        for sw in range(3):
            pr.wait("gpsimd", sdv)
            issue_v(idxu[0], sw * 32)
            pr.wait("vector", sg)
            nl_values32(sw * 32, first=(sw == 0))
        finish_tile(0)
        free("s0")

        cast_idx(1)
        pr.wait("gpsimd", sdv)
        kv_done[("k", 1)] = issue_k(idxu[1], 0)
        kv_done[("v", 1)] = issue_v(idxu[1], 0)
        kv_done[("v_issued", 1)] = sg.c
        for t in range(1, NT):
            par = t % 2
            load_strips(t)
            local_scores(t)
            pr.wait("scalar", sg, kv_done[("k", t)])
            pr.wait("vector", sg, kv_done[("k", t)])
            nl_scores32(t, ca_nl, sa_nl, 192, 0)
            if t + 1 < NT:
                cast_idx(t + 1)
                pr.wait("gpsimd", sdv)
                kv_done[("k", t + 1)] = issue_k(idxu[(t + 1) % 2], 0)
            softmax_and_local(t, 224)
            pr.wait("vector", sg, kv_done[("v", t)])
            nl_values32(192, first=True)
            if t + 1 < NT:
                pr.wait("gpsimd", sdv)
                kv_done[("v", t + 1)] = issue_v(idxu[(t + 1) % 2], 0)
                kv_done[("v_issued", t + 1)] = sg.c
            finish_tile(t)
        if tap("partial", (D, L)) is not None:
            pr.wait("sync", sst)
            pr.dma("sync", tap_out["partial"][:, :], partial_hbm[:, :], sst)
        free("b2")
        pr.mark("b2")

        xnh = [rbuf(f"xnh{c}", (128, 2048), st="f") for c in range(8)]
        wup_b = [rbuf(f"wup{b_}", (128, 8, FF), st="f") for b_ in range(2)]
        cwrow = [rbuf(f"cwrow{b_}", (1, 2048), BF16, st="f")
                 for b_ in range(2)]
        cwrep = [rbuf(f"cwrep{b_}", (128, 2048), BF16, st="f")
                 for b_ in range(2)]
        hsil = [rbuf(f"hsil{b_}", (128, 1024), st="f") for b_ in range(2)]
        hscl = [rbuf(f"hscl{b_}", (128, 1024), st="f") for b_ in range(2)]
        ones_bf = rbuf("ones_bf", (1, 128), BF16, st="f")
        pr.wait("vector", sl)
        pr.op("vector", "memset", ones_bf[:, :], 1.0, sem=sdv)
        wup_done = [0, 0]
        cwrow_done = [0, 0]
        cwrep_done = [0, 0]
        pair_sac = [0, 0]
        hbuf_done = [0, 0]
        hdma_done = [0, 0]
        for th in range(2):
            tb = th * 2048
            pr.wait("sync", spe)
            for c in range(8):
                pr.dma("sync", xnh[c][:, :],
                       xn_fm_hbm[c * 128:(c + 1) * 128, tb:tb + 2048], sl)
            for e in range(NE):
                par = e % 2
                bb = wup_b[par]
                pr.wait("gpsimd", spe, wup_done[par])
                for kk in range(8):
                    pr.dma("gpsimd", bb[:, kk, :],
                           w_up[e, kk * 128:(kk + 1) * 128, :], sg)
                pr.wait("sync", spe, cwrow_done[par])
                pr.dma("sync", cwrow[par][:, :],
                       cw_bf_hbm[e:e + 1, tb:tb + 2048], sl)
                pr.wait("tensor", sl)
                pr.wait("tensor", sdv, cwrep_done[par])
                rep_mm = []
                for j in range(4):
                    ps = PS_B if j % 2 == 0 else PS_C
                    pr.wait("tensor", sdv)
                    pr.op("tensor", "matmul", ps[:, 0:512],
                          ones_bf[:, :],
                          cwrow[par][:, j * 512:(j + 1) * 512],
                          start=True, stop=True, sem=spe)
                    pr.wait("vector", spe)
                    pr.op("vector", "tensor_copy",
                          cwrep[par][:, j * 512:(j + 1) * 512],
                          ps[:, 0:512], sem=sdv)
                cwrow_done[par] = spe.c
                cw_ready = sdv.c
                pr.wait("tensor", sg)
                for fc in range(4):
                    for u in range(2):
                        pp = (fc * 2 + u) % 2
                        pcol = pp * 1024
                        pr.wait("tensor", sac, pair_sac[pp])
                        for kk in range(8):
                            for s2 in range(2):
                                pr.op("tensor", "matmul",
                                      PS_A[:, pcol + s2 * 512:
                                           pcol + s2 * 512 + 512],
                                      bb[:, kk, fc * 128:(fc + 1) * 128],
                                      xnh[kk][:, u * 1024 + s2 * 512:
                                              u * 1024 + s2 * 512 + 512],
                                      start=(kk == 0), stop=(kk == 7),
                                      sem=(spe if (kk == 7 and s2 == 1)
                                           else None))
                        pr.wait("scalar", spe)
                        pr.wait("scalar", sdv, hbuf_done[pp])
                        pr.op("scalar", "activation", hsil[pp][:, :],
                              PS_A[:, pcol:pcol + 1024], AF.Silu, sem=sac)
                        pair_sac[pp] = sac.c
                        pr.wait("vector", sac)
                        pr.wait("vector", sdv, cw_ready)
                        pr.wait("vector", sst, hdma_done[pp])
                        pr.op("vector", "tensor_tensor", hscl[pp][:, :],
                              hsil[pp][:, :],
                              cwrep[par][:, u * 1024:(u + 1) * 1024],
                              op=ALU.mult, sem=sdv)
                        hbuf_done[pp] = sdv.c
                        pr.wait("sync", sdv)
                        Tb0 = th * 4 + u * 2
                        pr.dma("sync",
                               h_hbm[e, Tb0:Tb0 + 2, :,
                                     fc * 512:(fc + 1) * 512].rearrange(
                                   "tb p x -> p tb x"),
                               hscl[pp][:, :].rearrange(
                                   "p (tb x) -> p tb x", x=512), sst)
                        hdma_done[pp] = sst.c
                wup_done[par] = spe.c
                cwrep_done[par] = sdv.c

        free("ps_main")
        PS_F = stacks.setdefault("ps_f", ctx.enter_context(ExitStack())) \
            .enter_context(nc.psum_tensor("PS_F", [128, 4096], F32))
        if tap("wdall", (NE * FF, D), BF16) is not None:
            tbuf = rbuf("tbuf_dbg", (128, 4, D), st="f")
            for e in range(NE):
                pr.wait("sync", sst)
                pr.dma("sync", tbuf[:, :, :],
                       w_down_bf[e, :, :].rearrange("(c p) d -> p c d",
                                                    p=128), sl)
                pr.wait("sync", sl)
                for fc in range(4):
                    pr.dma("sync",
                           tap_out["wdall"][e * FF + fc * 128:
                                            e * FF + (fc + 1) * 128, :],
                           tbuf[:, fc, :], sst)
        if tap("hball", (NE * 128, 2048), BF16) is not None:
            tbuf2 = rbuf("tbuf2_dbg", (128, 2048), st="f")
            for e in range(NE):
                pr.wait("sync", sst)
                pr.dma("sync", tbuf2[:, :], h_hbm[e, 0, :, :], sl)
                pr.wait("sync", sl)
                pr.dma("sync", tap_out["hball"][e * 128:(e + 1) * 128, :],
                       tbuf2[:, :], sst)
        drain_hbm = dscratch("drain_hbm", (128, 2048), BF16)
        settle_sb = rbuf("settle_sb", (128, 4, D), st="f")
        settle_flat = settle_sb[:, :, :].rearrange("p c d -> p (c d)")
        for e in range(NE):
            pr.wait("sync", sst)
            pr.dma("sync", settle_sb[:, :, :],
                   w_down_bf[e, :, :].rearrange("(c p) d -> p c d", p=128),
                   sl)
            pr.wait("sync", sl)
            pr.dma("sync", drain_hbm[:, :], settle_flat[:, 0:2048], sst)
        for e in range(NE):
            pr.wait("sync", sst)
            pr.dma("sync", settle_flat[:, 0:2048], h_hbm[e, 0, :, :], sl)
            pr.wait("sync", sl)
            pr.dma("sync", drain_hbm[:, :], settle_flat[:, 0:2048], sst)
        pr.wait("sync", sst)
        hrd = [rbuf(f"hrd{b_}", (128, 2048), st="f") for b_ in range(2)]
        wdn_b = [rbuf(f"wdn{b_}", (128, 4, D), st="f") for b_ in range(2)]
        po_p = rbuf("po_p", (128, 512), F32, st="f")
        y_sb = rbuf("y_sb", (128, 512), F32, st="f")
        buf_done = [0, 0]
        for Tb in range(8):
            tb = Tb * 512
            pr.wait("tensor", sdv)
            for e in range(NE):
                par = e % 2
                hb = hrd[par]
                wb = wdn_b[par]
                pr.wait("sync", spe, buf_done[par])
                pr.dma("sync", hb[:, :], h_hbm[e, Tb, :, :], sl)
                pr.dma("sync", wb[:, :, :],
                       w_down_bf[e, :, :].rearrange("(c p) d -> p c d",
                                                    p=128), sl)
                if Tb == 0 and e == 3:
                    if tap("wb3", (128, 4 * D), BF16) is not None:
                        pr.wait("sync", sl)
                        pr.dma("sync", tap_out["wb3"][:, :],
                               wb[:, :, :].rearrange("p c d -> p (c d)"), sst)
                    if tap("hb3", (128, 2048), BF16) is not None:
                        pr.wait("sync", sl)
                        pr.dma("sync", tap_out["hb3"][:, :], hb[:, :], sst)
                    pr.wait("tensor", sst)
                pr.wait("tensor", sl)
                for dd in range(8):
                    for fc in range(4):
                        pr.op("tensor", "matmul",
                              PS_F[:, dd * 512:(dd + 1) * 512],
                              wb[:, fc, dd * 128:(dd + 1) * 128],
                              hb[:, fc * 512:(fc + 1) * 512],
                              start=(e == 0 and fc == 0),
                              stop=(e == NE - 1 and fc == 3),
                              sem=(spe if (fc == 3 and dd == 7) else None))
                buf_done[par] = spe.c
            pr.wait("vector", spe)
            for dd in range(8):
                pr.wait("sync", sdv)
                pr.dma("sync", po_p[:, :],
                       partial_hbm[dd * 128:(dd + 1) * 128, tb:tb + 512],
                       sl)
                pr.wait("vector", sl)
                pr.wait("vector", sst)
                pr.op("vector", "tensor_tensor", y_sb[:, :],
                      PS_F[:, dd * 512:(dd + 1) * 512], po_p[:, :],
                      op=ALU.add, sem=sdv)
                pr.wait("sync", sdv)
                pr.dma("sync", y_fm[dd * 128:(dd + 1) * 128, tb:tb + 512],
                       y_sb[:, :], sst)
        pr.wait("sync", sst)
        pr.mark("f")
        pr.truncate(limit)

        block = ctx.enter_context(nc.Block())

        @block.sync
        def _(q):
            for fn in pr.ops["sync"]:
                fn(q)

        @block.gpsimd
        def _(q):
            for fn in pr.ops["gpsimd"]:
                fn(q)

        @block.scalar
        def _(q):
            for fn in pr.ops["scalar"]:
                fn(q)

        @block.vector
        def _(q):
            for fn in pr.ops["vector"]:
                fn(q)

        @block.tensor
        def _(q):
            for fn in pr.ops["tensor"]:
                fn(q)

    return nc, list(tap_out.keys())


def kernel(**inputs):
    HT = host_tables()
    nc, tap_names = build_program(taps=TAPS)
    A = {k: np.asarray(v) for k, v in inputs.items()}
    base = {
        "W_dkv": A["L0_attn_W_dkv"], "W_uk": A["L0_attn_W_uk"],
        "W_uv": A["L0_attn_W_uv"], "W_dq": A["L0_attn_W_dq"],
        "W_uq": A["L0_attn_W_uq"], "W_qr": A["L0_attn_W_qr"],
        "W_kr": A["L0_attn_W_kr"], "W_out": A["L0_attn_W_out"],
        "idx_wq": A["L0_attn_idx_wq"], "idx_wk": A["L0_attn_idx_wk"],
        "idx_w": np.ascontiguousarray(A["L0_attn_idx_w"].reshape(1, NIH)),
        "raw_delta": np.ascontiguousarray(
            A["L0_attn_raw_delta"].reshape(1, HALF)),
        "cent_T": np.ascontiguousarray(A["L0_ffn_centroids"].T),
        "exp_bias": np.ascontiguousarray(
            A["L0_ffn_expert_biases"].reshape(1, NE)),
        "Ws_up": A["L0_ffn_Ws_up"], "Ws_down": A["L0_ffn_Ws_down"],
        "W_up": A["L0_ffn_W_up"], "W_down": A["L0_ffn_W_down"],
    }
    for nm, arr in HT.items():
        base["t_" + nm] = arr
    in_maps = []
    for c in range(B):
        m = dict(base)
        xc = np.ascontiguousarray(A["L0_attn_x"][c])
        m["x_tok"] = xc
        m["x_fm"] = np.ascontiguousarray(xc.T)
        in_maps.append(m)
    res = run_bass_kernel_spmd(nc, in_maps, core_ids=list(range(B)))
    outs = []
    taps = []
    for c in range(B):
        r = res.results[c]
        outs.append(np.ascontiguousarray(np.asarray(r["y_fm"]).T))
        taps.append({n: np.asarray(r["tap_" + n]) for n in tap_names})
    y = np.stack(outs, axis=0).astype(np.float32)
    if tap_names:
        kernel.last_taps = taps
    return y


# version 12 (final)
# speedup vs baseline: 1.5439x; 1.0366x over previous
import math
import numpy as np
import ml_dtypes
import concourse.bass as bass
import concourse.mybir as mybir
from concourse.bass import IndirectOffsetOnAxis
from concourse.bass_utils import run_bass_kernel_spmd
from contextlib import ExitStack

F32 = mybir.dt.float32
BF16 = mybir.dt.bfloat16
U32 = mybir.dt.uint32
AF = mybir.ActivationFunctionType
ALU = mybir.AluOpType
AX = mybir.AxisListType
bfdt = ml_dtypes.bfloat16

B, L, D = 8, 4096, 1024
DCKV, DCQ = 256, 512
NH, DH, DR, HALF = 8, 64, 32, 16
NIH = 4
NE, FF = 64, 512
SCALE = (DH + DR) ** -0.5
NT = L // 128
EPS_TIE = 1e-10
NEG = -1.0e30
NSL = 16
TSL = 512

TAPS = []

ENGINES = ("sync", "gpsimd", "scalar", "vector", "tensor")

_HOST_CACHE = {}


def host_tables():
    if _HOST_CACHE:
        return _HOST_CACHE
    t = {}
    f32 = np.float32
    theta = (1.0 / 10000.0 ** (2.0 * np.arange(HALF) / DR)).astype(np.float64)
    k = np.arange(L, dtype=np.float64)[:, None]

    def tile_view(a):
        return np.ascontiguousarray(
            a.reshape(NT, 128, HALF).transpose(1, 0, 2)).astype(f32)

    t["cos_q"] = tile_view(np.cos(k * theta))
    t["sin_q"] = tile_view(np.sin(k * theta))
    t["cos_2q"] = tile_view(np.cos(2.0 * k * theta))
    t["sin_2q"] = tile_view(np.sin(2.0 * k * theta))
    t["cos_k63"] = tile_view(np.cos((k + 63.0) * theta))
    t["sin_k63"] = tile_view(np.sin((k + 63.0) * theta))
    j = np.arange(64, 96, dtype=np.float64)[:, None]
    t["cos_snl"] = np.broadcast_to(np.cos(j * theta).reshape(1, -1),
                                   (128, 32 * HALF)).astype(f32).copy()
    t["sin_snl"] = np.broadcast_to(np.sin(j * theta).reshape(1, -1),
                                   (128, 32 * HALF)).astype(f32).copy()
    j0 = np.arange(0, 96, dtype=np.float64)[:, None]
    t["cos_s0"] = np.broadcast_to(np.cos(j0 * theta).reshape(1, -1),
                                  (128, 96 * HALF)).astype(f32).copy()
    t["sin_s0"] = np.broadcast_to(np.sin(j0 * theta).reshape(1, -1),
                                  (128, 96 * HALF)).astype(f32).copy()
    t["ident_bf"] = np.eye(128).astype(bfdt)
    t["ident_f32"] = np.eye(128).astype(f32)
    t["colv"] = np.broadcast_to(np.arange(L, dtype=f32).reshape(1, L),
                                (128, L)).copy()
    p = np.arange(128)[:, None]
    s = np.arange(192)[None, :]
    t["band"] = np.where((s >= p + 1) & (s <= p + 64), 0.0, NEG).astype(f32)
    c0 = np.arange(128)[None, :].astype(np.float64)
    q0 = np.arange(128)[:, None].astype(np.float64)
    loc = (c0 >= q0 - 63) & (c0 <= q0)
    fut = c0 > q0
    t["m0"] = np.where(loc, 2e9 - 1e3 * c0,
                       np.where(fut, -2e9 - 1e3 * c0,
                                -EPS_TIE * c0)).astype(f32)
    t["iota64"] = np.broadcast_to(np.arange(NE, dtype=f32).reshape(1, NE),
                                  (128, NE)).copy()
    t["pvec"] = np.arange(128, dtype=f32).reshape(128, 1).copy()
    t["qm63"] = (np.arange(NT, dtype=f32)[None, :] * 128.0 - 63.0
                 + np.arange(128, dtype=f32)[:, None]).copy()
    t["ones_row_f"] = np.ones((1, 128), f32)
    _HOST_CACHE.update(t)
    return t


class Sem:
    def __init__(self, handle, name):
        self.h = handle
        self.c = 0
        self.name = name


class Prog:

    def __init__(self):
        self.ops = {e: [] for e in ENGINES}
        self.n = 0
        self.marks = {}

    def mark(self, name):
        self.marks[name] = {e: len(self.ops[e]) for e in ENGINES}

    def truncate(self, name):
        if name is None or name not in self.marks:
            return
        for e in ENGINES:
            del self.ops[e][self.marks[name][e]:]

    def emit(self, eng, fn):
        self.ops[eng].append(fn)
        self.n += 1

    def wait(self, eng, sem, val=None):
        v = sem.c if val is None else val
        if v > 0:
            self.emit(eng, lambda q, s=sem.h, v=v: q.wait_ge(s, v))

    def dma(self, eng, out, in_, sem, **kw):
        sem.c += 16
        self.emit(eng, lambda q, o=out, i=in_, s=sem.h, kw=kw:
                  q.dma_start(out=o, in_=i, **kw).then_inc(s, 16))
        return sem.c

    DRAIN_AFTER = ("tensor_reduce", "reciprocal")
    NO_RT = ("max", "max_index", "match_replace", "drain", "engine_nop")
    rt_sem = None
    paranoid = True

    def op(self, eng, meth, *args, sem=None, **kw):
        need_drain = (eng == "vector" and meth in self.DRAIN_AFTER)
        need_rt = False
        if (eng in ("vector", "scalar") and meth not in self.NO_RT
                and not need_drain and args):
            if self.paranoid:
                need_rt = True
            else:
                try:
                    out0 = args[0] if not ("out" in kw) else kw["out"]
                    if hasattr(out0, "free_size") and out0.free_size <= 64:
                        need_rt = True
                except Exception:
                    pass
        if sem is not None:
            sem.c += 1
        use = sem if sem is not None else (self.rt_sem if need_rt else None)
        if sem is None and need_rt:
            self.rt_sem.c += 1

        def fn(q, m=meth, a=args, kw=kw, s=(None if need_drain else use)):
            r = getattr(q, m)(*a, **kw)
            if s is not None:
                r.then_inc(s.h, 1)
            return r
        self.emit(eng, fn)
        if need_drain:
            def fd(q, s=sem):
                r = q.drain()
                if s is not None:
                    r.then_inc(s.h, 1)
                return r
            self.emit(eng, fd)
        if need_rt and not need_drain:
            self.wait(eng, use)
        return None if sem is None else sem.c

    def gather(self, out_ap, table_ap, idx_col_ap, sem):
        sem.c += 16
        self.emit("gpsimd", lambda q, o=out_ap, t=table_ap, i=idx_col_ap,
                  s=sem.h: q.indirect_dma_start(
                      out=o, out_offset=None, in_=t,
                      in_offset=IndirectOffsetOnAxis(ap=i, axis=0)
                  ).then_inc(s, 16))
        return sem.c


def build_program(taps=(), limit=None):
    nc = bass.Bass("TRN2", target_bir_lowering=False)
    HT = host_tables()
    pr = Prog()

    d_in = {}

    def din(name, shape, dt=F32):
        d_in[name] = nc.dram_tensor(name, list(shape), dt,
                                    kind="ExternalInput").ap()
        return d_in[name]

    x_tok = din("x_tok", (L, D))
    x_fm = din("x_fm", (D, L))
    w_dkv = din("W_dkv", (D, DCKV))
    w_uk = din("W_uk", (DCKV, NH * DH))
    w_uv = din("W_uv", (DCKV, NH * DH))
    w_dq = din("W_dq", (D, DCQ))
    w_uq = din("W_uq", (DCQ, NH * DH))
    w_qr = din("W_qr", (DCQ, NH * DR))
    w_kr = din("W_kr", (D, DR))
    w_out = din("W_out", (NH * DH, D))
    idx_wq = din("idx_wq", (D, NIH * DH))
    idx_wk = din("idx_wk", (D, DH))
    idx_w = din("idx_w", (1, NIH))
    raw_delta = din("raw_delta", (1, HALF))
    cent_T = din("cent_T", (D, NE))
    exp_bias = din("exp_bias", (1, NE))
    ws_up = din("Ws_up", (D, FF))
    ws_down = din("Ws_down", (FF, D))
    w_up = din("W_up", (NE, D, FF))
    w_down = din("W_down", (NE, FF, D))
    for nm, arr in HT.items():
        din("t_" + nm, arr.shape, BF16 if arr.dtype == bfdt else F32)

    y_fm = nc.dram_tensor("y_fm", [D, L], F32, kind="ExternalOutput").ap()
    tap_out = {}

    def tap(name, shape, dt=F32):
        if name in taps:
            tap_out[name] = nc.dram_tensor("tap_" + name, list(shape), dt,
                                           kind="ExternalOutput").ap()
            return tap_out[name]
        return None

    def dscratch(name, shape, dt):
        return nc.dram_tensor(name, list(shape), dt, kind="Internal").ap()

    xn_fm_hbm = dscratch("xn_fm_hbm", (D, L), BF16)
    ktab = dscratch("ktab", (L, NH * DH + DR), BF16)
    vtab = dscratch("vtab", (L, NH * DH), BF16)
    shared_hbm = dscratch("shared_hbm", (D, L), F32)
    partial_hbm = dscratch("partial_hbm", (D, L), F32)
    cw_hbm = dscratch("cw_hbm", (NE, L), F32)
    h_hbm = dscratch("h_hbm", (NE, 8, 128, 2048), BF16)
    w_down_bf = dscratch("w_down_bf", (NE, FF, D), BF16)
    cw_bf_hbm = dscratch("cw_bf_hbm", (NE, L), BF16)
    kcT_hbm = dscratch("kcT_hbm", (NH * DH, L), BF16)
    krl_hbm = dscratch("krl_hbm", (DR, L), BF16)
    qrl_hbm = dscratch("qrl_hbm", (NH * DR, L), BF16)
    qc_hbm = dscratch("qc_hbm", (128, NT * NH * DH), BF16)
    qrot_hbm = dscratch("qrot_hbm", (128, NT * NH * DR), BF16)

    with ExitStack() as ctx:
        stacks = {}

        def sbuf(name, shape, dt=BF16, st="persist", side="left"):
            if st not in stacks:
                stacks[st] = ctx.enter_context(ExitStack())
            return stacks[st].enter_context(
                nc.sbuf_tensor(name, list(shape), dt, side=side))

        def rbuf(name, shape, dt=BF16, st=None):
            return sbuf(name, shape, dt, st=st, side="right")

        def sem(name):
            return Sem(ctx.enter_context(nc.semaphore(name)), name)

        sl = sem("sl")
        sg = sem("sg")
        sst = sem("sst")
        spe = sem("spe")
        sdv = sem("sdv")
        sac = sem("sac")
        srt = sem("srt")
        sep = sem("sep")
        sarr = sem("sarr")
        pr.rt_sem = srt
        ALLSEMS = (sl, sg, sst, spe, sdv, sac, srt)
        arrs = sbuf("arrs", (128, 16), F32, st="persist")

        def barrier(reset=True):
            for e in ENGINES:
                for s in ALLSEMS:
                    pr.wait(e, s)

        def free(*names):
            barrier()
            for nm in names:
                if nm in stacks:
                    stacks[nm].pop_all().close()
                    del stacks[nm]

        C = {}
        for nm in ("cos_q", "sin_q", "cos_2q", "sin_2q", "cos_k63", "sin_k63",
                   "cos_snl", "sin_snl", "ident_bf", "ident_f32", "band",
                   "m0", "iota64", "pvec", "qm63", "ones_row_f"):
            arr = HT[nm]
            dt = BF16 if arr.dtype == bfdt else F32
            C[nm] = sbuf("c_" + nm, arr.shape, dt)
            src_ap = d_in["t_" + nm]
            dst_ap = C[nm][:, :] if arr.ndim == 2 else C[nm][:, :, :]
            src_ap = src_ap[:, :] if arr.ndim == 2 else src_ap[:, :, :]
            pr.dma("sync", dst_ap, src_ap, sl)

        pr.wait("vector", sl)
        one_col = sbuf("one_col", (128, 1), F32)
        pr.op("vector", "memset", one_col[:, :], 1.0)
        epsln = sbuf("epsln", (128, 1), F32)
        pr.op("vector", "memset", epsln[:, :], 1e-5)
        ones1 = sbuf("ones1", (128, 1), F32)
        pr.op("vector", "memset", ones1[:, :], 1.0)
        halfpi = sbuf("halfpi", (1, 1), F32)
        pr.op("vector", "memset", halfpi[:, :], math.pi / 2.0, sem=sdv)

        idx32f = sbuf("idx32f", (128, NT, 32), F32)
        idx0f = sbuf("idx0f", (128, 96), F32)

        stacks["ps_main"] = ctx.enter_context(ExitStack())
        _psm = stacks["ps_main"]
        PS_A = _psm.enter_context(nc.psum_tensor("PS_A", [128, 2048], F32))
        PS_B = _psm.enter_context(nc.psum_tensor("PS_B", [128, 512], F32))
        PS_C = _psm.enter_context(nc.psum_tensor("PS_C", [128, 512], F32))
        PS_D = _psm.enter_context(nc.psum_tensor("PS_D", [128, 512], F32))
        PS_T = _psm.enter_context(nc.psum_tensor("PS_T", [128, 512], BF16))

        def wload(dst, src, kchunks):
            for kk in range(kchunks):
                pr.dma("gpsimd", dst[:, kk, :],
                       src[kk * 128:(kk + 1) * 128, :], sg)

        idxw_sb = sbuf("idxw_sb", (1, NIH), F32)
        pr.dma("sync", idxw_sb[:, :], idx_w[:, :], sl)
        rdelta_sb = sbuf("rdelta_sb", (1, HALF), F32)
        pr.dma("sync", rdelta_sb[:, :], raw_delta[:, :], sl)
        ebias_sb = sbuf("ebias_sb", (1, NE), F32)
        pr.dma("sync", ebias_sb[:, :], exp_bias[:, :], sl)

        dl = sbuf("dl", (1, 8 * HALF), F32)
        pr.wait("scalar", sl)
        pr.wait("scalar", sdv)
        sigv = dl[:, 0:HALF]
        hv = dl[:, HALF:2 * HALF]
        sinh = dl[:, 2 * HALF:3 * HALF]
        cosh = dl[:, 3 * HALF:4 * HALF]
        cosd1 = dl[:, 4 * HALF:5 * HALF]
        sind1 = dl[:, 5 * HALF:6 * HALF]
        pr.op("scalar", "activation", sigv, rdelta_sb[:, :], AF.Sigmoid)
        pr.op("scalar", "mul", hv, sigv, -math.pi)
        pr.op("scalar", "activation", sinh, hv, AF.Sin)
        pr.op("scalar", "activation", cosh, hv, AF.Sin, bias=halfpi[0:1, :],
              sem=sac)
        pr.wait("vector", sac)
        pr.op("vector", "tensor_tensor", cosd1, sinh, sinh, op=ALU.mult)
        pr.op("vector", "tensor_scalar", cosd1, cosd1, -2.0, 1.0,
              op0=ALU.mult, op1=ALU.add)
        pr.op("vector", "tensor_tensor", sind1, sinh, cosh, op=ALU.mult)
        pr.op("vector", "tensor_scalar", sind1, sind1, 2.0, None,
              op0=ALU.mult, sem=sdv)
        cosd = sbuf("cosd", (128, HALF), F32)
        sind = sbuf("sind", (128, HALF), F32)
        ebias_rep = sbuf("ebias_rep", (128, NE), F32)
        idxw_rep = sbuf("idxw_rep", (128, NIH), F32)
        pr.wait("tensor", sdv)
        pr.wait("tensor", sl)
        pr.op("tensor", "matmul", PS_D[:, 0:HALF], C["ones_row_f"][:, :],
              cosd1, start=True, stop=True)
        pr.op("tensor", "matmul", PS_D[:, HALF:2 * HALF],
              C["ones_row_f"][:, :], sind1, start=True, stop=True)
        pr.op("tensor", "matmul", PS_D[:, 64:64 + NE], C["ones_row_f"][:, :],
              ebias_sb[:, :], start=True, stop=True)
        pr.op("tensor", "matmul", PS_D[:, 128:128 + NIH],
              C["ones_row_f"][:, :], idxw_sb[:, :], start=True, stop=True,
              sem=spe)
        pr.wait("vector", spe)
        pr.op("vector", "tensor_copy", cosd[:, :], PS_D[:, 0:HALF])
        pr.op("vector", "tensor_copy", sind[:, :], PS_D[:, HALF:2 * HALF])
        pr.op("vector", "tensor_copy", ebias_rep[:, :], PS_D[:, 64:64 + NE])
        pr.op("vector", "tensor_copy", idxw_rep[:, :],
              PS_D[:, 128:128 + NIH], sem=sdv)

        def make_slot_tables(cosrep_ap, sinrep_ap, nslots, nm, st="persist",
                             side="left"):
            ca = sbuf("ca_" + nm, (128, nslots * HALF), F32, st=st, side=side)
            sa = sbuf("sa_" + nm, (128, nslots * HALF), F32, st=st, side=side)
            t1 = sbuf("tmp_" + nm, (128, nslots * HALF), F32, st=st,
                      side=side)
            cv = cosrep_ap.rearrange("p (j f) -> p j f", f=HALF)
            sv = sinrep_ap.rearrange("p (j f) -> p j f", f=HALF)
            cosd3 = cosd[:, :].unsqueeze(1).to_broadcast([128, nslots, HALF])
            sind3 = sind[:, :].unsqueeze(1).to_broadcast([128, nslots, HALF])
            ca3 = ca[:, :].rearrange("p (j f) -> p j f", f=HALF)
            sa3 = sa[:, :].rearrange("p (j f) -> p j f", f=HALF)
            t13 = t1[:, :].rearrange("p (j f) -> p j f", f=HALF)
            pr.op("vector", "tensor_tensor", ca3, cv, cosd3, op=ALU.mult)
            pr.op("vector", "tensor_tensor", t13, sv, sind3, op=ALU.mult)
            pr.op("vector", "tensor_tensor", ca3, ca3, t13, op=ALU.subtract)
            pr.op("vector", "tensor_tensor", sa3, sv, cosd3, op=ALU.mult)
            pr.op("vector", "tensor_tensor", t13, cv, sind3, op=ALU.mult)
            pr.op("vector", "tensor_tensor", sa3, sa3, t13, op=ALU.add,
                  sem=sdv)
            return ca, sa

        ca_nl, sa_nl = make_slot_tables(C["cos_snl"][:, :],
                                        C["sin_snl"][:, :], 32, "nl")

        def angle_combine(cv, sv, cc, ss, tmp):
            pr.op("vector", "tensor_tensor", cc, cv, cosd[:, :], op=ALU.mult)
            pr.op("vector", "tensor_tensor", tmp, sv, sind[:, :], op=ALU.mult)
            pr.op("vector", "tensor_tensor", cc, cc, tmp, op=ALU.subtract)
            pr.op("vector", "tensor_tensor", ss, sv, cosd[:, :], op=ALU.mult)
            pr.op("vector", "tensor_tensor", tmp, cv, sind[:, :], op=ALU.mult)
            pr.op("vector", "tensor_tensor", ss, ss, tmp, op=ALU.add)

        def softplus(src, dst, scr):
            pr.wait("scalar", sdv)
            pr.op("scalar", "activation", scr, src, AF.Abs)
            pr.op("scalar", "activation", scr, scr, AF.Exp, scale=-1.0)
            pr.op("scalar", "activation", scr, scr, AF.Ln, bias=ones1[:, :])
            pr.op("scalar", "activation", dst, src, AF.Relu, sem=sac)
            pr.wait("vector", sac)
            pr.op("vector", "tensor_tensor", dst, dst, scr, op=ALU.add)

        xn_fm = [sbuf(f"xn_fm{c}", (128, L), st="xn") for c in range(8)]
        ln_x = rbuf("ln_x", (128, D), F32, st="ln")
        ln_sq = rbuf("ln_sq", (128, D), F32, st="ln")
        ln_st = rbuf("ln_st", (128, 8), F32, st="ln")
        ln_bf = rbuf("ln_bf", (128, D), st="ln")
        for t in range(NT):
            pr.wait("sync", sdv)
            pr.dma("sync", ln_x[:, :], x_tok[t * 128:(t + 1) * 128, :], sl)
            pr.wait("vector", sl)
            pr.wait("vector", sac)
            mean = ln_st[:, 0:1]
            m2 = ln_st[:, 1:2]
            var = ln_st[:, 2:3]
            lnv = ln_st[:, 3:4]
            rstd = ln_st[:, 4:5]
            pr.op("vector", "tensor_reduce", mean, ln_x[:, :], axis=AX.X,
                  op=ALU.add)
            pr.op("vector", "tensor_scalar", mean, mean, 1.0 / D, None,
                  op0=ALU.mult)
            pr.op("vector", "tensor_copy", ln_st[:, 5:6], ln_st[:, 0:1],
                  sem=sdv)
            pr.wait("scalar", sdv)
            pr.op("scalar", "activation", ln_sq[:, :], ln_x[:, :], AF.Square,
                  sem=sac)
            pr.wait("vector", sac)
            pr.op("vector", "tensor_reduce", m2, ln_sq[:, :], axis=AX.X,
                  op=ALU.add)
            pr.op("vector", "tensor_scalar", m2, m2, 1.0 / D, None,
                  op0=ALU.mult)
            pr.op("vector", "tensor_copy", ln_st[:, 6:7], ln_st[:, 1:2],
                  sem=sdv)
            pr.wait("scalar", sdv)
            pr.op("scalar", "activation", var, mean, AF.Square, sem=sac)
            pr.wait("vector", sac)
            pr.op("vector", "tensor_tensor", var, m2, var, op=ALU.subtract,
                  sem=sdv)
            pr.wait("scalar", sdv)
            pr.op("scalar", "activation", lnv, var, AF.Ln, bias=epsln[:, :])
            pr.op("scalar", "activation", rstd, lnv, AF.Exp, scale=-0.5,
                  sem=sac)
            pr.wait("vector", sac)
            pr.op("vector", "tensor_scalar", ln_bf[:, :], ln_x[:, :], mean,
                  rstd, op0=ALU.subtract, op1=ALU.mult, sem=sdv)
            pr.wait("tensor", sdv)
            for c in range(8):
                slot = (c % 4) * 128
                pr.wait("tensor", sac)
                pr.op("tensor", "transpose", PS_T[:, slot:slot + 128],
                      ln_bf[:, c * 128:(c + 1) * 128], C["ident_bf"][:, :],
                      sem=spe)
                pr.wait("scalar", spe)
                pr.op("scalar", "copy", xn_fm[c][:, t * 128:(t + 1) * 128],
                      PS_T[:, slot:slot + 128], sem=sac)
        barrier()
        for c in range(8):
            pr.dma("sync", xn_fm_hbm[c * 128:(c + 1) * 128, :],
                   xn_fm[c][:, :], sst)
        if tap("xn_fm0", (128, L), BF16) is not None:
            pr.dma("sync", tap_out["xn_fm0"][:, :], xn_fm[0][:, :], sst)
        free("ln")
        pr.mark("aa")

        def fm_chain(dst_chunks, wsb, kchunks, mtot, src_chunks=None,
                     mchunk=128):
            src = src_chunks or xn_fm
            nm = (mtot + mchunk - 1) // mchunk
            for ts in range(L // TSL):
                for mc in range(nm):
                    mw = min(mchunk, mtot - mc * mchunk)
                    ps = PS_B if (mc % 2 == 0) else PS_C
                    pr.wait("tensor", sdv)
                    for kk in range(kchunks):
                        pr.op("tensor", "matmul", ps[0:mw, 0:TSL],
                              wsb[:, kk, mc * mchunk:mc * mchunk + mw],
                              src[kk][:, ts * TSL:(ts + 1) * TSL],
                              start=(kk == 0), stop=(kk == kchunks - 1),
                              sem=(spe if kk == kchunks - 1 else None))
                    pr.wait("vector", spe)
                    pr.op("vector", "tensor_copy",
                          dst_chunks[mc][0:mw, ts * TSL:(ts + 1) * TSL],
                          ps[0:mw, 0:TSL], sem=sdv)

        def fm_chain_hbm(dst_hbm, stage, wsb, kchunks, mtot,
                         src_chunks=None):
            src = src_chunks or xn_fm
            nm = (mtot + 127) // 128
            for ts in range(L // TSL):
                for mc in range(nm):
                    mw = min(128, mtot - mc * 128)
                    pr.wait("tensor", sdv)
                    pr.wait("tensor", sst)
                    for kk in range(kchunks):
                        pr.op("tensor", "matmul", PS_B[0:mw, 0:TSL],
                              wsb[:, kk, mc * 128:mc * 128 + mw],
                              src[kk][:, ts * TSL:(ts + 1) * TSL],
                              start=(kk == 0), stop=(kk == kchunks - 1),
                              sem=(spe if kk == kchunks - 1 else None))
                    pr.wait("vector", spe)
                    pr.wait("vector", sst)
                    pr.op("vector", "tensor_copy", stage[0:mw, :],
                          PS_B[0:mw, 0:TSL], sem=sdv)
                    pr.wait("sync", sdv)
                    pr.dma("sync",
                           dst_hbm[mc * 128:mc * 128 + mw,
                                   ts * TSL:(ts + 1) * TSL],
                           stage[0:mw, :], sst)

        k_iT = rbuf("k_iT", (64, L), st="qk")
        q_iT = [rbuf(f"q_iT{c}", (64, L), st="qk") for c in range(4)]

        W1 = {}
        W1["idx_wq"] = rbuf("s_idx_wq", (128, 8, NIH * DH), st="w1")
        wload(W1["idx_wq"], idx_wq, 8)
        W1["idx_wk"] = rbuf("s_idx_wk", (128, 8, DH), st="w1")
        wload(W1["idx_wk"], idx_wk, 8)
        W1["w_kr"] = rbuf("s_w_kr", (128, 8, DR), st="w1")
        wload(W1["w_kr"], w_kr, 8)
        W1["cent"] = rbuf("s_cent", (128, 8, NE), st="w1")
        wload(W1["cent"], cent_T, 8)
        W1["ws_up"] = rbuf("s_ws_up", (128, 8, FF), st="w1")
        wload(W1["ws_up"], ws_up, 8)
        W1["ws_down"] = rbuf("s_ws_down", (128, 4, D), st="w1")
        wload(W1["ws_down"], ws_down, 4)
        pr.wait("tensor", sg)

        fm_chain([k_iT], W1["idx_wk"], 8, DH)
        fm_chain(q_iT, W1["idx_wq"], 8, NIH * DH, mchunk=64)
        if tap("k_iT", (64, L), BF16) is not None:
            pr.wait("sync", sdv)
            pr.dma("sync", tap_out["k_iT"][:, :], k_iT[:, :], sst)

        kr_f = rbuf("kr_f", (128, DR), F32, st="w1")
        kr_scr = rbuf("kr_scr", (128, 4 * DR), F32, st="w1")
        kr_bf = rbuf("kr_bf", (128, DR), st="w1")
        krl_st = rbuf("krl_st", (32, 128), st="w1")
        ang_t = rbuf("ang_t", (128, 3 * HALF), F32, st="w1")
        for t in range(NT):
            pr.wait("tensor", sdv)
            pr.wait("tensor", sst)
            for kk in range(8):
                pr.op("tensor", "matmul", PS_D[:, 0:DR],
                      xn_fm[kk][:, t * 128:(t + 1) * 128],
                      W1["w_kr"][:, kk, :], start=(kk == 0), stop=(kk == 7),
                      sem=(spe if kk == 7 else None))
            pr.wait("vector", spe)
            pr.op("vector", "tensor_copy", kr_f[:, :], PS_D[:, 0:DR])
            pr.op("vector", "tensor_copy", kr_bf[:, :], PS_D[:, 0:DR],
                  sem=sdv)
            pr.wait("sync", sdv)
            pr.dma("sync", ktab[t * 128:(t + 1) * 128, NH * DH:NH * DH + DR],
                   kr_bf[:, :], sst)
            sp_ = kr_scr[:, 0:DR]
            scr = kr_scr[:, DR:2 * DR]
            softplus(kr_f[:, :], sp_, scr)
            cc = ang_t[:, 0:HALF]
            ss = ang_t[:, HALF:2 * HALF]
            tmp = ang_t[:, 2 * HALF:3 * HALF]
            angle_combine(C["cos_k63"][:, t, :], C["sin_k63"][:, t, :],
                          cc, ss, tmp)
            mu1 = sp_[:, 0:HALF]
            mu2 = sp_[:, HALF:DR]
            t2 = kr_scr[:, 2 * DR:2 * DR + HALF]
            t3 = kr_scr[:, 2 * DR + HALF:3 * DR]
            pr.wait("vector", sst)
            pr.op("vector", "tensor_tensor", t2, mu1, cc, op=ALU.mult)
            pr.op("vector", "tensor_tensor", t3, mu2, ss, op=ALU.mult)
            pr.op("vector", "tensor_tensor", kr_bf[:, 0:HALF], t2, t3,
                  op=ALU.subtract)
            pr.op("vector", "tensor_tensor", t2, mu1, ss, op=ALU.mult)
            pr.op("vector", "tensor_tensor", t3, mu2, cc, op=ALU.mult)
            pr.op("vector", "tensor_tensor", kr_bf[:, HALF:DR], t2, t3,
                  op=ALU.add, sem=sdv)
            pr.wait("tensor", sdv)
            pr.wait("tensor", sac)
            pr.op("tensor", "transpose", PS_T[0:32, 0:128], kr_bf[:, :],
                  C["ident_bf"][:, :], sem=spe)
            pr.wait("scalar", spe)
            pr.wait("scalar", sst)
            pr.op("scalar", "copy", krl_st[:, :], PS_T[0:32, 0:128], sem=sac)
            pr.wait("sync", sac)
            pr.dma("sync", krl_hbm[:, t * 128:(t + 1) * 128], krl_st[:, :],
                   sst)

        sh_h = rbuf("sh_h", (128, 4, TSL), st="w1")
        sh_o = rbuf("sh_o", (128, TSL), F32, st="w1")
        for ts in range(L // TSL):
            for fc in range(4):
                pr.wait("tensor", sac)
                pr.wait("tensor", sst)
                for kk in range(8):
                    pr.op("tensor", "matmul", PS_B[:, 0:TSL],
                          W1["ws_up"][:, kk, fc * 128:(fc + 1) * 128],
                          xn_fm[kk][:, ts * TSL:(ts + 1) * TSL],
                          start=(kk == 0), stop=(kk == 7),
                          sem=(spe if kk == 7 else None))
                pr.wait("scalar", spe)
                pr.op("scalar", "activation", sh_h[:, fc, :], PS_B[:, 0:TSL],
                      AF.Silu, sem=sac)
            for dd in range(8):
                pr.wait("tensor", sac)
                pr.wait("tensor", sdv)
                for fc in range(4):
                    pr.op("tensor", "matmul", PS_C[:, 0:TSL],
                          W1["ws_down"][:, fc, dd * 128:(dd + 1) * 128],
                          sh_h[:, fc, :], start=(fc == 0), stop=(fc == 3),
                          sem=(spe if fc == 3 else None))
                pr.wait("vector", spe)
                pr.wait("vector", sst)
                pr.op("vector", "tensor_copy", sh_o[:, :], PS_C[:, 0:TSL],
                      sem=sdv)
                pr.wait("sync", sdv)
                pr.dma("sync", shared_hbm[dd * 128:(dd + 1) * 128,
                                          ts * TSL:(ts + 1) * TSL],
                       sh_o[:, :], sst)

        aff = rbuf("aff", (128, NE), F32, st="w1")
        rt_sc = rbuf("rt_sc", (128, NE + 8), F32, st="w1")
        top8v = rbuf("top8v", (128, 8), F32, st="w1")
        idx8u = rbuf("idx8u", (128, 8), U32, st="w1")
        idx8f = rbuf("idx8f", (128, 8), F32, st="w1")
        wsel = rbuf("wsel", (128, 8), F32, st="w1")
        eq3 = rbuf("eq3", (128, 8 * NE), F32, st="w1")
        cwt = rbuf("cwt", (128, NE), F32, st="w1")
        cw_fm = rbuf("cw_fm", (64, L), F32, st="w1")
        for t in range(NT):
            pr.wait("tensor", sdv)
            for kk in range(8):
                pr.op("tensor", "matmul", PS_D[:, 0:NE],
                      xn_fm[kk][:, t * 128:(t + 1) * 128],
                      W1["cent"][:, kk, :], start=(kk == 0), stop=(kk == 7),
                      sem=(spe if kk == 7 else None))
            pr.wait("vector", spe)
            pr.op("vector", "tensor_copy", aff[:, :], PS_D[:, 0:NE], sem=sdv)
            biased = rt_sc[:, 0:NE]
            pr.op("vector", "tensor_tensor", biased, aff[:, :],
                  ebias_rep[:, :], op=ALU.add)
            pr.op("vector", "max", out=top8v[:, :], in_=biased, sem=sdv)
            pr.wait("vector", sdv)
            pr.op("vector", "max_index", idx8u[:, :], top8v[:, :], biased,
                  sem=sdv)
            pr.wait("vector", sdv)
            pr.op("vector", "tensor_copy", idx8f[:, :], idx8u[:, :])
            eqv = eq3[:, :].rearrange("p (s e) -> p s e", e=NE)
            io3 = C["iota64"][:, :].unsqueeze(1).to_broadcast([128, 8, NE])
            id3 = idx8f[:, :].unsqueeze(2).to_broadcast([128, 8, NE])
            pr.op("vector", "tensor_tensor", eqv, io3, id3, op=ALU.is_equal)
            af3 = aff[:, :].unsqueeze(1).to_broadcast([128, 8, NE])
            pr.op("vector", "tensor_tensor", eqv, eqv, af3, op=ALU.mult)
            pr.op("vector", "tensor_reduce", wsel[:, :], eqv, axis=AX.X,
                  op=ALU.add, sem=sdv)
            pr.wait("scalar", sdv)
            pr.op("scalar", "activation", wsel[:, :], wsel[:, :], AF.Sigmoid,
                  sem=sac)
            pr.wait("vector", sac)
            wsum = rt_sc[:, NE:NE + 1]
            pr.op("vector", "tensor_reduce", wsum, wsel[:, :], axis=AX.X,
                  op=ALU.add)
            pr.op("vector", "tensor_scalar", wsum, wsum, 1e-8, None,
                  op0=ALU.add)
            pr.op("vector", "reciprocal", wsum, wsum)
            pr.op("vector", "tensor_scalar", wsel[:, :], wsel[:, :], wsum,
                  None, op0=ALU.mult)
            eq2 = eq3[:, :].rearrange("p (e s) -> p e s", s=8)
            io2 = C["iota64"][:, :].unsqueeze(2).to_broadcast([128, NE, 8])
            id2 = idx8f[:, :].unsqueeze(1).to_broadcast([128, NE, 8])
            pr.op("vector", "tensor_tensor", eq2, io2, id2, op=ALU.is_equal)
            w2 = wsel[:, :].unsqueeze(1).to_broadcast([128, NE, 8])
            pr.op("vector", "tensor_tensor", eq2, eq2, w2, op=ALU.mult)
            pr.op("vector", "tensor_reduce", cwt[:, :], eq2, axis=AX.X,
                  op=ALU.add, sem=sdv)
            if t == 2:
                for tn, tb_ in (("aff2", aff), ("idx8f2", idx8f),
                                ("wsel2", wsel), ("cwt2", cwt),
                                ("eq32", eq3)):
                    if tap(tn, (128, tb_.shape[-1])) is not None:
                        pr.wait("sync", sdv)
                        pr.dma("sync", tap_out[tn][:, :], tb_[:, :], sst)
            pr.wait("tensor", sdv)
            pr.op("tensor", "transpose", PS_D[0:64, 128:256], cwt[:, :],
                  C["ident_f32"][:, :], sem=spe)
            pr.wait("vector", spe)
            pr.op("vector", "tensor_copy", cw_fm[:, t * 128:(t + 1) * 128],
                  PS_D[0:64, 128:256], sem=sdv)
            pr.wait("tensor", sdv)
        cw_bfs = rbuf("cw_bfs", (64, L), BF16, st="w1")
        pr.op("vector", "tensor_copy", cw_bfs[:, :], cw_fm[:, :], sem=sdv)
        pr.wait("sync", sdv)
        pr.dma("sync", cw_hbm[:, :], cw_fm[:, :], sst)
        pr.dma("sync", cw_bf_hbm[:, :], cw_bfs[:, :], sst)
        if tap("cw_fm", (64, L)) is not None:
            pr.dma("sync", tap_out["cw_fm"][:, :], cw_fm[:, :], sst)
        free("w1")
        pr.mark("ab1")

        c_qT = [rbuf(f"c_qT{c}", (128, L), st="cq") for c in range(4)]
        c_kvT = [rbuf(f"c_kvT{c}", (128, L), st="ckv") for c in range(2)]
        W2 = {}
        W2["w_dkv"] = rbuf("s_w_dkv", (128, 8, DCKV), st="w2")
        wload(W2["w_dkv"], w_dkv, 8)
        W2["w_dq"] = rbuf("s_w_dq", (128, 8, DCQ), st="w2")
        wload(W2["w_dq"], w_dq, 8)
        pr.wait("tensor", sg)
        fm_chain(c_kvT, W2["w_dkv"], 8, DCKV)
        fm_chain(c_qT, W2["w_dq"], 8, DCQ)
        if tap("c_kvT0", (128, L), BF16) is not None:
            pr.wait("sync", sdv)
            pr.dma("sync", tap_out["c_kvT0"][:, :], c_kvT[0][:, :], sst)
        free("w2")
        pr.mark("ab2")

        W3 = {}
        W3["w_uk"] = rbuf("s_w_uk", (128, 2, NH * DH), st="w3")
        wload(W3["w_uk"], w_uk, 2)
        W3["w_uv"] = rbuf("s_w_uv", (128, 2, NH * DH), st="w3")
        wload(W3["w_uv"], w_uv, 2)
        kc_stage = rbuf("kc_stage", (128, TSL), st="w3")
        kv_st = rbuf("kv_st", (128, NH * DH), st="w3")
        pr.wait("tensor", sg)
        fm_chain_hbm(kcT_hbm, kc_stage, W3["w_uk"], 2, NH * DH,
                     src_chunks=c_kvT)
        for t in range(NT):
            for which in ("kc", "v"):
                wsb = W3["w_uk"] if which == "kc" else W3["w_uv"]
                pr.wait("tensor", sac)
                pr.wait("tensor", sst)
                for kk in range(2):
                    pr.op("tensor", "matmul", PS_B[:, 0:NH * DH],
                          c_kvT[kk][:, t * 128:(t + 1) * 128], wsb[:, kk, :],
                          start=(kk == 0), stop=(kk == 1),
                          sem=(spe if kk == 1 else None))
                pr.wait("scalar", spe)
                pr.wait("scalar", sst)
                pr.op("scalar", "copy", kv_st[:, :], PS_B[:, 0:NH * DH],
                      sem=sac)
                pr.wait("sync", sac)
                if which == "kc":
                    pr.dma("sync", ktab[t * 128:(t + 1) * 128, 0:NH * DH],
                           kv_st[:, :], sst)
                else:
                    pr.dma("sync", vtab[t * 128:(t + 1) * 128, :],
                           kv_st[:, :], sst)
        free("w3", "ckv")
        pr.mark("ac")

        W4 = {}
        W4["w_uq"] = rbuf("s_w_uq", (128, 4, NH * DH), st="w4")
        wload(W4["w_uq"], w_uq, 4)
        W4["w_qr"] = rbuf("s_w_qr", (128, 4, NH * DR), st="w4")
        wload(W4["w_qr"], w_qr, 4)
        qc_st = rbuf("qc_st", (128, NH * DH), st="w4")
        qrot_st = rbuf("qrot_st", (128, NH * DR), st="w4")
        qrl_st = rbuf("qrl_st", (128, NH * DR), st="w4")
        qp_f = rbuf("qp_f", (128, NH * DR), F32, st="w4")
        qp_sp = rbuf("qp_sp", (128, 2 * NH * DR), F32, st="w4")
        qang = rbuf("qang", (128, 3 * HALF), F32, st="w4")
        qrl_bf = rbuf("qrl_bf", (128, NH * DR), st="w4")
        pr.wait("tensor", sg)

        def pope_heads(src_f32, cc, ss, dst_bf, scr2):
            n32 = NH * DR
            sp_ = scr2[:, 0:n32]
            scr = scr2[:, n32:2 * n32]
            softplus(src_f32, sp_, scr)
            sp3 = sp_.rearrange("p (h two f) -> p h two f", two=2, f=HALF)
            mu1 = sp3[:, :, 0, :]
            mu2 = sp3[:, :, 1, :]
            d3 = dst_bf.rearrange("p (h two f) -> p h two f", two=2, f=HALF)
            cc3 = cc.unsqueeze(1).to_broadcast([128, NH, HALF])
            ss3 = ss.unsqueeze(1).to_broadcast([128, NH, HALF])
            t3 = scr.rearrange("p (h two f) -> p h two f", two=2, f=HALF)
            pr.op("vector", "tensor_tensor", t3[:, :, 0, :], mu1, cc3,
                  op=ALU.mult)
            pr.op("vector", "tensor_tensor", t3[:, :, 1, :], mu2, ss3,
                  op=ALU.mult)
            pr.op("vector", "tensor_tensor", d3[:, :, 0, :], t3[:, :, 0, :],
                  t3[:, :, 1, :], op=ALU.subtract)
            pr.op("vector", "tensor_tensor", t3[:, :, 0, :], mu1, ss3,
                  op=ALU.mult)
            pr.op("vector", "tensor_tensor", t3[:, :, 1, :], mu2, cc3,
                  op=ALU.mult)
            pr.op("vector", "tensor_tensor", d3[:, :, 1, :], t3[:, :, 0, :],
                  t3[:, :, 1, :], op=ALU.add, sem=sdv)

        for t in range(NT):
            pr.wait("tensor", sdv)
            pr.wait("tensor", sac)
            pr.wait("tensor", sst)
            for kk in range(4):
                pr.op("tensor", "matmul", PS_B[:, 0:NH * DH],
                      c_qT[kk][:, t * 128:(t + 1) * 128],
                      W4["w_uq"][:, kk, :], start=(kk == 0), stop=(kk == 3),
                      sem=(spe if kk == 3 else None))
            pr.wait("scalar", spe)
            pr.wait("scalar", sst)
            pr.op("scalar", "copy", qc_st[:, :], PS_B[:, 0:NH * DH], sem=sac)
            pr.wait("sync", sac)
            pr.dma("sync", qc_hbm[:, t * NH * DH:(t + 1) * NH * DH],
                   qc_st[:, :], sst)
            pr.wait("tensor", sac)
            for kk in range(4):
                pr.op("tensor", "matmul", PS_C[:, 0:NH * DR],
                      c_qT[kk][:, t * 128:(t + 1) * 128],
                      W4["w_qr"][:, kk, :], start=(kk == 0), stop=(kk == 3),
                      sem=(spe if kk == 3 else None))
            pr.wait("vector", spe)
            pr.op("vector", "tensor_copy", qp_f[:, :], PS_C[:, 0:NH * DR],
                  sem=sdv)
            cc = qang[:, 0:HALF]
            ss = qang[:, HALF:2 * HALF]
            tmp = qang[:, 2 * HALF:3 * HALF]
            pr.wait("vector", sst)
            angle_combine(C["cos_q"][:, t, :], C["sin_q"][:, t, :], cc, ss,
                          tmp)
            pope_heads(qp_f[:, :], cc, ss, qrot_st[:, :], qp_sp)
            pr.wait("sync", sdv)
            pr.dma("sync", qrot_hbm[:, t * NH * DR:(t + 1) * NH * DR],
                   qrot_st[:, :], sst)
            angle_combine(C["cos_2q"][:, t, :], C["sin_2q"][:, t, :], cc, ss,
                          tmp)
            pope_heads(qp_f[:, :], cc, ss, qrl_bf[:, :], qp_sp)
            pr.wait("tensor", sdv)
            pr.wait("tensor", sac)
            for c in range(2):
                pr.op("tensor", "transpose", PS_T[:, c * 128:(c + 1) * 128],
                      qrl_bf[:, c * 128:(c + 1) * 128], C["ident_bf"][:, :],
                      sem=(spe if c == 1 else None))
            pr.wait("scalar", spe)
            pr.wait("scalar", sst)
            for c in range(2):
                pr.op("scalar", "copy", qrl_st[:, c * 128:(c + 1) * 128],
                      PS_T[:, c * 128:(c + 1) * 128],
                      sem=(sac if c == 1 else None))
            pr.wait("sync", sac)
            for c in range(2):
                pr.dma("sync",
                       qrl_hbm[c * 128:(c + 1) * 128, t * 128:(t + 1) * 128],
                       qrl_st[:, c * 128:(c + 1) * 128], sst)
        free("w4", "cq", "xn")
        pr.mark("ad")

        colv = rbuf("colv_b1", (128, L), F32, st="b1x")
        pr.dma("sync", colv[:, :], d_in["t_colv"][:, :], sl)
        Ibuf = rbuf("Ibuf", (128, L), F32, st="b1x")
        relu_b = rbuf("relu_b", (128, 4, 512), F32, st="b1x")
        qm63 = rbuf("qm63", (128, 1), F32, st="b1x")
        i8v = rbuf("i8v", (128, 8), F32, st="b1x")
        i8u = rbuf("i8u", (128, 8), U32, st="b1x")
        pr.wait("vector", sl)

        for t in range(NT):
            q0 = t * 128
            Lt = 128 if t == 0 else q0 + 64
            if t == 0:
                pr.op("vector", "tensor_copy", Ibuf[:, 0:128], C["m0"][:, :],
                      sem=sdv)
            else:
                for cc_ in range(0, Lt, 512):
                    cw_ = min(512, Lt - cc_)
                    pen = Ibuf[:, cc_:cc_ + cw_]
                    pr.op("vector", "tensor_scalar", pen,
                          colv[:, cc_:cc_ + cw_], C["qm63"][:, t:t + 1], NEG,
                          op0=ALU.is_ge, op1=ALU.mult)
                    pr.op("vector", "scalar_tensor_tensor", pen,
                          colv[:, cc_:cc_ + cw_], -EPS_TIE, pen,
                          op0=ALU.mult, op1=ALU.add, sem=sdv)
            for cc_ in range(0, Lt, 512):
                cw_ = min(512, Lt - cc_)
                pr.wait("tensor", sac)
                pr.wait("tensor", sdv)
                for h in range(NIH):
                    pr.op("tensor", "matmul", PS_A[:, h * 512:h * 512 + cw_],
                          q_iT[h][:, q0:q0 + 128],
                          k_iT[0:64, cc_:cc_ + cw_], start=True, stop=True,
                          sem=(spe if h == NIH - 1 else None))
                pr.wait("scalar", spe)
                for h in range(NIH):
                    pr.op("scalar", "activation", relu_b[:, h, 0:cw_],
                          PS_A[:, h * 512:h * 512 + cw_], AF.Relu,
                          sem=(sac if h == NIH - 1 else None))
                pr.wait("vector", sac)
                for h in range(NIH):
                    pr.op("vector", "scalar_tensor_tensor",
                          Ibuf[:, cc_:cc_ + cw_], relu_b[:, h, 0:cw_],
                          idxw_rep[:, h:h + 1], Ibuf[:, cc_:cc_ + cw_],
                          op0=ALU.mult, op1=ALU.add,
                          sem=(sdv if h == NIH - 1 else None))
            pr.wait("vector", sdv)
            nrounds = 12 if t == 0 else 4
            for r in range(nrounds):
                pr.op("vector", "max", out=i8v[:, :], in_=Ibuf[:, 0:Lt],
                      sem=sdv)
                pr.wait("vector", sdv)
                pr.op("vector", "max_index", i8u[:, :], i8v[:, :],
                      Ibuf[:, 0:Lt], sem=sdv)
                pr.wait("vector", sdv)
                if t == 0:
                    pr.op("vector", "tensor_copy", idx0f[:, r * 8:r * 8 + 8],
                          i8u[:, :])
                else:
                    pr.op("vector", "tensor_copy",
                          idx32f[:, t, r * 8:r * 8 + 8], i8u[:, :])
                if r < nrounds - 1:
                    pr.op("vector", "match_replace", Ibuf[:, 0:Lt],
                          i8v[:, :], Ibuf[:, 0:Lt], imm_value=NEG, sem=sdv)
                    pr.wait("vector", sdv)
            pr.op("vector", "tensor_copy", i8v[:, :], i8v[:, :], sem=sdv)
        if tap("idx32f", (128, NT * 32)) is not None:
            pr.wait("sync", sdv)
            pr.dma("sync", tap_out["idx32f"][:, :],
                   idx32f[:, :, :].rearrange("p t s -> p (t s)"), sst)
        if tap("idx0f", (128, 96)) is not None:
            pr.wait("sync", sdv)
            pr.dma("sync", tap_out["idx0f"][:, :], idx0f[:, :], sst)
        wcast = [rbuf(f"wcast{i}", (128, 4, D), st="b1x") for i in range(2)]
        wc_done = [0, 0]
        for e in range(NE):
            par = e % 2
            pr.wait("gpsimd", sst, wc_done[par])
            for fc in range(4):
                pr.dma("gpsimd", wcast[par][:, fc, :],
                       w_down[e, fc * 128:(fc + 1) * 128, :], sg)
            pr.wait("sync", sg)
            for fc in range(4):
                pr.dma("sync", w_down_bf[e, fc * 128:(fc + 1) * 128, :],
                       wcast[par][:, fc, :], sst)
            wc_done[par] = sst.c
        free("b1x", "qk")
        pr.mark("b1")

        kbuf = rbuf("kbuf", (128, 32, NH * DH + DR), st="b2")
        vbuf = rbuf("vbuf", (128, 32, NH * DH), st="b2")
        idxu = [rbuf(f"idxu{i}", (128, 96), U32, st="b2") for i in range(2)]
        vsA = rbuf("vsA", (128, NH * DH), st="b2")
        vsB = rbuf("vsB", (64, NH * DH), st="b2")
        kcs = rbuf("kcs", (64, 8, 192), st="b2")
        krls = rbuf("krls", (32, 192), st="b2")
        qrls = rbuf("qrls", (32, 8, 128), st="b2")
        qct = rbuf("qct", (128, NH * DH), st="b2")
        qrott = rbuf("qrott", (128, NH * DR), st="b2")
        qcfm = rbuf("qcfm", (64, 8, 128), st="b2")
        scores = rbuf("scores", (128, NH, 224), F32, st="b2")
        prod = rbuf("prod", (128, 8192), st="b2")
        tree = rbuf("tree", (128, 4096), st="b2")
        krot = rbuf("krot", (128, 32, DR), st="b2")
        scsave = rbuf("scsave", (128, 32 * NH), F32, st="b2")
        smx = rbuf("smx", (128, 2 * NH), F32, st="b2")
        attn_bf = rbuf("attn_bf", (128, NH, 224), st="b2")
        attnT_A = rbuf("attnT_A", (128, NH, 128), st="b2")
        attnT_B = rbuf("attnT_B", (64, NH, 128), st="b2")
        nl_acc = rbuf("nl_acc", (128, NH * DH), F32, st="b2")
        mix_bf = rbuf("mix_bf", (128, NH * DH), st="b2")
        mixT = rbuf("mixT", (128, 4, 128), st="b2")
        po_x = rbuf("po_x", (128, 128), F32, st="b2")
        po_sh = rbuf("po_sh", (128, 128), F32, st="b2")
        po_o = rbuf("po_o", (128, 128), F32, st="b2")
        w_out_sb = rbuf("w_out_sb", (128, 4, D), st="b2")
        wload(w_out_sb, w_out, 4)

        c_s0 = rbuf("c_s0", (128, 96 * HALF), F32, st="s0")
        s_s0 = rbuf("s_s0", (128, 96 * HALF), F32, st="s0")
        pr.dma("sync", c_s0[:, :], d_in["t_cos_s0"][:, :], sl)
        pr.dma("sync", s_s0[:, :], d_in["t_sin_s0"][:, :], sl)
        pr.wait("vector", sl)
        ca_s0, sa_s0 = make_slot_tables(c_s0[:, :], s_s0[:, :], 96, "s0",
                                        st="s0", side="right")

        kg = kbuf[:, :, :]
        vg = vbuf[:, :, :]
        NS2 = 32

        def issue_k(idx_ap, base):
            for j in range(NS2):
                pr.gather(kg[:, j, :], ktab[:, :],
                          idx_ap[:, base + j:base + j + 1], sg)
            return sg.c

        def issue_v(idx_ap, base):
            for j in range(NS2):
                pr.gather(vg[:, j, :], vtab[:, :],
                          idx_ap[:, base + j:base + j + 1], sg)
            return sg.c

        def nl_scores32(t, ca_ap, sa_ap, dst_base, slot_base):
            ksp = tree[:, :].bitcast(F32)
            kr_v = kg[:, :, NH * DH:NH * DH + DR]
            n32 = NS2 * DR
            sp_ = ksp[:, 0:n32]
            scr = ksp[:, n32:2 * n32]
            softplus(kr_v, sp_, scr)
            sp3 = sp_.rearrange("p (j two f) -> p j two f", two=2, f=HALF)
            ca3 = ca_ap[:, :].rearrange("p (j f) -> p j f", f=HALF)[
                :, slot_base:slot_base + NS2, :]
            sa3 = sa_ap[:, :].rearrange("p (j f) -> p j f", f=HALF)[
                :, slot_base:slot_base + NS2, :]
            k3 = krot[:, :, :].rearrange("p j (two f) -> p j two f",
                                         two=2, f=HALF)
            t3 = scr.rearrange("p (j two f) -> p j two f", two=2, f=HALF)
            pr.op("vector", "tensor_tensor", t3[:, :, 0, :], sp3[:, :, 0, :],
                  ca3, op=ALU.mult)
            pr.op("vector", "tensor_tensor", t3[:, :, 1, :], sp3[:, :, 1, :],
                  sa3, op=ALU.mult)
            pr.op("vector", "tensor_tensor", k3[:, :, 0, :], t3[:, :, 0, :],
                  t3[:, :, 1, :], op=ALU.subtract)
            pr.op("vector", "tensor_tensor", t3[:, :, 0, :], sp3[:, :, 0, :],
                  sa3, op=ALU.mult)
            pr.op("vector", "tensor_tensor", t3[:, :, 1, :], sp3[:, :, 1, :],
                  ca3, op=ALU.mult)
            pr.op("vector", "tensor_tensor", k3[:, :, 1, :], t3[:, :, 0, :],
                  t3[:, :, 1, :], op=ALU.add)
            for ch in range(2):
                j0 = ch * 16
                pv = prod[:, 0:16 * 512].rearrange("p (j m) -> p j m", m=512)
                qcb = qct[:, :].unsqueeze(1).to_broadcast([128, 16, 512])
                pr.op("vector", "tensor_tensor", pv,
                      kg[:, j0:j0 + 16, 0:NH * DH], qcb, op=ALU.mult)
                src_ = prod[:, 0:16 * 512]
                width = DH
                tv = tree
                while width > 1:
                    half_ = width // 2
                    s4 = src_.rearrange("p (j h d) -> p j h d", h=NH,
                                        d=width)
                    d4 = tv[:, 0:16 * NH * half_].rearrange(
                        "p (j h d) -> p j h d", h=NH, d=half_)
                    pr.op("vector", "tensor_tensor", d4,
                          s4[:, :, :, 0:half_], s4[:, :, :, half_:width],
                          op=ALU.add)
                    src_ = tv[:, 0:16 * NH * half_]
                    tv = prod if tv is tree else tree
                    width = half_
                pr.op("vector", "tensor_copy",
                      scsave[:, ch * 128:(ch + 1) * 128], src_)
            pr_r = prod[:, 0:NS2 * NH * DR].rearrange(
                "p (j h f) -> p j h f", h=NH, f=DR)
            kro4 = krot[:, :, :].unsqueeze(2).to_broadcast(
                [128, NS2, NH, DR])
            qro4 = qrott[:, :].rearrange("p (h f) -> p h f",
                                         f=DR).unsqueeze(1).to_broadcast(
                [128, NS2, NH, DR])
            pr.op("vector", "tensor_tensor", pr_r, kro4, qro4, op=ALU.mult)
            src_ = prod[:, 0:NS2 * NH * DR]
            width = DR
            tv = tree
            while width > 1:
                half_ = width // 2
                s4 = src_.rearrange("p (j h f) -> p j h f", h=NH, f=width)
                d4 = tv[:, 0:NS2 * NH * half_].rearrange(
                    "p (j h f) -> p j h f", h=NH, f=half_)
                pr.op("vector", "tensor_tensor", d4, s4[:, :, :, 0:half_],
                      s4[:, :, :, half_:width], op=ALU.add)
                src_ = tv[:, 0:NS2 * NH * half_]
                tv = prod if tv is tree else tree
                width = half_
            scr_jh = src_.rearrange("p (j h) -> p j h", h=NH)
            scv_jh = scsave[:, :].rearrange("p (j h) -> p j h", h=NH)
            dsl = scores[:, :, dst_base:dst_base + NS2]
            dsl = dsl.rearrange("p h j -> p j h")
            pr.op("vector", "tensor_tensor", scv_jh, scv_jh, scr_jh,
                  op=ALU.add)
            pr.op("vector", "tensor_scalar", dsl, scv_jh, SCALE, None,
                  op0=ALU.mult, sem=sdv)

        def nl_values32(kbase, first):
            for ch in range(2):
                j0 = ch * 16
                a4 = attn_bf[:, :, kbase + j0:kbase + j0 + 16]
                a4 = a4.rearrange("p h j -> p j h").unsqueeze(3).to_broadcast(
                    [128, 16, NH, DH])
                v4 = vg[:, j0:j0 + 16, :].rearrange("p j (h d) -> p j h d",
                                                    d=DH)
                pv = prod[:, 0:16 * 512].rearrange("p (j h d) -> p j h d",
                                                   h=NH, d=DH)
                pr.op("vector", "tensor_tensor", pv, v4, a4, op=ALU.mult)
                src_ = prod[:, 0:16 * 512]
                width = 16
                tv = tree
                while width > 1:
                    half_ = width // 2
                    s3 = src_.rearrange("p (j m) -> p j m", m=512)
                    d3 = tv[:, 0:half_ * 512].rearrange(
                        "p (j m) -> p j m", m=512)
                    pr.op("vector", "tensor_tensor", d3, s3[:, 0:half_, :],
                          s3[:, half_:width, :], op=ALU.add)
                    src_ = tv[:, 0:half_ * 512]
                    tv = prod if tv is tree else tree
                    width = half_
                if first and ch == 0:
                    pr.op("vector", "tensor_copy", nl_acc[:, :], src_,
                          sem=sdv)
                else:
                    pr.op("vector", "tensor_tensor", nl_acc[:, :],
                          nl_acc[:, :], src_, op=ALU.add, sem=sdv)

        kv_done = {}

        def cast_idx(t):
            par = t % 2
            pr.wait("vector", sg, kv_done.get(("v_issued", t - 2), 0))
            if t == 0:
                pr.op("vector", "tensor_copy", idxu[par][:, 0:96],
                      idx0f[:, :], sem=sdv)
            else:
                pr.op("vector", "tensor_copy", idxu[par][:, 0:32],
                      idx32f[:, t, :], sem=sdv)
            kv_done[("idx", t)] = sdv.c

        def softmax_and_local(t, ncols):
            q0 = t * 128
            mx = smx[:, 0:NH]
            sm = smx[:, NH:2 * NH]
            scv = scores[:, :, 0:ncols]
            pr.op("vector", "tensor_reduce", mx, scv, axis=AX.X, op=ALU.max)
            mxb = mx.unsqueeze(2).to_broadcast([128, NH, ncols])
            pr.op("vector", "tensor_tensor", scv, scv, mxb, op=ALU.subtract,
                  sem=sdv)
            pr.wait("scalar", sdv)
            pr.op("scalar", "activation", scv, scv, AF.Exp, sem=sac)
            pr.wait("vector", sac)
            pr.op("vector", "tensor_reduce", sm, scv, axis=AX.X, op=ALU.add)
            pr.op("vector", "reciprocal", sm, sm)
            smb = sm.unsqueeze(2).to_broadcast([128, NH, ncols])
            pr.op("vector", "tensor_tensor", attn_bf[:, :, 0:ncols], scv,
                  smb, op=ALU.mult, sem=sdv)
            if t > 0:
                pr.wait("tensor", sdv)
                for h in range(NH):
                    slot = (h % 4) * 128
                    pr.wait("tensor", sac)
                    pr.op("tensor", "transpose", PS_T[:, slot:slot + 128],
                          attn_bf[:, h, 0:128], C["ident_bf"][:, :], sem=spe)
                    pr.wait("scalar", spe)
                    pr.op("scalar", "copy", attnT_A[:, h, :],
                          PS_T[:, slot:slot + 128], sem=sac)
                for h in range(NH):
                    slot = (h % 4) * 128
                    pr.wait("tensor", sac)
                    pr.op("tensor", "transpose", PS_T[0:64, slot:slot + 128],
                          attn_bf[:, h, 128:192], C["ident_bf"][:, :],
                          sem=spe)
                    pr.wait("scalar", spe)
                    pr.op("scalar", "copy", attnT_B[:, h, :],
                          PS_T[0:64, slot:slot + 128], sem=sac)
                pr.wait("tensor", sac)
                pr.wait("tensor", sdv)
                for h in range(NH):
                    pr.op("tensor", "matmul", PS_B[:, h * DH:(h + 1) * DH],
                          attnT_A[:, h, :], vsA[:, h * DH:(h + 1) * DH],
                          start=True, stop=False)
                    pr.op("tensor", "matmul", PS_B[:, h * DH:(h + 1) * DH],
                          attnT_B[:, h, :], vsB[:, h * DH:(h + 1) * DH],
                          start=False, stop=True,
                          sem=(spe if h == NH - 1 else None))

        def finish_tile(t):
            q0 = t * 128
            if t > 0:
                pr.wait("vector", spe)
                pr.op("vector", "tensor_tensor", mix_bf[:, :],
                      PS_B[:, 0:512], nl_acc[:, :], op=ALU.add, sem=sdv)
            else:
                pr.op("vector", "tensor_copy", mix_bf[:, :], nl_acc[:, :],
                      sem=sdv)
            pr.wait("tensor", sdv)
            pr.wait("tensor", sac)
            for c in range(4):
                pr.op("tensor", "transpose", PS_T[:, c * 128:(c + 1) * 128],
                      mix_bf[:, c * 128:(c + 1) * 128], C["ident_bf"][:, :],
                      sem=(spe if c == 3 else None))
            pr.wait("scalar", spe)
            for c in range(4):
                pr.op("scalar", "copy", mixT[:, c, :],
                      PS_T[:, c * 128:(c + 1) * 128],
                      sem=(sac if c == 3 else None))
            pr.wait("tensor", sac)
            for dd in range(8):
                pr.wait("sync", sdv)
                pr.dma("sync", po_x[:, :],
                       x_fm[dd * 128:(dd + 1) * 128, q0:q0 + 128], sl)
                pr.dma("sync", po_sh[:, :],
                       shared_hbm[dd * 128:(dd + 1) * 128, q0:q0 + 128], sl)
                pr.wait("tensor", sdv)
                for c in range(4):
                    pr.op("tensor", "matmul", PS_C[:, 0:128],
                          w_out_sb[:, c, dd * 128:(dd + 1) * 128],
                          mixT[:, c, :], start=(c == 0), stop=(c == 3),
                          sem=(spe if c == 3 else None))
                pr.wait("vector", spe)
                pr.wait("vector", sl)
                pr.wait("vector", sst)
                pr.op("vector", "tensor_tensor", po_o[:, :], PS_C[:, 0:128],
                      po_x[:, :], op=ALU.add)
                pr.op("vector", "tensor_tensor", po_o[:, :], po_o[:, :],
                      po_sh[:, :], op=ALU.add, sem=sdv)
                pr.wait("sync", sdv)
                pr.dma("sync", partial_hbm[dd * 128:(dd + 1) * 128,
                                           q0:q0 + 128], po_o[:, :], sst)

        def load_strips(t):
            q0 = t * 128
            pr.wait("sync", spe)
            pr.wait("sync", sdv)
            pr.dma("sync", vsA[:, :], vtab[q0 - 64:q0 + 64, :], sl)
            pr.dma("sync", vsB[:, :], vtab[q0 + 64:q0 + 128, :], sl)
            pr.dma("sync", kcs[:, :, :],
                   kcT_hbm[:, :].rearrange("(h p) l -> p h l", p=64)[
                       :, :, q0 - 64:q0 + 128], sl)
            pr.dma("sync", krls[:, :], krl_hbm[:, q0 - 64:q0 + 128], sl)
            pr.dma("sync", qrls[:, :, :],
                   qrl_hbm[:, :].rearrange("(h p) l -> p h l", p=32)[
                       :, :, q0:q0 + 128], sl)
            pr.dma("sync", qct[:, :],
                   qc_hbm[:, t * NH * DH:(t + 1) * NH * DH], sl)
            pr.dma("sync", qrott[:, :],
                   qrot_hbm[:, t * NH * DR:(t + 1) * NH * DR], sl)

        def local_scores(t):
            pr.wait("tensor", sac)
            pr.wait("tensor", sl)
            for c in range(4):
                pr.op("tensor", "transpose", PS_T[:, c * 128:(c + 1) * 128],
                      qct[:, c * 128:(c + 1) * 128], C["ident_bf"][:, :],
                      sem=(spe if c == 3 else None))
            pr.wait("scalar", spe)
            for c in range(4):
                pr.op("scalar", "copy", qcfm[:, 2 * c, :],
                      PS_T[0:64, c * 128:(c + 1) * 128])
                pr.op("scalar", "copy", qcfm[:, 2 * c + 1, :],
                      PS_T[64:128, c * 128:(c + 1) * 128],
                      sem=(sac if c == 3 else None))
            pr.wait("tensor", sac)
            for h in range(NH):
                pcol = (h % 2) * 224
                pr.wait("tensor", sdv)
                pr.op("tensor", "matmul", PS_D[:, pcol:pcol + 192],
                      qcfm[:, h, :], kcs[:, h, :], start=True, stop=False)
                pr.op("tensor", "matmul", PS_D[:, pcol:pcol + 192],
                      qrls[:, h, :], krls[:, :], start=False, stop=True,
                      sem=spe)
                pr.wait("vector", spe)
                pr.op("vector", "scalar_tensor_tensor",
                      scores[:, h, 0:192], PS_D[:, pcol:pcol + 192],
                      SCALE, C["band"][:, :], op0=ALU.mult, op1=ALU.add,
                      sem=sdv)

        cast_idx(0)
        pr.wait("gpsimd", sdv)
        pr.dma("sync", qct[:, :], qc_hbm[:, 0:NH * DH], sl)
        pr.dma("sync", qrott[:, :], qrot_hbm[:, 0:NH * DR], sl)
        pr.wait("vector", sl)
        pr.wait("scalar", sl)
        for sw in range(3):
            pr.wait("gpsimd", sdv)
            issue_k(idxu[0], sw * 32)
            pr.wait("scalar", sg)
            pr.wait("vector", sg)
            nl_scores32(0, ca_s0, sa_s0, sw * 32, sw * 32)
        softmax_and_local(0, 96)
        for sw in range(3):
            pr.wait("gpsimd", sdv)
            issue_v(idxu[0], sw * 32)
            pr.wait("vector", sg)
            nl_values32(sw * 32, first=(sw == 0))
        finish_tile(0)
        free("s0")

        cast_idx(1)
        pr.wait("gpsimd", sdv)
        kv_done[("k", 1)] = issue_k(idxu[1], 0)
        kv_done[("v", 1)] = issue_v(idxu[1], 0)
        kv_done[("v_issued", 1)] = sg.c
        for t in range(1, NT):
            par = t % 2
            load_strips(t)
            local_scores(t)
            pr.wait("scalar", sg, kv_done[("k", t)])
            pr.wait("vector", sg, kv_done[("k", t)])
            nl_scores32(t, ca_nl, sa_nl, 192, 0)
            if t + 1 < NT:
                cast_idx(t + 1)
                pr.wait("gpsimd", sdv)
                kv_done[("k", t + 1)] = issue_k(idxu[(t + 1) % 2], 0)
            softmax_and_local(t, 224)
            pr.wait("vector", sg, kv_done[("v", t)])
            nl_values32(192, first=True)
            if t + 1 < NT:
                pr.wait("gpsimd", sdv)
                kv_done[("v", t + 1)] = issue_v(idxu[(t + 1) % 2], 0)
                kv_done[("v_issued", t + 1)] = sg.c
            finish_tile(t)
        if tap("partial", (D, L)) is not None:
            pr.wait("sync", sst)
            pr.dma("sync", tap_out["partial"][:, :], partial_hbm[:, :], sst)
        free("b2")
        pr.mark("b2")

        xnh = [rbuf(f"xnh{c}", (128, 2048), st="f") for c in range(8)]
        wup_b = [rbuf(f"wup{b_}", (128, 8, FF), st="f") for b_ in range(2)]
        cwrow = [rbuf(f"cwrow{b_}", (1, 2048), BF16, st="f")
                 for b_ in range(2)]
        cwrep = [rbuf(f"cwrep{b_}", (128, 2048), BF16, st="f")
                 for b_ in range(2)]
        hsil = [rbuf(f"hsil{b_}", (128, 1024), st="f") for b_ in range(2)]
        hscl = [rbuf(f"hscl{b_}", (128, 1024), st="f") for b_ in range(2)]
        ones_bf = rbuf("ones_bf", (1, 128), BF16, st="f")
        pr.wait("vector", sl)
        pr.op("vector", "memset", ones_bf[:, :], 1.0, sem=sdv)
        wup_done = [0, 0]
        cwrow_done = [0, 0]
        cwrep_done = [0, 0]
        pair_sac = [0, 0]
        hbuf_done = [0, 0]
        hdma_done = [0, 0]
        for th in range(2):
            tb = th * 2048
            pr.wait("sync", spe)
            for c in range(8):
                pr.dma("sync", xnh[c][:, :],
                       xn_fm_hbm[c * 128:(c + 1) * 128, tb:tb + 2048], sl)
            for e in range(NE):
                par = e % 2
                bb = wup_b[par]
                pr.wait("gpsimd", spe, wup_done[par])
                for kk in range(8):
                    pr.dma("gpsimd", bb[:, kk, :],
                           w_up[e, kk * 128:(kk + 1) * 128, :], sg)
                pr.wait("sync", spe, cwrow_done[par])
                pr.dma("sync", cwrow[par][:, :],
                       cw_bf_hbm[e:e + 1, tb:tb + 2048], sl)
                pr.wait("tensor", sl)
                pr.wait("tensor", sdv, cwrep_done[par])
                rep_mm = []
                for j in range(4):
                    ps = PS_B if j % 2 == 0 else PS_C
                    pr.wait("tensor", sdv)
                    pr.op("tensor", "matmul", ps[:, 0:512],
                          ones_bf[:, :],
                          cwrow[par][:, j * 512:(j + 1) * 512],
                          start=True, stop=True, sem=spe)
                    pr.wait("vector", spe)
                    pr.op("vector", "tensor_copy",
                          cwrep[par][:, j * 512:(j + 1) * 512],
                          ps[:, 0:512], sem=sdv)
                cwrow_done[par] = spe.c
                cw_ready = sdv.c
                pr.wait("tensor", sg)
                for fc in range(4):
                    for u in range(2):
                        pp = (fc * 2 + u) % 2
                        pcol = pp * 1024
                        pr.wait("tensor", sac, pair_sac[pp])
                        for kk in range(8):
                            for s2 in range(2):
                                pr.op("tensor", "matmul",
                                      PS_A[:, pcol + s2 * 512:
                                           pcol + s2 * 512 + 512],
                                      bb[:, kk, fc * 128:(fc + 1) * 128],
                                      xnh[kk][:, u * 1024 + s2 * 512:
                                              u * 1024 + s2 * 512 + 512],
                                      start=(kk == 0), stop=(kk == 7),
                                      sem=(spe if (kk == 7 and s2 == 1)
                                           else None))
                        pr.wait("scalar", spe)
                        pr.wait("scalar", sdv, hbuf_done[pp])
                        pr.op("scalar", "activation", hsil[pp][:, :],
                              PS_A[:, pcol:pcol + 1024], AF.Silu, sem=sac)
                        pair_sac[pp] = sac.c
                        pr.wait("vector", sac)
                        pr.wait("vector", sdv, cw_ready)
                        pr.wait("vector", sst, hdma_done[pp])
                        pr.op("vector", "tensor_tensor", hscl[pp][:, :],
                              hsil[pp][:, :],
                              cwrep[par][:, u * 1024:(u + 1) * 1024],
                              op=ALU.mult, sem=sdv)
                        hbuf_done[pp] = sdv.c
                        pr.wait("sync", sdv)
                        Tb0 = th * 4 + u * 2
                        pr.dma("sync",
                               h_hbm[e, Tb0:Tb0 + 2, :,
                                     fc * 512:(fc + 1) * 512].rearrange(
                                   "tb p x -> p tb x"),
                               hscl[pp][:, :].rearrange(
                                   "p (tb x) -> p tb x", x=512), sst)
                        hdma_done[pp] = sst.c
                wup_done[par] = spe.c
                cwrep_done[par] = sdv.c

        free("ps_main")
        PS_F = stacks.setdefault("ps_f", ctx.enter_context(ExitStack())) \
            .enter_context(nc.psum_tensor("PS_F", [128, 4096], F32))
        if tap("wdall", (NE * FF, D), BF16) is not None:
            tbuf = rbuf("tbuf_dbg", (128, 4, D), st="f")
            for e in range(NE):
                pr.wait("sync", sst)
                pr.dma("sync", tbuf[:, :, :],
                       w_down_bf[e, :, :].rearrange("(c p) d -> p c d",
                                                    p=128), sl)
                pr.wait("sync", sl)
                for fc in range(4):
                    pr.dma("sync",
                           tap_out["wdall"][e * FF + fc * 128:
                                            e * FF + (fc + 1) * 128, :],
                           tbuf[:, fc, :], sst)
        if tap("hball", (NE * 128, 2048), BF16) is not None:
            tbuf2 = rbuf("tbuf2_dbg", (128, 2048), st="f")
            for e in range(NE):
                pr.wait("sync", sst)
                pr.dma("sync", tbuf2[:, :], h_hbm[e, 0, :, :], sl)
                pr.wait("sync", sl)
                pr.dma("sync", tap_out["hball"][e * 128:(e + 1) * 128, :],
                       tbuf2[:, :], sst)
        drain_hbm = dscratch("drain_hbm", (128, 2048), BF16)
        settle_sb = rbuf("settle_sb", (128, 4, D), st="f")
        settle_flat = settle_sb[:, :, :].rearrange("p c d -> p (c d)")
        for e in range(12):
            pr.wait("sync", sst)
            pr.dma("sync", settle_sb[:, :, :],
                   w_down_bf[e, :, :].rearrange("(c p) d -> p c d", p=128),
                   sl)
            pr.wait("sync", sl)
            pr.dma("sync", drain_hbm[:, :], settle_flat[:, 0:2048], sst)
        for e in range(12):
            pr.wait("sync", sst)
            pr.dma("sync", settle_flat[:, 0:2048], h_hbm[e, 0, :, :], sl)
            pr.wait("sync", sl)
            pr.dma("sync", drain_hbm[:, :], settle_flat[:, 0:2048], sst)
        pr.wait("sync", sst)
        hrd = [rbuf(f"hrd{b_}", (128, 2048), st="f") for b_ in range(2)]
        wdn_b = [rbuf(f"wdn{b_}", (128, 4, D), st="f") for b_ in range(2)]
        po_p = rbuf("po_p", (128, 512), F32, st="f")
        y_sb = rbuf("y_sb", (128, 512), F32, st="f")
        buf_done = [0, 0]
        for Tb in range(8):
            tb = Tb * 512
            pr.wait("tensor", sdv)
            for e in range(NE):
                par = e % 2
                hb = hrd[par]
                wb = wdn_b[par]
                pr.wait("sync", spe, buf_done[par])
                pr.dma("sync", hb[:, :], h_hbm[e, Tb, :, :], sl)
                pr.dma("sync", wb[:, :, :],
                       w_down_bf[e, :, :].rearrange("(c p) d -> p c d",
                                                    p=128), sl)
                if Tb == 0 and e == 3:
                    if tap("wb3", (128, 4 * D), BF16) is not None:
                        pr.wait("sync", sl)
                        pr.dma("sync", tap_out["wb3"][:, :],
                               wb[:, :, :].rearrange("p c d -> p (c d)"), sst)
                    if tap("hb3", (128, 2048), BF16) is not None:
                        pr.wait("sync", sl)
                        pr.dma("sync", tap_out["hb3"][:, :], hb[:, :], sst)
                    pr.wait("tensor", sst)
                pr.wait("tensor", sl)
                for dd in range(8):
                    for fc in range(4):
                        pr.op("tensor", "matmul",
                              PS_F[:, dd * 512:(dd + 1) * 512],
                              wb[:, fc, dd * 128:(dd + 1) * 128],
                              hb[:, fc * 512:(fc + 1) * 512],
                              start=(e == 0 and fc == 0),
                              stop=(e == NE - 1 and fc == 3),
                              sem=(spe if (fc == 3 and dd == 7) else None))
                buf_done[par] = spe.c
            pr.wait("vector", spe)
            for dd in range(8):
                pr.wait("sync", sdv)
                pr.dma("sync", po_p[:, :],
                       partial_hbm[dd * 128:(dd + 1) * 128, tb:tb + 512],
                       sl)
                pr.wait("vector", sl)
                pr.wait("vector", sst)
                pr.op("vector", "tensor_tensor", y_sb[:, :],
                      PS_F[:, dd * 512:(dd + 1) * 512], po_p[:, :],
                      op=ALU.add, sem=sdv)
                pr.wait("sync", sdv)
                pr.dma("sync", y_fm[dd * 128:(dd + 1) * 128, tb:tb + 512],
                       y_sb[:, :], sst)
        pr.wait("sync", sst)
        pr.mark("f")
        pr.truncate(limit)

        block = ctx.enter_context(nc.Block())

        @block.sync
        def _(q):
            for fn in pr.ops["sync"]:
                fn(q)

        @block.gpsimd
        def _(q):
            for fn in pr.ops["gpsimd"]:
                fn(q)

        @block.scalar
        def _(q):
            for fn in pr.ops["scalar"]:
                fn(q)

        @block.vector
        def _(q):
            for fn in pr.ops["vector"]:
                fn(q)

        @block.tensor
        def _(q):
            for fn in pr.ops["tensor"]:
                fn(q)

    return nc, list(tap_out.keys())


def kernel(**inputs):
    HT = host_tables()
    nc, tap_names = build_program(taps=TAPS)
    A = {k: np.asarray(v) for k, v in inputs.items()}
    base = {
        "W_dkv": A["L0_attn_W_dkv"], "W_uk": A["L0_attn_W_uk"],
        "W_uv": A["L0_attn_W_uv"], "W_dq": A["L0_attn_W_dq"],
        "W_uq": A["L0_attn_W_uq"], "W_qr": A["L0_attn_W_qr"],
        "W_kr": A["L0_attn_W_kr"], "W_out": A["L0_attn_W_out"],
        "idx_wq": A["L0_attn_idx_wq"], "idx_wk": A["L0_attn_idx_wk"],
        "idx_w": np.ascontiguousarray(A["L0_attn_idx_w"].reshape(1, NIH)),
        "raw_delta": np.ascontiguousarray(
            A["L0_attn_raw_delta"].reshape(1, HALF)),
        "cent_T": np.ascontiguousarray(A["L0_ffn_centroids"].T),
        "exp_bias": np.ascontiguousarray(
            A["L0_ffn_expert_biases"].reshape(1, NE)),
        "Ws_up": A["L0_ffn_Ws_up"], "Ws_down": A["L0_ffn_Ws_down"],
        "W_up": A["L0_ffn_W_up"], "W_down": A["L0_ffn_W_down"],
    }
    for nm, arr in HT.items():
        base["t_" + nm] = arr
    in_maps = []
    for c in range(B):
        m = dict(base)
        xc = np.ascontiguousarray(A["L0_attn_x"][c])
        m["x_tok"] = xc
        m["x_fm"] = np.ascontiguousarray(xc.T)
        in_maps.append(m)
    res = run_bass_kernel_spmd(nc, in_maps, core_ids=list(range(B)))
    outs = []
    taps = []
    for c in range(B):
        r = res.results[c]
        outs.append(np.ascontiguousarray(np.asarray(r["y_fm"]).T))
        taps.append({n: np.asarray(r["tap_" + n]) for n in tap_names})
    y = np.stack(outs, axis=0).astype(np.float32)
    if tap_names:
        kernel.last_taps = taps
    return y


# version 13
# speedup vs baseline: 1.5687x; 1.0160x over previous
import math
import numpy as np
import ml_dtypes
import concourse.bass as bass
import concourse.mybir as mybir
from concourse.bass import IndirectOffsetOnAxis
from concourse.bass_utils import run_bass_kernel_spmd
from contextlib import ExitStack

F32 = mybir.dt.float32
BF16 = mybir.dt.bfloat16
U32 = mybir.dt.uint32
AF = mybir.ActivationFunctionType
ALU = mybir.AluOpType
AX = mybir.AxisListType
bfdt = ml_dtypes.bfloat16

B, L, D = 8, 4096, 1024
DCKV, DCQ = 256, 512
NH, DH, DR, HALF = 8, 64, 32, 16
NIH = 4
NE, FF = 64, 512
SCALE = (DH + DR) ** -0.5
NT = L // 128
EPS_TIE = 1e-10
NEG = -1.0e30
NSL = 16
TSL = 512

TAPS = []

ENGINES = ("sync", "gpsimd", "scalar", "vector", "tensor")

_HOST_CACHE = {}


def host_tables():
    if _HOST_CACHE:
        return _HOST_CACHE
    t = {}
    f32 = np.float32
    theta = (1.0 / 10000.0 ** (2.0 * np.arange(HALF) / DR)).astype(np.float64)
    k = np.arange(L, dtype=np.float64)[:, None]

    def tile_view(a):
        return np.ascontiguousarray(
            a.reshape(NT, 128, HALF).transpose(1, 0, 2)).astype(f32)

    t["cos_q"] = tile_view(np.cos(k * theta))
    t["sin_q"] = tile_view(np.sin(k * theta))
    t["cos_2q"] = tile_view(np.cos(2.0 * k * theta))
    t["sin_2q"] = tile_view(np.sin(2.0 * k * theta))
    t["cos_k63"] = tile_view(np.cos((k + 63.0) * theta))
    t["sin_k63"] = tile_view(np.sin((k + 63.0) * theta))
    j = np.arange(64, 96, dtype=np.float64)[:, None]
    t["cos_snl"] = np.broadcast_to(np.cos(j * theta).reshape(1, -1),
                                   (128, 32 * HALF)).astype(f32).copy()
    t["sin_snl"] = np.broadcast_to(np.sin(j * theta).reshape(1, -1),
                                   (128, 32 * HALF)).astype(f32).copy()
    j0 = np.arange(0, 96, dtype=np.float64)[:, None]
    t["cos_s0"] = np.broadcast_to(np.cos(j0 * theta).reshape(1, -1),
                                  (128, 96 * HALF)).astype(f32).copy()
    t["sin_s0"] = np.broadcast_to(np.sin(j0 * theta).reshape(1, -1),
                                  (128, 96 * HALF)).astype(f32).copy()
    t["ident_bf"] = np.eye(128).astype(bfdt)
    t["ident_f32"] = np.eye(128).astype(f32)
    t["colv"] = np.broadcast_to(np.arange(L, dtype=f32).reshape(1, L),
                                (128, L)).copy()
    p = np.arange(128)[:, None]
    s = np.arange(192)[None, :]
    t["band"] = np.where((s >= p + 1) & (s <= p + 64), 0.0, NEG).astype(f32)
    c0 = np.arange(128)[None, :].astype(np.float64)
    q0 = np.arange(128)[:, None].astype(np.float64)
    loc = (c0 >= q0 - 63) & (c0 <= q0)
    fut = c0 > q0
    t["m0"] = np.where(loc, 2e9 - 1e3 * c0,
                       np.where(fut, -2e9 - 1e3 * c0,
                                -EPS_TIE * c0)).astype(f32)
    t["iota64"] = np.broadcast_to(np.arange(NE, dtype=f32).reshape(1, NE),
                                  (128, NE)).copy()
    t["pvec"] = np.arange(128, dtype=f32).reshape(128, 1).copy()
    t["qm63"] = (np.arange(NT, dtype=f32)[None, :] * 128.0 - 63.0
                 + np.arange(128, dtype=f32)[:, None]).copy()
    t["ones_row_f"] = np.ones((1, 128), f32)
    _HOST_CACHE.update(t)
    return t


class Sem:
    def __init__(self, handle, name):
        self.h = handle
        self.c = 0
        self.name = name


class Prog:

    def __init__(self):
        self.ops = {e: [] for e in ENGINES}
        self.n = 0
        self.marks = {}

    def mark(self, name):
        self.marks[name] = {e: len(self.ops[e]) for e in ENGINES}

    def truncate(self, name):
        if name is None or name not in self.marks:
            return
        for e in ENGINES:
            del self.ops[e][self.marks[name][e]:]

    def emit(self, eng, fn):
        self.ops[eng].append(fn)
        self.n += 1

    def wait(self, eng, sem, val=None):
        v = sem.c if val is None else val
        if v > 0:
            self.emit(eng, lambda q, s=sem.h, v=v: q.wait_ge(s, v))

    def dma(self, eng, out, in_, sem, **kw):
        sem.c += 16
        self.emit(eng, lambda q, o=out, i=in_, s=sem.h, kw=kw:
                  q.dma_start(out=o, in_=i, **kw).then_inc(s, 16))
        return sem.c

    DRAIN_AFTER = ("tensor_reduce", "reciprocal")
    NO_RT = ("max", "max_index", "match_replace", "drain", "engine_nop")
    rt_sem = None
    paranoid = True

    def op(self, eng, meth, *args, sem=None, **kw):
        need_drain = (eng == "vector" and meth in self.DRAIN_AFTER)
        need_rt = False
        if (eng in ("vector", "scalar") and meth not in self.NO_RT
                and not need_drain and args):
            if self.paranoid:
                need_rt = True
            else:
                try:
                    out0 = args[0] if not ("out" in kw) else kw["out"]
                    if hasattr(out0, "free_size") and out0.free_size <= 64:
                        need_rt = True
                except Exception:
                    pass
        if sem is not None:
            sem.c += 1
        use = sem if sem is not None else (self.rt_sem if need_rt else None)
        if sem is None and need_rt:
            self.rt_sem.c += 1

        def fn(q, m=meth, a=args, kw=kw, s=(None if need_drain else use)):
            r = getattr(q, m)(*a, **kw)
            if s is not None:
                r.then_inc(s.h, 1)
            return r
        self.emit(eng, fn)
        if need_drain:
            def fd(q, s=sem):
                r = q.drain()
                if s is not None:
                    r.then_inc(s.h, 1)
                return r
            self.emit(eng, fd)
        if need_rt and not need_drain:
            self.wait(eng, use)
        return None if sem is None else sem.c

    def gather(self, out_ap, table_ap, idx_col_ap, sem):
        sem.c += 16
        self.emit("gpsimd", lambda q, o=out_ap, t=table_ap, i=idx_col_ap,
                  s=sem.h: q.indirect_dma_start(
                      out=o, out_offset=None, in_=t,
                      in_offset=IndirectOffsetOnAxis(ap=i, axis=0)
                  ).then_inc(s, 16))
        return sem.c


def build_program(taps=(), limit=None):
    nc = bass.Bass("TRN2", target_bir_lowering=False)
    HT = host_tables()
    pr = Prog()

    d_in = {}

    def din(name, shape, dt=F32):
        d_in[name] = nc.dram_tensor(name, list(shape), dt,
                                    kind="ExternalInput").ap()
        return d_in[name]

    x_tok = din("x_tok", (L, D))
    x_fm = din("x_fm", (D, L))
    w_dkv = din("W_dkv", (D, DCKV))
    w_uk = din("W_uk", (DCKV, NH * DH))
    w_uv = din("W_uv", (DCKV, NH * DH))
    w_dq = din("W_dq", (D, DCQ))
    w_uq = din("W_uq", (DCQ, NH * DH))
    w_qr = din("W_qr", (DCQ, NH * DR))
    w_kr = din("W_kr", (D, DR))
    w_out = din("W_out", (NH * DH, D))
    idx_wq = din("idx_wq", (D, NIH * DH))
    idx_wk = din("idx_wk", (D, DH))
    idx_w = din("idx_w", (1, NIH))
    raw_delta = din("raw_delta", (1, HALF))
    cent_T = din("cent_T", (D, NE))
    exp_bias = din("exp_bias", (1, NE))
    ws_up = din("Ws_up", (D, FF))
    ws_down = din("Ws_down", (FF, D))
    w_up = din("W_up", (NE, D, FF))
    w_down = din("W_down", (NE, FF, D))
    for nm, arr in HT.items():
        din("t_" + nm, arr.shape, BF16 if arr.dtype == bfdt else F32)

    y_fm = nc.dram_tensor("y_fm", [D, L], F32, kind="ExternalOutput").ap()
    tap_out = {}

    def tap(name, shape, dt=F32):
        if name in taps:
            tap_out[name] = nc.dram_tensor("tap_" + name, list(shape), dt,
                                           kind="ExternalOutput").ap()
            return tap_out[name]
        return None

    def dscratch(name, shape, dt):
        return nc.dram_tensor(name, list(shape), dt, kind="Internal").ap()

    xn_fm_hbm = dscratch("xn_fm_hbm", (D, L), BF16)
    ktab = dscratch("ktab", (L, NH * DH + DR), BF16)
    vtab = dscratch("vtab", (L, NH * DH), BF16)
    shared_hbm = dscratch("shared_hbm", (D, L), F32)
    partial_hbm = dscratch("partial_hbm", (D, L), F32)
    cw_hbm = dscratch("cw_hbm", (NE, L), F32)
    h_hbm = dscratch("h_hbm", (NE, 8, 128, 2048), BF16)
    w_down_bf = dscratch("w_down_bf", (NE, FF, D), BF16)
    cw_bf_hbm = dscratch("cw_bf_hbm", (NE, L), BF16)
    kcT_hbm = dscratch("kcT_hbm", (NH * DH, L), BF16)
    krl_hbm = dscratch("krl_hbm", (DR, L), BF16)
    qrl_hbm = dscratch("qrl_hbm", (NH * DR, L), BF16)
    qc_hbm = dscratch("qc_hbm", (128, NT * NH * DH), BF16)
    qrot_hbm = dscratch("qrot_hbm", (128, NT * NH * DR), BF16)

    with ExitStack() as ctx:
        stacks = {}

        def sbuf(name, shape, dt=BF16, st="persist", side="left"):
            if st not in stacks:
                stacks[st] = ctx.enter_context(ExitStack())
            return stacks[st].enter_context(
                nc.sbuf_tensor(name, list(shape), dt, side=side))

        def rbuf(name, shape, dt=BF16, st=None):
            return sbuf(name, shape, dt, st=st, side="right")

        def sem(name):
            return Sem(ctx.enter_context(nc.semaphore(name)), name)

        sl = sem("sl")
        sg = sem("sg")
        sst = sem("sst")
        spe = sem("spe")
        sdv = sem("sdv")
        sac = sem("sac")
        srt = sem("srt")
        sep = sem("sep")
        sarr = sem("sarr")
        pr.rt_sem = srt
        ALLSEMS = (sl, sg, sst, spe, sdv, sac, srt)
        arrs = sbuf("arrs", (128, 16), F32, st="persist")

        def barrier(reset=True):
            for e in ENGINES:
                for s in ALLSEMS:
                    pr.wait(e, s)

        def free(*names):
            barrier()
            for nm in names:
                if nm in stacks:
                    stacks[nm].pop_all().close()
                    del stacks[nm]

        C = {}
        for nm in ("cos_q", "sin_q", "cos_2q", "sin_2q", "cos_k63", "sin_k63",
                   "cos_snl", "sin_snl", "ident_bf", "ident_f32", "band",
                   "m0", "iota64", "pvec", "qm63", "ones_row_f"):
            arr = HT[nm]
            dt = BF16 if arr.dtype == bfdt else F32
            C[nm] = sbuf("c_" + nm, arr.shape, dt)
            src_ap = d_in["t_" + nm]
            dst_ap = C[nm][:, :] if arr.ndim == 2 else C[nm][:, :, :]
            src_ap = src_ap[:, :] if arr.ndim == 2 else src_ap[:, :, :]
            pr.dma("sync", dst_ap, src_ap, sl)

        pr.wait("vector", sl)
        one_col = sbuf("one_col", (128, 1), F32)
        pr.op("vector", "memset", one_col[:, :], 1.0)
        epsln = sbuf("epsln", (128, 1), F32)
        pr.op("vector", "memset", epsln[:, :], 1e-5)
        ones1 = sbuf("ones1", (128, 1), F32)
        pr.op("vector", "memset", ones1[:, :], 1.0)
        halfpi = sbuf("halfpi", (1, 1), F32)
        pr.op("vector", "memset", halfpi[:, :], math.pi / 2.0, sem=sdv)

        idx32f = sbuf("idx32f", (128, NT, 32), F32)
        idx0f = sbuf("idx0f", (128, 96), F32)

        stacks["ps_main"] = ctx.enter_context(ExitStack())
        _psm = stacks["ps_main"]
        PS_A = _psm.enter_context(nc.psum_tensor("PS_A", [128, 2048], F32))
        PS_B = _psm.enter_context(nc.psum_tensor("PS_B", [128, 512], F32))
        PS_C = _psm.enter_context(nc.psum_tensor("PS_C", [128, 512], F32))
        PS_D = _psm.enter_context(nc.psum_tensor("PS_D", [128, 512], F32))
        PS_T = _psm.enter_context(nc.psum_tensor("PS_T", [128, 512], BF16))

        def wload(dst, src, kchunks):
            for kk in range(kchunks):
                pr.dma("gpsimd", dst[:, kk, :],
                       src[kk * 128:(kk + 1) * 128, :], sg)

        idxw_sb = sbuf("idxw_sb", (1, NIH), F32)
        pr.dma("sync", idxw_sb[:, :], idx_w[:, :], sl)
        rdelta_sb = sbuf("rdelta_sb", (1, HALF), F32)
        pr.dma("sync", rdelta_sb[:, :], raw_delta[:, :], sl)
        ebias_sb = sbuf("ebias_sb", (1, NE), F32)
        pr.dma("sync", ebias_sb[:, :], exp_bias[:, :], sl)

        dl = sbuf("dl", (1, 8 * HALF), F32)
        pr.wait("scalar", sl)
        pr.wait("scalar", sdv)
        sigv = dl[:, 0:HALF]
        hv = dl[:, HALF:2 * HALF]
        sinh = dl[:, 2 * HALF:3 * HALF]
        cosh = dl[:, 3 * HALF:4 * HALF]
        cosd1 = dl[:, 4 * HALF:5 * HALF]
        sind1 = dl[:, 5 * HALF:6 * HALF]
        pr.op("scalar", "activation", sigv, rdelta_sb[:, :], AF.Sigmoid)
        pr.op("scalar", "mul", hv, sigv, -math.pi)
        pr.op("scalar", "activation", sinh, hv, AF.Sin)
        pr.op("scalar", "activation", cosh, hv, AF.Sin, bias=halfpi[0:1, :],
              sem=sac)
        pr.wait("vector", sac)
        pr.op("vector", "tensor_tensor", cosd1, sinh, sinh, op=ALU.mult)
        pr.op("vector", "tensor_scalar", cosd1, cosd1, -2.0, 1.0,
              op0=ALU.mult, op1=ALU.add)
        pr.op("vector", "tensor_tensor", sind1, sinh, cosh, op=ALU.mult)
        pr.op("vector", "tensor_scalar", sind1, sind1, 2.0, None,
              op0=ALU.mult, sem=sdv)
        cosd = sbuf("cosd", (128, HALF), F32)
        sind = sbuf("sind", (128, HALF), F32)
        ebias_rep = sbuf("ebias_rep", (128, NE), F32)
        idxw_rep = sbuf("idxw_rep", (128, NIH), F32)
        pr.wait("tensor", sdv)
        pr.wait("tensor", sl)
        pr.op("tensor", "matmul", PS_D[:, 0:HALF], C["ones_row_f"][:, :],
              cosd1, start=True, stop=True)
        pr.op("tensor", "matmul", PS_D[:, HALF:2 * HALF],
              C["ones_row_f"][:, :], sind1, start=True, stop=True)
        pr.op("tensor", "matmul", PS_D[:, 64:64 + NE], C["ones_row_f"][:, :],
              ebias_sb[:, :], start=True, stop=True)
        pr.op("tensor", "matmul", PS_D[:, 128:128 + NIH],
              C["ones_row_f"][:, :], idxw_sb[:, :], start=True, stop=True,
              sem=spe)
        pr.wait("vector", spe)
        pr.op("vector", "tensor_copy", cosd[:, :], PS_D[:, 0:HALF])
        pr.op("vector", "tensor_copy", sind[:, :], PS_D[:, HALF:2 * HALF])
        pr.op("vector", "tensor_copy", ebias_rep[:, :], PS_D[:, 64:64 + NE])
        pr.op("vector", "tensor_copy", idxw_rep[:, :],
              PS_D[:, 128:128 + NIH], sem=sdv)

        def make_slot_tables(cosrep_ap, sinrep_ap, nslots, nm, st="persist",
                             side="left"):
            ca = sbuf("ca_" + nm, (128, nslots * HALF), F32, st=st, side=side)
            sa = sbuf("sa_" + nm, (128, nslots * HALF), F32, st=st, side=side)
            t1 = sbuf("tmp_" + nm, (128, nslots * HALF), F32, st=st,
                      side=side)
            cv = cosrep_ap.rearrange("p (j f) -> p j f", f=HALF)
            sv = sinrep_ap.rearrange("p (j f) -> p j f", f=HALF)
            cosd3 = cosd[:, :].unsqueeze(1).to_broadcast([128, nslots, HALF])
            sind3 = sind[:, :].unsqueeze(1).to_broadcast([128, nslots, HALF])
            ca3 = ca[:, :].rearrange("p (j f) -> p j f", f=HALF)
            sa3 = sa[:, :].rearrange("p (j f) -> p j f", f=HALF)
            t13 = t1[:, :].rearrange("p (j f) -> p j f", f=HALF)
            pr.op("vector", "tensor_tensor", ca3, cv, cosd3, op=ALU.mult)
            pr.op("vector", "tensor_tensor", t13, sv, sind3, op=ALU.mult)
            pr.op("vector", "tensor_tensor", ca3, ca3, t13, op=ALU.subtract)
            pr.op("vector", "tensor_tensor", sa3, sv, cosd3, op=ALU.mult)
            pr.op("vector", "tensor_tensor", t13, cv, sind3, op=ALU.mult)
            pr.op("vector", "tensor_tensor", sa3, sa3, t13, op=ALU.add,
                  sem=sdv)
            return ca, sa

        ca_nl, sa_nl = make_slot_tables(C["cos_snl"][:, :],
                                        C["sin_snl"][:, :], 32, "nl")

        def angle_combine(cv, sv, cc, ss, tmp):
            pr.op("vector", "tensor_tensor", cc, cv, cosd[:, :], op=ALU.mult)
            pr.op("vector", "tensor_tensor", tmp, sv, sind[:, :], op=ALU.mult)
            pr.op("vector", "tensor_tensor", cc, cc, tmp, op=ALU.subtract)
            pr.op("vector", "tensor_tensor", ss, sv, cosd[:, :], op=ALU.mult)
            pr.op("vector", "tensor_tensor", tmp, cv, sind[:, :], op=ALU.mult)
            pr.op("vector", "tensor_tensor", ss, ss, tmp, op=ALU.add)

        def softplus(src, dst, scr):
            pr.wait("scalar", sdv)
            pr.op("scalar", "activation", scr, src, AF.Abs)
            pr.op("scalar", "activation", scr, scr, AF.Exp, scale=-1.0)
            pr.op("scalar", "activation", scr, scr, AF.Ln, bias=ones1[:, :])
            pr.op("scalar", "activation", dst, src, AF.Relu, sem=sac)
            pr.wait("vector", sac)
            pr.op("vector", "tensor_tensor", dst, dst, scr, op=ALU.add)

        xn_fm = [sbuf(f"xn_fm{c}", (128, L), st="xn") for c in range(8)]
        ln_x = rbuf("ln_x", (128, D), F32, st="ln")
        ln_sq = rbuf("ln_sq", (128, D), F32, st="ln")
        ln_st = rbuf("ln_st", (128, 8), F32, st="ln")
        ln_bf = rbuf("ln_bf", (128, D), st="ln")
        for t in range(NT):
            pr.wait("sync", sdv)
            pr.dma("sync", ln_x[:, :], x_tok[t * 128:(t + 1) * 128, :], sl)
            pr.wait("vector", sl)
            pr.wait("vector", sac)
            mean = ln_st[:, 0:1]
            m2 = ln_st[:, 1:2]
            var = ln_st[:, 2:3]
            lnv = ln_st[:, 3:4]
            rstd = ln_st[:, 4:5]
            pr.op("vector", "tensor_reduce", mean, ln_x[:, :], axis=AX.X,
                  op=ALU.add)
            pr.op("vector", "tensor_scalar", mean, mean, 1.0 / D, None,
                  op0=ALU.mult)
            pr.op("vector", "tensor_copy", ln_st[:, 5:6], ln_st[:, 0:1],
                  sem=sdv)
            pr.wait("scalar", sdv)
            pr.op("scalar", "activation", ln_sq[:, :], ln_x[:, :], AF.Square,
                  sem=sac)
            pr.wait("vector", sac)
            pr.op("vector", "tensor_reduce", m2, ln_sq[:, :], axis=AX.X,
                  op=ALU.add)
            pr.op("vector", "tensor_scalar", m2, m2, 1.0 / D, None,
                  op0=ALU.mult)
            pr.op("vector", "tensor_copy", ln_st[:, 6:7], ln_st[:, 1:2],
                  sem=sdv)
            pr.wait("scalar", sdv)
            pr.op("scalar", "activation", var, mean, AF.Square, sem=sac)
            pr.wait("vector", sac)
            pr.op("vector", "tensor_tensor", var, m2, var, op=ALU.subtract,
                  sem=sdv)
            pr.wait("scalar", sdv)
            pr.op("scalar", "activation", lnv, var, AF.Ln, bias=epsln[:, :])
            pr.op("scalar", "activation", rstd, lnv, AF.Exp, scale=-0.5,
                  sem=sac)
            pr.wait("vector", sac)
            pr.op("vector", "tensor_scalar", ln_bf[:, :], ln_x[:, :], mean,
                  rstd, op0=ALU.subtract, op1=ALU.mult, sem=sdv)
            pr.wait("tensor", sdv)
            for c in range(8):
                slot = (c % 4) * 128
                pr.wait("tensor", sac)
                pr.op("tensor", "transpose", PS_T[:, slot:slot + 128],
                      ln_bf[:, c * 128:(c + 1) * 128], C["ident_bf"][:, :],
                      sem=spe)
                pr.wait("scalar", spe)
                pr.op("scalar", "copy", xn_fm[c][:, t * 128:(t + 1) * 128],
                      PS_T[:, slot:slot + 128], sem=sac)
        barrier()
        for c in range(8):
            pr.dma("sync", xn_fm_hbm[c * 128:(c + 1) * 128, :],
                   xn_fm[c][:, :], sst)
        if tap("xn_fm0", (128, L), BF16) is not None:
            pr.dma("sync", tap_out["xn_fm0"][:, :], xn_fm[0][:, :], sst)
        free("ln")
        pr.mark("aa")

        def fm_chain(dst_chunks, wsb, kchunks, mtot, src_chunks=None,
                     mchunk=128):
            src = src_chunks or xn_fm
            nm = (mtot + mchunk - 1) // mchunk
            for ts in range(L // TSL):
                for mc in range(nm):
                    mw = min(mchunk, mtot - mc * mchunk)
                    ps = PS_B if (mc % 2 == 0) else PS_C
                    pr.wait("tensor", sdv)
                    for kk in range(kchunks):
                        pr.op("tensor", "matmul", ps[0:mw, 0:TSL],
                              wsb[:, kk, mc * mchunk:mc * mchunk + mw],
                              src[kk][:, ts * TSL:(ts + 1) * TSL],
                              start=(kk == 0), stop=(kk == kchunks - 1),
                              sem=(spe if kk == kchunks - 1 else None))
                    pr.wait("vector", spe)
                    pr.op("vector", "tensor_copy",
                          dst_chunks[mc][0:mw, ts * TSL:(ts + 1) * TSL],
                          ps[0:mw, 0:TSL], sem=sdv)

        def fm_chain_hbm(dst_hbm, stage, wsb, kchunks, mtot,
                         src_chunks=None):
            src = src_chunks or xn_fm
            nm = (mtot + 127) // 128
            for ts in range(L // TSL):
                for mc in range(nm):
                    mw = min(128, mtot - mc * 128)
                    pr.wait("tensor", sdv)
                    pr.wait("tensor", sst)
                    for kk in range(kchunks):
                        pr.op("tensor", "matmul", PS_B[0:mw, 0:TSL],
                              wsb[:, kk, mc * 128:mc * 128 + mw],
                              src[kk][:, ts * TSL:(ts + 1) * TSL],
                              start=(kk == 0), stop=(kk == kchunks - 1),
                              sem=(spe if kk == kchunks - 1 else None))
                    pr.wait("vector", spe)
                    pr.wait("vector", sst)
                    pr.op("vector", "tensor_copy", stage[0:mw, :],
                          PS_B[0:mw, 0:TSL], sem=sdv)
                    pr.wait("sync", sdv)
                    pr.dma("sync",
                           dst_hbm[mc * 128:mc * 128 + mw,
                                   ts * TSL:(ts + 1) * TSL],
                           stage[0:mw, :], sst)

        k_iT = rbuf("k_iT", (64, L), st="qk")
        q_iT = [rbuf(f"q_iT{c}", (64, L), st="qk") for c in range(4)]

        W1 = {}
        W1["idx_wq"] = rbuf("s_idx_wq", (128, 8, NIH * DH), st="w1")
        wload(W1["idx_wq"], idx_wq, 8)
        W1["idx_wk"] = rbuf("s_idx_wk", (128, 8, DH), st="w1")
        wload(W1["idx_wk"], idx_wk, 8)
        W1["w_kr"] = rbuf("s_w_kr", (128, 8, DR), st="w1")
        wload(W1["w_kr"], w_kr, 8)
        W1["cent"] = rbuf("s_cent", (128, 8, NE), st="w1")
        wload(W1["cent"], cent_T, 8)
        W1["ws_up"] = rbuf("s_ws_up", (128, 8, FF), st="w1")
        wload(W1["ws_up"], ws_up, 8)
        W1["ws_down"] = rbuf("s_ws_down", (128, 4, D), st="w1")
        wload(W1["ws_down"], ws_down, 4)
        pr.wait("tensor", sg)

        fm_chain([k_iT], W1["idx_wk"], 8, DH)
        fm_chain(q_iT, W1["idx_wq"], 8, NIH * DH, mchunk=64)
        if tap("k_iT", (64, L), BF16) is not None:
            pr.wait("sync", sdv)
            pr.dma("sync", tap_out["k_iT"][:, :], k_iT[:, :], sst)

        kr_f = rbuf("kr_f", (128, DR), F32, st="w1")
        kr_scr = rbuf("kr_scr", (128, 4 * DR), F32, st="w1")
        kr_bf = rbuf("kr_bf", (128, DR), st="w1")
        krl_st = rbuf("krl_st", (32, 128), st="w1")
        ang_t = rbuf("ang_t", (128, 3 * HALF), F32, st="w1")
        for t in range(NT):
            pr.wait("tensor", sdv)
            pr.wait("tensor", sst)
            for kk in range(8):
                pr.op("tensor", "matmul", PS_D[:, 0:DR],
                      xn_fm[kk][:, t * 128:(t + 1) * 128],
                      W1["w_kr"][:, kk, :], start=(kk == 0), stop=(kk == 7),
                      sem=(spe if kk == 7 else None))
            pr.wait("vector", spe)
            pr.op("vector", "tensor_copy", kr_f[:, :], PS_D[:, 0:DR])
            pr.op("vector", "tensor_copy", kr_bf[:, :], PS_D[:, 0:DR],
                  sem=sdv)
            pr.wait("sync", sdv)
            pr.dma("sync", ktab[t * 128:(t + 1) * 128, NH * DH:NH * DH + DR],
                   kr_bf[:, :], sst)
            sp_ = kr_scr[:, 0:DR]
            scr = kr_scr[:, DR:2 * DR]
            softplus(kr_f[:, :], sp_, scr)
            cc = ang_t[:, 0:HALF]
            ss = ang_t[:, HALF:2 * HALF]
            tmp = ang_t[:, 2 * HALF:3 * HALF]
            angle_combine(C["cos_k63"][:, t, :], C["sin_k63"][:, t, :],
                          cc, ss, tmp)
            mu1 = sp_[:, 0:HALF]
            mu2 = sp_[:, HALF:DR]
            t2 = kr_scr[:, 2 * DR:2 * DR + HALF]
            t3 = kr_scr[:, 2 * DR + HALF:3 * DR]
            pr.wait("vector", sst)
            pr.op("vector", "tensor_tensor", t2, mu1, cc, op=ALU.mult)
            pr.op("vector", "tensor_tensor", t3, mu2, ss, op=ALU.mult)
            pr.op("vector", "tensor_tensor", kr_bf[:, 0:HALF], t2, t3,
                  op=ALU.subtract)
            pr.op("vector", "tensor_tensor", t2, mu1, ss, op=ALU.mult)
            pr.op("vector", "tensor_tensor", t3, mu2, cc, op=ALU.mult)
            pr.op("vector", "tensor_tensor", kr_bf[:, HALF:DR], t2, t3,
                  op=ALU.add, sem=sdv)
            pr.wait("tensor", sdv)
            pr.wait("tensor", sac)
            pr.op("tensor", "transpose", PS_T[0:32, 0:128], kr_bf[:, :],
                  C["ident_bf"][:, :], sem=spe)
            pr.wait("scalar", spe)
            pr.wait("scalar", sst)
            pr.op("scalar", "copy", krl_st[:, :], PS_T[0:32, 0:128], sem=sac)
            pr.wait("sync", sac)
            pr.dma("sync", krl_hbm[:, t * 128:(t + 1) * 128], krl_st[:, :],
                   sst)

        sh_h = rbuf("sh_h", (128, 4, TSL), st="w1")
        sh_o = rbuf("sh_o", (128, TSL), F32, st="w1")
        for ts in range(L // TSL):
            for fc in range(4):
                pr.wait("tensor", sac)
                pr.wait("tensor", sst)
                for kk in range(8):
                    pr.op("tensor", "matmul", PS_B[:, 0:TSL],
                          W1["ws_up"][:, kk, fc * 128:(fc + 1) * 128],
                          xn_fm[kk][:, ts * TSL:(ts + 1) * TSL],
                          start=(kk == 0), stop=(kk == 7),
                          sem=(spe if kk == 7 else None))
                pr.wait("scalar", spe)
                pr.op("scalar", "activation", sh_h[:, fc, :], PS_B[:, 0:TSL],
                      AF.Silu, sem=sac)
            for dd in range(8):
                pr.wait("tensor", sac)
                pr.wait("tensor", sdv)
                for fc in range(4):
                    pr.op("tensor", "matmul", PS_C[:, 0:TSL],
                          W1["ws_down"][:, fc, dd * 128:(dd + 1) * 128],
                          sh_h[:, fc, :], start=(fc == 0), stop=(fc == 3),
                          sem=(spe if fc == 3 else None))
                pr.wait("vector", spe)
                pr.wait("vector", sst)
                pr.op("vector", "tensor_copy", sh_o[:, :], PS_C[:, 0:TSL],
                      sem=sdv)
                pr.wait("sync", sdv)
                pr.dma("sync", shared_hbm[dd * 128:(dd + 1) * 128,
                                          ts * TSL:(ts + 1) * TSL],
                       sh_o[:, :], sst)

        aff = rbuf("aff", (128, NE), F32, st="w1")
        rt_sc = rbuf("rt_sc", (128, NE + 8), F32, st="w1")
        top8v = rbuf("top8v", (128, 8), F32, st="w1")
        idx8u = rbuf("idx8u", (128, 8), U32, st="w1")
        idx8f = rbuf("idx8f", (128, 8), F32, st="w1")
        wsel = rbuf("wsel", (128, 8), F32, st="w1")
        eq3 = rbuf("eq3", (128, 8 * NE), F32, st="w1")
        cwt = rbuf("cwt", (128, NE), F32, st="w1")
        cw_fm = rbuf("cw_fm", (64, L), F32, st="w1")
        for t in range(NT):
            pr.wait("tensor", sdv)
            for kk in range(8):
                pr.op("tensor", "matmul", PS_D[:, 0:NE],
                      xn_fm[kk][:, t * 128:(t + 1) * 128],
                      W1["cent"][:, kk, :], start=(kk == 0), stop=(kk == 7),
                      sem=(spe if kk == 7 else None))
            pr.wait("vector", spe)
            pr.op("vector", "tensor_copy", aff[:, :], PS_D[:, 0:NE], sem=sdv)
            biased = rt_sc[:, 0:NE]
            pr.op("vector", "tensor_tensor", biased, aff[:, :],
                  ebias_rep[:, :], op=ALU.add)
            pr.op("vector", "max", out=top8v[:, :], in_=biased, sem=sdv)
            pr.wait("vector", sdv)
            pr.op("vector", "max_index", idx8u[:, :], top8v[:, :], biased,
                  sem=sdv)
            pr.wait("vector", sdv)
            pr.op("vector", "tensor_copy", idx8f[:, :], idx8u[:, :])
            eqv = eq3[:, :].rearrange("p (s e) -> p s e", e=NE)
            io3 = C["iota64"][:, :].unsqueeze(1).to_broadcast([128, 8, NE])
            id3 = idx8f[:, :].unsqueeze(2).to_broadcast([128, 8, NE])
            pr.op("vector", "tensor_tensor", eqv, io3, id3, op=ALU.is_equal)
            af3 = aff[:, :].unsqueeze(1).to_broadcast([128, 8, NE])
            pr.op("vector", "tensor_tensor", eqv, eqv, af3, op=ALU.mult)
            pr.op("vector", "tensor_reduce", wsel[:, :], eqv, axis=AX.X,
                  op=ALU.add, sem=sdv)
            pr.wait("scalar", sdv)
            pr.op("scalar", "activation", wsel[:, :], wsel[:, :], AF.Sigmoid,
                  sem=sac)
            pr.wait("vector", sac)
            wsum = rt_sc[:, NE:NE + 1]
            pr.op("vector", "tensor_reduce", wsum, wsel[:, :], axis=AX.X,
                  op=ALU.add)
            pr.op("vector", "tensor_scalar", wsum, wsum, 1e-8, None,
                  op0=ALU.add)
            pr.op("vector", "reciprocal", wsum, wsum)
            pr.op("vector", "tensor_scalar", wsel[:, :], wsel[:, :], wsum,
                  None, op0=ALU.mult)
            eq2 = eq3[:, :].rearrange("p (e s) -> p e s", s=8)
            io2 = C["iota64"][:, :].unsqueeze(2).to_broadcast([128, NE, 8])
            id2 = idx8f[:, :].unsqueeze(1).to_broadcast([128, NE, 8])
            pr.op("vector", "tensor_tensor", eq2, io2, id2, op=ALU.is_equal)
            w2 = wsel[:, :].unsqueeze(1).to_broadcast([128, NE, 8])
            pr.op("vector", "tensor_tensor", eq2, eq2, w2, op=ALU.mult)
            pr.op("vector", "tensor_reduce", cwt[:, :], eq2, axis=AX.X,
                  op=ALU.add, sem=sdv)
            if t == 2:
                for tn, tb_ in (("aff2", aff), ("idx8f2", idx8f),
                                ("wsel2", wsel), ("cwt2", cwt),
                                ("eq32", eq3)):
                    if tap(tn, (128, tb_.shape[-1])) is not None:
                        pr.wait("sync", sdv)
                        pr.dma("sync", tap_out[tn][:, :], tb_[:, :], sst)
            pr.wait("tensor", sdv)
            pr.op("tensor", "transpose", PS_D[0:64, 128:256], cwt[:, :],
                  C["ident_f32"][:, :], sem=spe)
            pr.wait("vector", spe)
            pr.op("vector", "tensor_copy", cw_fm[:, t * 128:(t + 1) * 128],
                  PS_D[0:64, 128:256], sem=sdv)
            pr.wait("tensor", sdv)
        cw_bfs = rbuf("cw_bfs", (64, L), BF16, st="w1")
        pr.op("vector", "tensor_copy", cw_bfs[:, :], cw_fm[:, :], sem=sdv)
        pr.wait("sync", sdv)
        pr.dma("sync", cw_hbm[:, :], cw_fm[:, :], sst)
        pr.dma("sync", cw_bf_hbm[:, :], cw_bfs[:, :], sst)
        if tap("cw_fm", (64, L)) is not None:
            pr.dma("sync", tap_out["cw_fm"][:, :], cw_fm[:, :], sst)
        free("w1")
        pr.mark("ab1")

        c_qT = [rbuf(f"c_qT{c}", (128, L), st="cq") for c in range(4)]
        c_kvT = [rbuf(f"c_kvT{c}", (128, L), st="ckv") for c in range(2)]
        W2 = {}
        W2["w_dkv"] = rbuf("s_w_dkv", (128, 8, DCKV), st="w2")
        wload(W2["w_dkv"], w_dkv, 8)
        W2["w_dq"] = rbuf("s_w_dq", (128, 8, DCQ), st="w2")
        wload(W2["w_dq"], w_dq, 8)
        pr.wait("tensor", sg)
        fm_chain(c_kvT, W2["w_dkv"], 8, DCKV)
        fm_chain(c_qT, W2["w_dq"], 8, DCQ)
        if tap("c_kvT0", (128, L), BF16) is not None:
            pr.wait("sync", sdv)
            pr.dma("sync", tap_out["c_kvT0"][:, :], c_kvT[0][:, :], sst)
        free("w2")
        pr.mark("ab2")

        W3 = {}
        W3["w_uk"] = rbuf("s_w_uk", (128, 2, NH * DH), st="w3")
        wload(W3["w_uk"], w_uk, 2)
        W3["w_uv"] = rbuf("s_w_uv", (128, 2, NH * DH), st="w3")
        wload(W3["w_uv"], w_uv, 2)
        kc_stage = rbuf("kc_stage", (128, TSL), st="w3")
        kv_st = rbuf("kv_st", (128, NH * DH), st="w3")
        pr.wait("tensor", sg)
        fm_chain_hbm(kcT_hbm, kc_stage, W3["w_uk"], 2, NH * DH,
                     src_chunks=c_kvT)
        for t in range(NT):
            for which in ("kc", "v"):
                wsb = W3["w_uk"] if which == "kc" else W3["w_uv"]
                pr.wait("tensor", sac)
                pr.wait("tensor", sst)
                for kk in range(2):
                    pr.op("tensor", "matmul", PS_B[:, 0:NH * DH],
                          c_kvT[kk][:, t * 128:(t + 1) * 128], wsb[:, kk, :],
                          start=(kk == 0), stop=(kk == 1),
                          sem=(spe if kk == 1 else None))
                pr.wait("scalar", spe)
                pr.wait("scalar", sst)
                pr.op("scalar", "copy", kv_st[:, :], PS_B[:, 0:NH * DH],
                      sem=sac)
                pr.wait("sync", sac)
                if which == "kc":
                    pr.dma("sync", ktab[t * 128:(t + 1) * 128, 0:NH * DH],
                           kv_st[:, :], sst)
                else:
                    pr.dma("sync", vtab[t * 128:(t + 1) * 128, :],
                           kv_st[:, :], sst)
        free("w3", "ckv")
        pr.mark("ac")

        W4 = {}
        W4["w_uq"] = rbuf("s_w_uq", (128, 4, NH * DH), st="w4")
        wload(W4["w_uq"], w_uq, 4)
        W4["w_qr"] = rbuf("s_w_qr", (128, 4, NH * DR), st="w4")
        wload(W4["w_qr"], w_qr, 4)
        qc_st = rbuf("qc_st", (128, NH * DH), st="w4")
        qrot_st = rbuf("qrot_st", (128, NH * DR), st="w4")
        qrl_st = rbuf("qrl_st", (128, NH * DR), st="w4")
        qp_f = rbuf("qp_f", (128, NH * DR), F32, st="w4")
        qp_sp = rbuf("qp_sp", (128, 2 * NH * DR), F32, st="w4")
        qang = rbuf("qang", (128, 3 * HALF), F32, st="w4")
        qrl_bf = rbuf("qrl_bf", (128, NH * DR), st="w4")
        pr.wait("tensor", sg)

        def pope_heads(src_f32, cc, ss, dst_bf, scr2):
            n32 = NH * DR
            sp_ = scr2[:, 0:n32]
            scr = scr2[:, n32:2 * n32]
            softplus(src_f32, sp_, scr)
            sp3 = sp_.rearrange("p (h two f) -> p h two f", two=2, f=HALF)
            mu1 = sp3[:, :, 0, :]
            mu2 = sp3[:, :, 1, :]
            d3 = dst_bf.rearrange("p (h two f) -> p h two f", two=2, f=HALF)
            cc3 = cc.unsqueeze(1).to_broadcast([128, NH, HALF])
            ss3 = ss.unsqueeze(1).to_broadcast([128, NH, HALF])
            t3 = scr.rearrange("p (h two f) -> p h two f", two=2, f=HALF)
            pr.op("vector", "tensor_tensor", t3[:, :, 0, :], mu1, cc3,
                  op=ALU.mult)
            pr.op("vector", "tensor_tensor", t3[:, :, 1, :], mu2, ss3,
                  op=ALU.mult)
            pr.op("vector", "tensor_tensor", d3[:, :, 0, :], t3[:, :, 0, :],
                  t3[:, :, 1, :], op=ALU.subtract)
            pr.op("vector", "tensor_tensor", t3[:, :, 0, :], mu1, ss3,
                  op=ALU.mult)
            pr.op("vector", "tensor_tensor", t3[:, :, 1, :], mu2, cc3,
                  op=ALU.mult)
            pr.op("vector", "tensor_tensor", d3[:, :, 1, :], t3[:, :, 0, :],
                  t3[:, :, 1, :], op=ALU.add, sem=sdv)

        for t in range(NT):
            pr.wait("tensor", sdv)
            pr.wait("tensor", sac)
            pr.wait("tensor", sst)
            for kk in range(4):
                pr.op("tensor", "matmul", PS_B[:, 0:NH * DH],
                      c_qT[kk][:, t * 128:(t + 1) * 128],
                      W4["w_uq"][:, kk, :], start=(kk == 0), stop=(kk == 3),
                      sem=(spe if kk == 3 else None))
            pr.wait("scalar", spe)
            pr.wait("scalar", sst)
            pr.op("scalar", "copy", qc_st[:, :], PS_B[:, 0:NH * DH], sem=sac)
            pr.wait("sync", sac)
            pr.dma("sync", qc_hbm[:, t * NH * DH:(t + 1) * NH * DH],
                   qc_st[:, :], sst)
            pr.wait("tensor", sac)
            for kk in range(4):
                pr.op("tensor", "matmul", PS_C[:, 0:NH * DR],
                      c_qT[kk][:, t * 128:(t + 1) * 128],
                      W4["w_qr"][:, kk, :], start=(kk == 0), stop=(kk == 3),
                      sem=(spe if kk == 3 else None))
            pr.wait("vector", spe)
            pr.op("vector", "tensor_copy", qp_f[:, :], PS_C[:, 0:NH * DR],
                  sem=sdv)
            cc = qang[:, 0:HALF]
            ss = qang[:, HALF:2 * HALF]
            tmp = qang[:, 2 * HALF:3 * HALF]
            pr.wait("vector", sst)
            angle_combine(C["cos_q"][:, t, :], C["sin_q"][:, t, :], cc, ss,
                          tmp)
            pope_heads(qp_f[:, :], cc, ss, qrot_st[:, :], qp_sp)
            pr.wait("sync", sdv)
            pr.dma("sync", qrot_hbm[:, t * NH * DR:(t + 1) * NH * DR],
                   qrot_st[:, :], sst)
            angle_combine(C["cos_2q"][:, t, :], C["sin_2q"][:, t, :], cc, ss,
                          tmp)
            pope_heads(qp_f[:, :], cc, ss, qrl_bf[:, :], qp_sp)
            pr.wait("tensor", sdv)
            pr.wait("tensor", sac)
            for c in range(2):
                pr.op("tensor", "transpose", PS_T[:, c * 128:(c + 1) * 128],
                      qrl_bf[:, c * 128:(c + 1) * 128], C["ident_bf"][:, :],
                      sem=(spe if c == 1 else None))
            pr.wait("scalar", spe)
            pr.wait("scalar", sst)
            for c in range(2):
                pr.op("scalar", "copy", qrl_st[:, c * 128:(c + 1) * 128],
                      PS_T[:, c * 128:(c + 1) * 128],
                      sem=(sac if c == 1 else None))
            pr.wait("sync", sac)
            for c in range(2):
                pr.dma("sync",
                       qrl_hbm[c * 128:(c + 1) * 128, t * 128:(t + 1) * 128],
                       qrl_st[:, c * 128:(c + 1) * 128], sst)
        free("w4", "cq", "xn")
        pr.mark("ad")

        colv = rbuf("colv_b1", (128, L), F32, st="b1x")
        pr.dma("sync", colv[:, :], d_in["t_colv"][:, :], sl)
        Ibuf = rbuf("Ibuf", (128, L), F32, st="b1x")
        relu_b = rbuf("relu_b", (128, 4, 512), F32, st="b1x")
        qm63 = rbuf("qm63", (128, 1), F32, st="b1x")
        i8v = rbuf("i8v", (128, 8), F32, st="b1x")
        i8u = rbuf("i8u", (128, 8), U32, st="b1x")
        pr.wait("vector", sl)

        for t in range(NT):
            q0 = t * 128
            Lt = 128 if t == 0 else q0 + 64
            if t == 0:
                pr.op("vector", "tensor_copy", Ibuf[:, 0:128], C["m0"][:, :],
                      sem=sdv)
            else:
                for cc_ in range(0, Lt, 512):
                    cw_ = min(512, Lt - cc_)
                    pen = Ibuf[:, cc_:cc_ + cw_]
                    pr.op("vector", "tensor_scalar", pen,
                          colv[:, cc_:cc_ + cw_], C["qm63"][:, t:t + 1], NEG,
                          op0=ALU.is_ge, op1=ALU.mult)
                    pr.op("vector", "scalar_tensor_tensor", pen,
                          colv[:, cc_:cc_ + cw_], -EPS_TIE, pen,
                          op0=ALU.mult, op1=ALU.add, sem=sdv)
            for cc_ in range(0, Lt, 512):
                cw_ = min(512, Lt - cc_)
                pr.wait("tensor", sac)
                pr.wait("tensor", sdv)
                for h in range(NIH):
                    pr.op("tensor", "matmul", PS_A[:, h * 512:h * 512 + cw_],
                          q_iT[h][:, q0:q0 + 128],
                          k_iT[0:64, cc_:cc_ + cw_], start=True, stop=True,
                          sem=(spe if h == NIH - 1 else None))
                pr.wait("scalar", spe)
                for h in range(NIH):
                    pr.op("scalar", "activation", relu_b[:, h, 0:cw_],
                          PS_A[:, h * 512:h * 512 + cw_], AF.Relu,
                          sem=(sac if h == NIH - 1 else None))
                pr.wait("vector", sac)
                for h in range(NIH):
                    pr.op("vector", "scalar_tensor_tensor",
                          Ibuf[:, cc_:cc_ + cw_], relu_b[:, h, 0:cw_],
                          idxw_rep[:, h:h + 1], Ibuf[:, cc_:cc_ + cw_],
                          op0=ALU.mult, op1=ALU.add,
                          sem=(sdv if h == NIH - 1 else None))
            pr.wait("vector", sdv)
            nrounds = 12 if t == 0 else 4
            for r in range(nrounds):
                pr.op("vector", "max", out=i8v[:, :], in_=Ibuf[:, 0:Lt],
                      sem=sdv)
                pr.wait("vector", sdv)
                pr.op("vector", "max_index", i8u[:, :], i8v[:, :],
                      Ibuf[:, 0:Lt], sem=sdv)
                pr.wait("vector", sdv)
                if t == 0:
                    pr.op("vector", "tensor_copy", idx0f[:, r * 8:r * 8 + 8],
                          i8u[:, :])
                else:
                    pr.op("vector", "tensor_copy",
                          idx32f[:, t, r * 8:r * 8 + 8], i8u[:, :])
                if r < nrounds - 1:
                    pr.op("vector", "match_replace", Ibuf[:, 0:Lt],
                          i8v[:, :], Ibuf[:, 0:Lt], imm_value=NEG, sem=sdv)
                    pr.wait("vector", sdv)
            pr.op("vector", "tensor_copy", i8v[:, :], i8v[:, :], sem=sdv)
        if tap("idx32f", (128, NT * 32)) is not None:
            pr.wait("sync", sdv)
            pr.dma("sync", tap_out["idx32f"][:, :],
                   idx32f[:, :, :].rearrange("p t s -> p (t s)"), sst)
        if tap("idx0f", (128, 96)) is not None:
            pr.wait("sync", sdv)
            pr.dma("sync", tap_out["idx0f"][:, :], idx0f[:, :], sst)
        wcast = [rbuf(f"wcast{i}", (128, 4, D), st="b1x") for i in range(2)]
        wc_done = [0, 0]
        for e in range(NE):
            par = e % 2
            pr.wait("gpsimd", sst, wc_done[par])
            for fc in range(4):
                pr.dma("gpsimd", wcast[par][:, fc, :],
                       w_down[e, fc * 128:(fc + 1) * 128, :], sg)
            pr.wait("sync", sg)
            for fc in range(4):
                pr.dma("sync", w_down_bf[e, fc * 128:(fc + 1) * 128, :],
                       wcast[par][:, fc, :], sst)
            wc_done[par] = sst.c
        free("b1x", "qk")
        pr.mark("b1")

        kbuf = rbuf("kbuf", (128, 32, NH * DH + DR), st="b2")
        vbuf = rbuf("vbuf", (128, 32, NH * DH), st="b2")
        idxu = [rbuf(f"idxu{i}", (128, 96), U32, st="b2") for i in range(2)]
        vsA = rbuf("vsA", (128, NH * DH), st="b2")
        vsB = rbuf("vsB", (64, NH * DH), st="b2")
        kcs = rbuf("kcs", (64, 8, 192), st="b2")
        krls = rbuf("krls", (32, 192), st="b2")
        qrls = rbuf("qrls", (32, 8, 128), st="b2")
        qct = rbuf("qct", (128, NH * DH), st="b2")
        qrott = rbuf("qrott", (128, NH * DR), st="b2")
        qcfm = rbuf("qcfm", (64, 8, 128), st="b2")
        scores = rbuf("scores", (128, NH, 224), F32, st="b2")
        prod = rbuf("prod", (128, 8192), st="b2")
        tree = rbuf("tree", (128, 4096), st="b2")
        krot = rbuf("krot", (128, 32, DR), st="b2")
        scsave = rbuf("scsave", (128, 32 * NH), F32, st="b2")
        smx = rbuf("smx", (128, 2 * NH), F32, st="b2")
        attn_bf = rbuf("attn_bf", (128, NH, 224), st="b2")
        attnT_A = rbuf("attnT_A", (128, NH, 128), st="b2")
        attnT_B = rbuf("attnT_B", (64, NH, 128), st="b2")
        nl_acc = rbuf("nl_acc", (128, NH * DH), F32, st="b2")
        mix_bf = rbuf("mix_bf", (128, NH * DH), st="b2")
        mixT = rbuf("mixT", (128, 4, 128), st="b2")
        po_x = rbuf("po_x", (128, 128), F32, st="b2")
        po_sh = rbuf("po_sh", (128, 128), F32, st="b2")
        po_o = rbuf("po_o", (128, 128), F32, st="b2")
        w_out_sb = rbuf("w_out_sb", (128, 4, D), st="b2")
        wload(w_out_sb, w_out, 4)

        c_s0 = rbuf("c_s0", (128, 96 * HALF), F32, st="s0")
        s_s0 = rbuf("s_s0", (128, 96 * HALF), F32, st="s0")
        pr.dma("sync", c_s0[:, :], d_in["t_cos_s0"][:, :], sl)
        pr.dma("sync", s_s0[:, :], d_in["t_sin_s0"][:, :], sl)
        pr.wait("vector", sl)
        ca_s0, sa_s0 = make_slot_tables(c_s0[:, :], s_s0[:, :], 96, "s0",
                                        st="s0", side="right")

        kg = kbuf[:, :, :]
        vg = vbuf[:, :, :]
        NS2 = 32

        def issue_k(idx_ap, base):
            for j in range(NS2):
                pr.gather(kg[:, j, :], ktab[:, :],
                          idx_ap[:, base + j:base + j + 1], sg)
            return sg.c

        def issue_v(idx_ap, base):
            for j in range(NS2):
                pr.gather(vg[:, j, :], vtab[:, :],
                          idx_ap[:, base + j:base + j + 1], sg)
            return sg.c

        def nl_scores32(t, ca_ap, sa_ap, dst_base, slot_base):
            ksp = tree[:, :].bitcast(F32)
            kr_v = kg[:, :, NH * DH:NH * DH + DR]
            n32 = NS2 * DR
            sp_ = ksp[:, 0:n32]
            scr = ksp[:, n32:2 * n32]
            softplus(kr_v, sp_, scr)
            sp3 = sp_.rearrange("p (j two f) -> p j two f", two=2, f=HALF)
            ca3 = ca_ap[:, :].rearrange("p (j f) -> p j f", f=HALF)[
                :, slot_base:slot_base + NS2, :]
            sa3 = sa_ap[:, :].rearrange("p (j f) -> p j f", f=HALF)[
                :, slot_base:slot_base + NS2, :]
            k3 = krot[:, :, :].rearrange("p j (two f) -> p j two f",
                                         two=2, f=HALF)
            t3 = scr.rearrange("p (j two f) -> p j two f", two=2, f=HALF)
            pr.op("vector", "tensor_tensor", t3[:, :, 0, :], sp3[:, :, 0, :],
                  ca3, op=ALU.mult)
            pr.op("vector", "tensor_tensor", t3[:, :, 1, :], sp3[:, :, 1, :],
                  sa3, op=ALU.mult)
            pr.op("vector", "tensor_tensor", k3[:, :, 0, :], t3[:, :, 0, :],
                  t3[:, :, 1, :], op=ALU.subtract)
            pr.op("vector", "tensor_tensor", t3[:, :, 0, :], sp3[:, :, 0, :],
                  sa3, op=ALU.mult)
            pr.op("vector", "tensor_tensor", t3[:, :, 1, :], sp3[:, :, 1, :],
                  ca3, op=ALU.mult)
            pr.op("vector", "tensor_tensor", k3[:, :, 1, :], t3[:, :, 0, :],
                  t3[:, :, 1, :], op=ALU.add)
            for ch in range(2):
                j0 = ch * 16
                pv = prod[:, 0:16 * 512].rearrange("p (j m) -> p j m", m=512)
                qcb = qct[:, :].unsqueeze(1).to_broadcast([128, 16, 512])
                pr.op("vector", "tensor_tensor", pv,
                      kg[:, j0:j0 + 16, 0:NH * DH], qcb, op=ALU.mult)
                src_ = prod[:, 0:16 * 512]
                width = DH
                tv = tree
                while width > 1:
                    half_ = width // 2
                    s4 = src_.rearrange("p (j h d) -> p j h d", h=NH,
                                        d=width)
                    d4 = tv[:, 0:16 * NH * half_].rearrange(
                        "p (j h d) -> p j h d", h=NH, d=half_)
                    pr.op("vector", "tensor_tensor", d4,
                          s4[:, :, :, 0:half_], s4[:, :, :, half_:width],
                          op=ALU.add)
                    src_ = tv[:, 0:16 * NH * half_]
                    tv = prod if tv is tree else tree
                    width = half_
                pr.op("vector", "tensor_copy",
                      scsave[:, ch * 128:(ch + 1) * 128], src_)
            pr_r = prod[:, 0:NS2 * NH * DR].rearrange(
                "p (j h f) -> p j h f", h=NH, f=DR)
            kro4 = krot[:, :, :].unsqueeze(2).to_broadcast(
                [128, NS2, NH, DR])
            qro4 = qrott[:, :].rearrange("p (h f) -> p h f",
                                         f=DR).unsqueeze(1).to_broadcast(
                [128, NS2, NH, DR])
            pr.op("vector", "tensor_tensor", pr_r, kro4, qro4, op=ALU.mult)
            src_ = prod[:, 0:NS2 * NH * DR]
            width = DR
            tv = tree
            while width > 1:
                half_ = width // 2
                s4 = src_.rearrange("p (j h f) -> p j h f", h=NH, f=width)
                d4 = tv[:, 0:NS2 * NH * half_].rearrange(
                    "p (j h f) -> p j h f", h=NH, f=half_)
                pr.op("vector", "tensor_tensor", d4, s4[:, :, :, 0:half_],
                      s4[:, :, :, half_:width], op=ALU.add)
                src_ = tv[:, 0:NS2 * NH * half_]
                tv = prod if tv is tree else tree
                width = half_
            scr_jh = src_.rearrange("p (j h) -> p j h", h=NH)
            scv_jh = scsave[:, :].rearrange("p (j h) -> p j h", h=NH)
            dsl = scores[:, :, dst_base:dst_base + NS2]
            dsl = dsl.rearrange("p h j -> p j h")
            pr.op("vector", "tensor_tensor", scv_jh, scv_jh, scr_jh,
                  op=ALU.add)
            pr.op("vector", "tensor_scalar", dsl, scv_jh, SCALE, None,
                  op0=ALU.mult, sem=sdv)

        def nl_values32(kbase, first):
            for ch in range(2):
                j0 = ch * 16
                a4 = attn_bf[:, :, kbase + j0:kbase + j0 + 16]
                a4 = a4.rearrange("p h j -> p j h").unsqueeze(3).to_broadcast(
                    [128, 16, NH, DH])
                v4 = vg[:, j0:j0 + 16, :].rearrange("p j (h d) -> p j h d",
                                                    d=DH)
                pv = prod[:, 0:16 * 512].rearrange("p (j h d) -> p j h d",
                                                   h=NH, d=DH)
                pr.op("vector", "tensor_tensor", pv, v4, a4, op=ALU.mult)
                src_ = prod[:, 0:16 * 512]
                width = 16
                tv = tree
                while width > 1:
                    half_ = width // 2
                    s3 = src_.rearrange("p (j m) -> p j m", m=512)
                    d3 = tv[:, 0:half_ * 512].rearrange(
                        "p (j m) -> p j m", m=512)
                    pr.op("vector", "tensor_tensor", d3, s3[:, 0:half_, :],
                          s3[:, half_:width, :], op=ALU.add)
                    src_ = tv[:, 0:half_ * 512]
                    tv = prod if tv is tree else tree
                    width = half_
                if first and ch == 0:
                    pr.op("vector", "tensor_copy", nl_acc[:, :], src_,
                          sem=sdv)
                else:
                    pr.op("vector", "tensor_tensor", nl_acc[:, :],
                          nl_acc[:, :], src_, op=ALU.add, sem=sdv)

        kv_done = {}

        def cast_idx(t):
            par = t % 2
            pr.wait("vector", sg, kv_done.get(("v_issued", t - 2), 0))
            if t == 0:
                pr.op("vector", "tensor_copy", idxu[par][:, 0:96],
                      idx0f[:, :], sem=sdv)
            else:
                pr.op("vector", "tensor_copy", idxu[par][:, 0:32],
                      idx32f[:, t, :], sem=sdv)
            kv_done[("idx", t)] = sdv.c

        def softmax_and_local(t, ncols):
            q0 = t * 128
            mx = smx[:, 0:NH]
            sm = smx[:, NH:2 * NH]
            scv = scores[:, :, 0:ncols]
            pr.op("vector", "tensor_reduce", mx, scv, axis=AX.X, op=ALU.max)
            mxb = mx.unsqueeze(2).to_broadcast([128, NH, ncols])
            pr.op("vector", "tensor_tensor", scv, scv, mxb, op=ALU.subtract,
                  sem=sdv)
            pr.wait("scalar", sdv)
            pr.op("scalar", "activation", scv, scv, AF.Exp, sem=sac)
            pr.wait("vector", sac)
            pr.op("vector", "tensor_reduce", sm, scv, axis=AX.X, op=ALU.add)
            pr.op("vector", "reciprocal", sm, sm)
            smb = sm.unsqueeze(2).to_broadcast([128, NH, ncols])
            pr.op("vector", "tensor_tensor", attn_bf[:, :, 0:ncols], scv,
                  smb, op=ALU.mult, sem=sdv)
            if t > 0:
                pr.wait("tensor", sdv)
                for h in range(NH):
                    slot = (h % 4) * 128
                    pr.wait("tensor", sac)
                    pr.op("tensor", "transpose", PS_T[:, slot:slot + 128],
                          attn_bf[:, h, 0:128], C["ident_bf"][:, :], sem=spe)
                    pr.wait("scalar", spe)
                    pr.op("scalar", "copy", attnT_A[:, h, :],
                          PS_T[:, slot:slot + 128], sem=sac)
                for h in range(NH):
                    slot = (h % 4) * 128
                    pr.wait("tensor", sac)
                    pr.op("tensor", "transpose", PS_T[0:64, slot:slot + 128],
                          attn_bf[:, h, 128:192], C["ident_bf"][:, :],
                          sem=spe)
                    pr.wait("scalar", spe)
                    pr.op("scalar", "copy", attnT_B[:, h, :],
                          PS_T[0:64, slot:slot + 128], sem=sac)
                pr.wait("tensor", sac)
                pr.wait("tensor", sdv)
                for h in range(NH):
                    pr.op("tensor", "matmul", PS_B[:, h * DH:(h + 1) * DH],
                          attnT_A[:, h, :], vsA[:, h * DH:(h + 1) * DH],
                          start=True, stop=False)
                    pr.op("tensor", "matmul", PS_B[:, h * DH:(h + 1) * DH],
                          attnT_B[:, h, :], vsB[:, h * DH:(h + 1) * DH],
                          start=False, stop=True,
                          sem=(spe if h == NH - 1 else None))

        def finish_tile(t):
            q0 = t * 128
            if t > 0:
                pr.wait("vector", spe)
                pr.op("vector", "tensor_tensor", mix_bf[:, :],
                      PS_B[:, 0:512], nl_acc[:, :], op=ALU.add, sem=sdv)
            else:
                pr.op("vector", "tensor_copy", mix_bf[:, :], nl_acc[:, :],
                      sem=sdv)
            pr.wait("tensor", sdv)
            pr.wait("tensor", sac)
            for c in range(4):
                pr.op("tensor", "transpose", PS_T[:, c * 128:(c + 1) * 128],
                      mix_bf[:, c * 128:(c + 1) * 128], C["ident_bf"][:, :],
                      sem=(spe if c == 3 else None))
            pr.wait("scalar", spe)
            for c in range(4):
                pr.op("scalar", "copy", mixT[:, c, :],
                      PS_T[:, c * 128:(c + 1) * 128],
                      sem=(sac if c == 3 else None))
            pr.wait("tensor", sac)
            for dd in range(8):
                pr.wait("sync", sdv)
                pr.dma("sync", po_x[:, :],
                       x_fm[dd * 128:(dd + 1) * 128, q0:q0 + 128], sl)
                pr.dma("sync", po_sh[:, :],
                       shared_hbm[dd * 128:(dd + 1) * 128, q0:q0 + 128], sl)
                pr.wait("tensor", sdv)
                for c in range(4):
                    pr.op("tensor", "matmul", PS_C[:, 0:128],
                          w_out_sb[:, c, dd * 128:(dd + 1) * 128],
                          mixT[:, c, :], start=(c == 0), stop=(c == 3),
                          sem=(spe if c == 3 else None))
                pr.wait("vector", spe)
                pr.wait("vector", sl)
                pr.wait("vector", sst)
                pr.op("vector", "tensor_tensor", po_o[:, :], PS_C[:, 0:128],
                      po_x[:, :], op=ALU.add)
                pr.op("vector", "tensor_tensor", po_o[:, :], po_o[:, :],
                      po_sh[:, :], op=ALU.add, sem=sdv)
                pr.wait("sync", sdv)
                pr.dma("sync", partial_hbm[dd * 128:(dd + 1) * 128,
                                           q0:q0 + 128], po_o[:, :], sst)

        def load_strips(t):
            q0 = t * 128
            pr.wait("sync", spe)
            pr.wait("sync", sdv)
            pr.dma("sync", vsA[:, :], vtab[q0 - 64:q0 + 64, :], sl)
            pr.dma("sync", vsB[:, :], vtab[q0 + 64:q0 + 128, :], sl)
            pr.dma("sync", kcs[:, :, :],
                   kcT_hbm[:, :].rearrange("(h p) l -> p h l", p=64)[
                       :, :, q0 - 64:q0 + 128], sl)
            pr.dma("sync", krls[:, :], krl_hbm[:, q0 - 64:q0 + 128], sl)
            pr.dma("sync", qrls[:, :, :],
                   qrl_hbm[:, :].rearrange("(h p) l -> p h l", p=32)[
                       :, :, q0:q0 + 128], sl)
            pr.dma("sync", qct[:, :],
                   qc_hbm[:, t * NH * DH:(t + 1) * NH * DH], sl)
            pr.dma("sync", qrott[:, :],
                   qrot_hbm[:, t * NH * DR:(t + 1) * NH * DR], sl)

        def local_scores(t):
            pr.wait("tensor", sac)
            pr.wait("tensor", sl)
            for c in range(4):
                pr.op("tensor", "transpose", PS_T[:, c * 128:(c + 1) * 128],
                      qct[:, c * 128:(c + 1) * 128], C["ident_bf"][:, :],
                      sem=(spe if c == 3 else None))
            pr.wait("scalar", spe)
            for c in range(4):
                pr.op("scalar", "copy", qcfm[:, 2 * c, :],
                      PS_T[0:64, c * 128:(c + 1) * 128])
                pr.op("scalar", "copy", qcfm[:, 2 * c + 1, :],
                      PS_T[64:128, c * 128:(c + 1) * 128],
                      sem=(sac if c == 3 else None))
            pr.wait("tensor", sac)
            for h in range(NH):
                pcol = (h % 2) * 224
                pr.wait("tensor", sdv)
                pr.op("tensor", "matmul", PS_D[:, pcol:pcol + 192],
                      qcfm[:, h, :], kcs[:, h, :], start=True, stop=False)
                pr.op("tensor", "matmul", PS_D[:, pcol:pcol + 192],
                      qrls[:, h, :], krls[:, :], start=False, stop=True,
                      sem=spe)
                pr.wait("vector", spe)
                pr.op("vector", "scalar_tensor_tensor",
                      scores[:, h, 0:192], PS_D[:, pcol:pcol + 192],
                      SCALE, C["band"][:, :], op0=ALU.mult, op1=ALU.add,
                      sem=sdv)

        cast_idx(0)
        pr.wait("gpsimd", sdv)
        pr.dma("sync", qct[:, :], qc_hbm[:, 0:NH * DH], sl)
        pr.dma("sync", qrott[:, :], qrot_hbm[:, 0:NH * DR], sl)
        pr.wait("vector", sl)
        pr.wait("scalar", sl)
        for sw in range(3):
            pr.wait("gpsimd", sdv)
            issue_k(idxu[0], sw * 32)
            pr.wait("scalar", sg)
            pr.wait("vector", sg)
            nl_scores32(0, ca_s0, sa_s0, sw * 32, sw * 32)
        softmax_and_local(0, 96)
        for sw in range(3):
            pr.wait("gpsimd", sdv)
            issue_v(idxu[0], sw * 32)
            pr.wait("vector", sg)
            nl_values32(sw * 32, first=(sw == 0))
        finish_tile(0)
        free("s0")

        cast_idx(1)
        pr.wait("gpsimd", sdv)
        kv_done[("k", 1)] = issue_k(idxu[1], 0)
        kv_done[("v", 1)] = issue_v(idxu[1], 0)
        kv_done[("v_issued", 1)] = sg.c
        for t in range(1, NT):
            par = t % 2
            load_strips(t)
            local_scores(t)
            pr.wait("scalar", sg, kv_done[("k", t)])
            pr.wait("vector", sg, kv_done[("k", t)])
            nl_scores32(t, ca_nl, sa_nl, 192, 0)
            if t + 1 < NT:
                cast_idx(t + 1)
                pr.wait("gpsimd", sdv)
                kv_done[("k", t + 1)] = issue_k(idxu[(t + 1) % 2], 0)
            softmax_and_local(t, 224)
            pr.wait("vector", sg, kv_done[("v", t)])
            nl_values32(192, first=True)
            if t + 1 < NT:
                pr.wait("gpsimd", sdv)
                kv_done[("v", t + 1)] = issue_v(idxu[(t + 1) % 2], 0)
                kv_done[("v_issued", t + 1)] = sg.c
            finish_tile(t)
        if tap("partial", (D, L)) is not None:
            pr.wait("sync", sst)
            pr.dma("sync", tap_out["partial"][:, :], partial_hbm[:, :], sst)
        free("b2")
        pr.mark("b2")

        xnh = [rbuf(f"xnh{c}", (128, 2048), st="f") for c in range(8)]
        wup_b = [rbuf(f"wup{b_}", (128, 8, FF), st="f") for b_ in range(2)]
        cwrow = [rbuf(f"cwrow{b_}", (1, 2048), BF16, st="f")
                 for b_ in range(2)]
        cwrep = [rbuf(f"cwrep{b_}", (128, 2048), BF16, st="f")
                 for b_ in range(2)]
        hsil = [rbuf(f"hsil{b_}", (128, 1024), st="f") for b_ in range(2)]
        hscl = [rbuf(f"hscl{b_}", (128, 1024), st="f") for b_ in range(2)]
        ones_bf = rbuf("ones_bf", (1, 128), BF16, st="f")
        pr.wait("vector", sl)
        pr.op("vector", "memset", ones_bf[:, :], 1.0, sem=sdv)
        wup_done = [0, 0]
        cwrow_done = [0, 0]
        cwrep_done = [0, 0]
        pair_sac = [0, 0]
        hbuf_done = [0, 0]
        hdma_done = [0, 0]
        for th in range(2):
            tb = th * 2048
            pr.wait("sync", spe)
            for c in range(8):
                pr.dma("sync", xnh[c][:, :],
                       xn_fm_hbm[c * 128:(c + 1) * 128, tb:tb + 2048], sl)
            for e in range(NE):
                par = e % 2
                bb = wup_b[par]
                pr.wait("gpsimd", spe, wup_done[par])
                for kk in range(8):
                    pr.dma("gpsimd", bb[:, kk, :],
                           w_up[e, kk * 128:(kk + 1) * 128, :], sg)
                pr.wait("sync", spe, cwrow_done[par])
                pr.dma("sync", cwrow[par][:, :],
                       cw_bf_hbm[e:e + 1, tb:tb + 2048], sl)
                pr.wait("tensor", sl)
                pr.wait("tensor", sdv, cwrep_done[par])
                rep_mm = []
                for j in range(4):
                    ps = PS_B if j % 2 == 0 else PS_C
                    pr.wait("tensor", sdv)
                    pr.op("tensor", "matmul", ps[:, 0:512],
                          ones_bf[:, :],
                          cwrow[par][:, j * 512:(j + 1) * 512],
                          start=True, stop=True, sem=spe)
                    pr.wait("vector", spe)
                    pr.op("vector", "tensor_copy",
                          cwrep[par][:, j * 512:(j + 1) * 512],
                          ps[:, 0:512], sem=sdv)
                cwrow_done[par] = spe.c
                cw_ready = sdv.c
                pr.wait("tensor", sg)
                for fc in range(4):
                    for u in range(2):
                        pp = (fc * 2 + u) % 2
                        pcol = pp * 1024
                        pr.wait("tensor", sac, pair_sac[pp])
                        for kk in range(8):
                            for s2 in range(2):
                                pr.op("tensor", "matmul",
                                      PS_A[:, pcol + s2 * 512:
                                           pcol + s2 * 512 + 512],
                                      bb[:, kk, fc * 128:(fc + 1) * 128],
                                      xnh[kk][:, u * 1024 + s2 * 512:
                                              u * 1024 + s2 * 512 + 512],
                                      start=(kk == 0), stop=(kk == 7),
                                      sem=(spe if (kk == 7 and s2 == 1)
                                           else None))
                        pr.wait("scalar", spe)
                        pr.wait("scalar", sdv, hbuf_done[pp])
                        pr.op("scalar", "activation", hsil[pp][:, :],
                              PS_A[:, pcol:pcol + 1024], AF.Silu, sem=sac)
                        pair_sac[pp] = sac.c
                        pr.wait("vector", sac)
                        pr.wait("vector", sdv, cw_ready)
                        pr.wait("vector", sst, hdma_done[pp])
                        pr.op("vector", "tensor_tensor", hscl[pp][:, :],
                              hsil[pp][:, :],
                              cwrep[par][:, u * 1024:(u + 1) * 1024],
                              op=ALU.mult, sem=sdv)
                        hbuf_done[pp] = sdv.c
                        pr.wait("sync", sdv)
                        Tb0 = th * 4 + u * 2
                        pr.dma("sync",
                               h_hbm[e, Tb0:Tb0 + 2, :,
                                     fc * 512:(fc + 1) * 512].rearrange(
                                   "tb p x -> p tb x"),
                               hscl[pp][:, :].rearrange(
                                   "p (tb x) -> p tb x", x=512), sst)
                        hdma_done[pp] = sst.c
                wup_done[par] = spe.c
                cwrep_done[par] = sdv.c

        free("ps_main")
        PS_F = stacks.setdefault("ps_f", ctx.enter_context(ExitStack())) \
            .enter_context(nc.psum_tensor("PS_F", [128, 4096], F32))
        if tap("wdall", (NE * FF, D), BF16) is not None:
            tbuf = rbuf("tbuf_dbg", (128, 4, D), st="f")
            for e in range(NE):
                pr.wait("sync", sst)
                pr.dma("sync", tbuf[:, :, :],
                       w_down_bf[e, :, :].rearrange("(c p) d -> p c d",
                                                    p=128), sl)
                pr.wait("sync", sl)
                for fc in range(4):
                    pr.dma("sync",
                           tap_out["wdall"][e * FF + fc * 128:
                                            e * FF + (fc + 1) * 128, :],
                           tbuf[:, fc, :], sst)
        if tap("hball", (NE * 128, 2048), BF16) is not None:
            tbuf2 = rbuf("tbuf2_dbg", (128, 2048), st="f")
            for e in range(NE):
                pr.wait("sync", sst)
                pr.dma("sync", tbuf2[:, :], h_hbm[e, 0, :, :], sl)
                pr.wait("sync", sl)
                pr.dma("sync", tap_out["hball"][e * 128:(e + 1) * 128, :],
                       tbuf2[:, :], sst)
        drain_hbm = dscratch("drain_hbm", (128, 2048), BF16)
        settle_sb = rbuf("settle_sb", (128, 4, D), st="f")
        settle_flat = settle_sb[:, :, :].rearrange("p c d -> p (c d)")
        for e in range(12):
            pr.wait("sync", sst)
            pr.dma("sync", settle_sb[:, :, :],
                   w_down_bf[e, :, :].rearrange("(c p) d -> p c d", p=128),
                   sl)
            pr.wait("sync", sl)
            pr.dma("sync", drain_hbm[:, :], settle_flat[:, 0:2048], sst)
        for e in range(12):
            pr.wait("sync", sst)
            pr.dma("sync", settle_flat[:, 0:2048], h_hbm[e, 0, :, :], sl)
            pr.wait("sync", sl)
            pr.dma("sync", drain_hbm[:, :], settle_flat[:, 0:2048], sst)
        pr.wait("sync", sst)
        hrd = [rbuf(f"hrd{b_}", (128, 2048), st="f") for b_ in range(2)]
        wdn_b = [rbuf(f"wdn{b_}", (128, 4, D), st="f") for b_ in range(2)]
        po_p = rbuf("po_p", (128, 8, 512), F32, st="f")
        y_sb = [rbuf(f"y_sb{b_}", (128, 512), F32, st="f") for b_ in range(2)]
        buf_done = [0, 0]
        ydma_done = [0, 0]
        evac_done = 0
        for Tb in range(8):
            tb = Tb * 512
            pr.wait("tensor", sdv)
            pr.wait("sync", sdv, evac_done)
            for dd in range(8):
                pr.dma("sync", po_p[:, dd, :],
                       partial_hbm[dd * 128:(dd + 1) * 128, tb:tb + 512],
                       sl)
            po_ready = sl.c
            for e in range(NE):
                par = e % 2
                hb = hrd[par]
                wb = wdn_b[par]
                pr.wait("sync", spe, buf_done[par])
                pr.dma("sync", hb[:, :], h_hbm[e, Tb, :, :], sl)
                pr.dma("sync", wb[:, :, :],
                       w_down_bf[e, :, :].rearrange("(c p) d -> p c d",
                                                    p=128), sl)
                if Tb == 0 and e == 3:
                    if tap("wb3", (128, 4 * D), BF16) is not None:
                        pr.wait("sync", sl)
                        pr.dma("sync", tap_out["wb3"][:, :],
                               wb[:, :, :].rearrange("p c d -> p (c d)"), sst)
                    if tap("hb3", (128, 2048), BF16) is not None:
                        pr.wait("sync", sl)
                        pr.dma("sync", tap_out["hb3"][:, :], hb[:, :], sst)
                    pr.wait("tensor", sst)
                pr.wait("tensor", sl)
                for dd in range(8):
                    for fc in range(4):
                        pr.op("tensor", "matmul",
                              PS_F[:, dd * 512:(dd + 1) * 512],
                              wb[:, fc, dd * 128:(dd + 1) * 128],
                              hb[:, fc * 512:(fc + 1) * 512],
                              start=(e == 0 and fc == 0),
                              stop=(e == NE - 1 and fc == 3),
                              sem=(spe if (fc == 3 and dd == 7) else None))
                buf_done[par] = spe.c
            pr.wait("vector", spe)
            pr.wait("vector", sl, po_ready)
            for dd in range(8):
                yb = y_sb[dd % 2]
                pr.wait("vector", sst, ydma_done[dd % 2])
                pr.op("vector", "tensor_tensor", yb[:, :],
                      PS_F[:, dd * 512:(dd + 1) * 512], po_p[:, dd, :],
                      op=ALU.add, sem=sdv)
                pr.wait("sync", sdv)
                pr.dma("sync", y_fm[dd * 128:(dd + 1) * 128, tb:tb + 512],
                       yb[:, :], sst)
                ydma_done[dd % 2] = sst.c
            evac_done = sdv.c
        pr.wait("sync", sst)
        pr.mark("f")
        pr.truncate(limit)

        block = ctx.enter_context(nc.Block())

        @block.sync
        def _(q):
            for fn in pr.ops["sync"]:
                fn(q)

        @block.gpsimd
        def _(q):
            for fn in pr.ops["gpsimd"]:
                fn(q)

        @block.scalar
        def _(q):
            for fn in pr.ops["scalar"]:
                fn(q)

        @block.vector
        def _(q):
            for fn in pr.ops["vector"]:
                fn(q)

        @block.tensor
        def _(q):
            for fn in pr.ops["tensor"]:
                fn(q)

    return nc, list(tap_out.keys())


def kernel(**inputs):
    HT = host_tables()
    nc, tap_names = build_program(taps=TAPS)
    A = {k: np.asarray(v) for k, v in inputs.items()}
    base = {
        "W_dkv": A["L0_attn_W_dkv"], "W_uk": A["L0_attn_W_uk"],
        "W_uv": A["L0_attn_W_uv"], "W_dq": A["L0_attn_W_dq"],
        "W_uq": A["L0_attn_W_uq"], "W_qr": A["L0_attn_W_qr"],
        "W_kr": A["L0_attn_W_kr"], "W_out": A["L0_attn_W_out"],
        "idx_wq": A["L0_attn_idx_wq"], "idx_wk": A["L0_attn_idx_wk"],
        "idx_w": np.ascontiguousarray(A["L0_attn_idx_w"].reshape(1, NIH)),
        "raw_delta": np.ascontiguousarray(
            A["L0_attn_raw_delta"].reshape(1, HALF)),
        "cent_T": np.ascontiguousarray(A["L0_ffn_centroids"].T),
        "exp_bias": np.ascontiguousarray(
            A["L0_ffn_expert_biases"].reshape(1, NE)),
        "Ws_up": A["L0_ffn_Ws_up"], "Ws_down": A["L0_ffn_Ws_down"],
        "W_up": A["L0_ffn_W_up"], "W_down": A["L0_ffn_W_down"],
    }
    for nm, arr in HT.items():
        base["t_" + nm] = arr
    in_maps = []
    for c in range(B):
        m = dict(base)
        xc = np.ascontiguousarray(A["L0_attn_x"][c])
        m["x_tok"] = xc
        m["x_fm"] = np.ascontiguousarray(xc.T)
        in_maps.append(m)
    res = run_bass_kernel_spmd(nc, in_maps, core_ids=list(range(B)))
    outs = []
    taps = []
    for c in range(B):
        r = res.results[c]
        outs.append(np.ascontiguousarray(np.asarray(r["y_fm"]).T))
        taps.append({n: np.asarray(r["tap_" + n]) for n in tap_names})
    y = np.stack(outs, axis=0).astype(np.float32)
    if tap_names:
        kernel.last_taps = taps
    return y
